# version 02
# baseline (speedup 1.0000x reference)
import sys
sys.path.insert(0, "/opt/trn_rl_repo")
import contextlib
import numpy as np
import ml_dtypes
import concourse.bass as bass
import concourse.mybir as mybir
from concourse.bass_utils import run_bass_kernel_spmd

F32 = mybir.dt.float32
BF16 = mybir.dt.bfloat16
AF = mybir.ActivationFunctionType
ALU = mybir.AluOpType
AX = mybir.AxisListType
BF = ml_dtypes.bfloat16

NCORE = 8
S, D, P = 8192, 4096, 1024
DIM, DFF = 512, 2048
CCH, CNC, CT, CF = 64, 64, 128, 256


def kstack(a):
    K, M = a.shape
    nk = K // 128
    return np.ascontiguousarray(a.reshape(nk, 128, M).transpose(1, 0, 2).reshape(128, nk * M))


class Sy:

    def __init__(self, nc, st):
        self.nc, self.st, self.sems, self.cnt, self.base = nc, st, {}, {}, {}

    def sem(self, name):
        if name not in self.sems:
            self.sems[name] = self.st.enter_context(self.nc.semaphore(name))
            self.cnt[name] = 0
            self.base[name] = 0
        return self.sems[name]

    def alias(self, name, old):
        assert name not in self.sems and old in self.sems
        self.sems[name] = self.sems[old]
        self.base[name] = self.base[old] + self.cnt[old]
        self.cnt[name] = 0

    def inc(self, instr, name, n=1):
        instr.then_inc(self.sem(name), n)
        self.cnt[name] += n
        return self.cnt[name]

    def wait(self, eng, name, val=None):
        v = self.cnt.get(name, 0) if val is None else val
        eng.wait_ge(self.sem(name), self.base.get(name, 0) + v)

    def val(self, name):
        return self.cnt.get(name, 0)

    def selfedge(self, eng, instr):
        nm = "selfb_" + type(eng).__name__
        n = self.inc(instr, nm, 1)
        eng.wait_ge(self.sem(nm), n)


def cmul(v, outre, outim, are, aim, bre, bim, w0, w1):
    v.tensor_tensor(w0, are, bre, op=ALU.mult)
    v.tensor_tensor(w1, aim, bim, op=ALU.mult)
    v.tensor_tensor(w0, w0, w1, op=ALU.subtract)
    v.tensor_tensor(w1, are, bim, op=ALU.mult)
    v.tensor_tensor(outim, aim, bre, op=ALU.mult)
    v.tensor_tensor(outim, outim, w1, op=ALU.add)
    v.tensor_copy(outre, w0)


def build_conv(debug=False):
    nc = bass.Bass("TRN2", target_bir_lowering=False)
    dI = lambda n, s: nc.dram_tensor(n, s, F32, kind="ExternalInput")
    y_d = dI("y", [128, 4096])
    phre_d, phim_d = dI("phre", [128, CCH]), dI("phim", [128, CCH])
    lc_d = dI("lc", [1, 2 * CCH])
    dftre_d, dftim_d = dI("dftre", [128, CF]), dI("dftim", [128, CF])
    ic_d = dI("ic", [128, 512])
    tl_d = dI("tl", [128, 8])
    shc_d = dI("shc", [64, 64 * 7])
    shp_d = dI("shp", [128, 128 * 7])
    om_d = dI("om", [128, 6])
    onesk_d = dI("onesk", [1, 128])
    out_d = nc.dram_tensor("out", [128, 4096], F32, kind="ExternalOutput")
    dbt = nc.dram_tensor("dbt", [2, 4096], F32)
    dci = nc.dram_tensor("dci", [2, 4096], F32)
    if debug:
        dbg_T = nc.dram_tensor("dbg_T", [64, 128], F32, kind="ExternalOutput")
        dbg_ci = nc.dram_tensor("dbg_ci", [64, 128], F32, kind="ExternalOutput")
        dbg_tr = nc.dram_tensor("dbg_tr", [2, 4096], F32, kind="ExternalOutput")
        dbg_pg = nc.dram_tensor("dbg_pg", [128, 128], F32, kind="ExternalOutput")
        dbg_crr = nc.dram_tensor("dbg_crr", [128, 2 * 4096], F32, kind="ExternalOutput")
        dbg_pgr = nc.dram_tensor("dbg_pgr", [128, 2 * 512], F32, kind="ExternalOutput")
        dbg_yh = nc.dram_tensor("dbg_yh", [128, 2 * 512], F32, kind="ExternalOutput")

    with contextlib.ExitStack() as st:
        T = lambda n, s, d=F32: st.enter_context(nc.sbuf_tensor(n, s, d))
        PS = lambda n, s: st.enter_context(nc.psum_tensor(n, s, F32))
        sy = Sy(nc, st)
        block = st.enter_context(nc.Block())

        s_y = T("s_y", [128, 4096])
        s_dre = T("s_dre", [128, CF])
        s_dim = T("s_dim", [128, CF])
        s_ic = T("s_ic", [128, 512])
        s_tl = T("s_tl", [128, 8])
        s_shc = T("s_shc", [64, 448])
        s_shp = T("s_shp", [128, 896], BF16)
        s_ph = T("s_ph", [128, 2 * CCH])
        s_lc = T("s_lc", [1, 2 * CCH])
        s_om = T("s_om", [128, 6])
        s_onesk = T("s_onesk", [1, 128])
        s_p = T("s_p", [128, 2 * CCH])
        s_pk = T("s_pk", [128, 2 * CCH])
        s_pow = T("s_pow", [128, 14 * CCH])
        s_qpow = T("s_qpow", [128, 12 * CCH])
        s_w = [T(f"s_w{i}", [128, CCH]) for i in range(6)]
        s_fh = T("s_fh", [128, 4 * CCH])
        s_fr = [T(f"s_fr{i}", [128, 512]) for i in range(4)]
        s_yhr = T("s_yhr", [128, 2 * 4096])
        s_yhi = T("s_yhi", [128, 2 * 4096])
        s_xr = [T(f"s_xr{i}", [128, 512]) for i in range(2)]
        s_xi = [T(f"s_xi{i}", [128, 512]) for i in range(2)]
        s_t1 = T("s_t1", [128, 512])
        s_t2 = T("s_t2", [128, 512])
        s_g1 = T("s_g1", [128, 512])
        s_pg = T("s_pg", [128, 2 * CCH])
        s_pgb = T("s_pgb", [128, 2 * CCH], BF16)
        s_pgp = T("s_pgp", [128, 2 * CCH])
        s_pgr = T("s_pgr", [128, 512])
        s_pgi = T("s_pgi", [128, 512])
        s_tr = T("s_tr", [2, 4096])
        s_T = T("s_T", [64, 128])
        s_shv = T("s_shv", [64, 128])
        s_ci = T("s_ci", [64, 128])
        s_cw = [T(f"s_cw{i}", [64, 64]) for i in range(2)]
        s_crr = T("s_crr", [128, 4096])
        s_cri = T("s_cri", [128, 4096])
        s_os = [T(f"s_os{i}", [128, 512]) for i in range(2)]

        p_a = PS("p_a", [128, 512])
        p_b = PS("p_b", [128, 512])
        p_c2 = PS("p_c2", [128, 512])
        p_d = PS("p_d", [128, 512])
        p_t = PS("p_t", [2, 512])
        p_g = PS("p_g", [128, 512])
        p_ch = PS("p_ch", [64, 512])
        p_f = PS("p_f", [1, 16])

        def mmfence(t, name, n=1):
            sy.inc(t.matmul(p_f[0:1, 0:1], s_tl[:, 0:1], s_tl[:, 0:1], start=True, stop=True), name, n)

        @block.sync
        def _(q):
            for dst, src in [(s_shc, shc_d), (s_om, om_d), (s_lc, lc_d),
                             (s_dre, dftre_d), (s_dim, dftim_d), (s_ic, ic_d), (s_tl, tl_d)]:
                sy.inc(q.dma_start(out=dst[:, :], in_=src.ap()), "dq", 16)
            sy.inc(q.dma_start(out=s_ph[:, 0:CCH], in_=phre_d.ap()), "dq", 16)
            sy.inc(q.dma_start(out=s_ph[:, CCH:], in_=phim_d.ap()), "dq", 16)
            sy.inc(q.dma_start(out=s_onesk[:, :], in_=onesk_d.ap()), "dq", 16)

        @block.gpsimd
        def _(g):
            for dst, src in [(s_y, y_d), (s_shp, shp_d)]:
                sy.inc(g.dma_start(out=dst[:, :], in_=src.ap()), "dg", 16)

        LDQ, LDG = sy.val("dq"), sy.val("dg")

        @block.vector
        def _(v):
            sy.wait(v, "dq", LDQ)
            a, b = s_ph[:, 0:CCH], s_ph[:, CCH:]
            v.tensor_tensor(s_w[0][:, :], a, a, op=ALU.mult)
            v.tensor_tensor(s_w[1][:, :], b, b, op=ALU.mult)
            sy.inc(v.tensor_tensor(s_w[0][:, :], s_w[0][:, :], s_w[1][:, :], op=ALU.add), "v", 1)

        @block.scalar
        def _(sc):
            sy.wait(sc, "v", 1)
            sc.activation(s_w[1][:, :], s_w[0][:, :], AF.Sqrt)
            sy.inc(sc.activation(s_w[2][:, :], s_w[1][:, :], AF.Tanh), "s", 1)

        def csq(v, cur):
            a, b = cur[:, 0:CCH], cur[:, CCH:2 * CCH]
            v.tensor_tensor(s_w[0][:, :], a, a, op=ALU.mult)
            v.tensor_tensor(s_w[1][:, :], b, b, op=ALU.mult)
            v.tensor_tensor(s_w[2][:, :], a, b, op=ALU.mult)
            v.tensor_tensor(a, s_w[0][:, :], s_w[1][:, :], op=ALU.subtract)
            v.tensor_scalar(b, s_w[2][:, :], 2.0, None, op0=ALU.mult)

        @block.vector
        def _(v):
            sy.wait(v, "s", 1)
            sy.selfedge(v, v.reciprocal(s_w[3][:, :], s_w[1][:, :]))
            v.tensor_tensor(s_w[3][:, :], s_w[3][:, :], s_w[2][:, :], op=ALU.mult)
            v.tensor_tensor(s_p[:, 0:CCH], s_w[3][:, :], s_ph[:, 0:CCH], op=ALU.mult)
            v.tensor_tensor(s_p[:, CCH:], s_w[3][:, :], s_ph[:, CCH:], op=ALU.mult)
            v.tensor_copy(s_pk[:, :], s_p[:, :])
            for r in range(7):
                v.tensor_copy(s_pow[:, 2 * r * CCH:(2 * r + 2) * CCH], s_pk[:, :])
                csq(v, s_pk)
            v.tensor_copy(s_qpow[:, 0:2 * CCH], s_pk[:, :])
            for r in range(1, 6):
                v.tensor_copy(s_qpow[:, 2 * r * CCH:(2 * r + 2) * CCH],
                              s_qpow[:, 2 * (r - 1) * CCH:2 * r * CCH])
                csq(v, s_qpow[:, 2 * r * CCH:(2 * r + 2) * CCH])
            pre, pim = s_p[:, 0:CCH], s_p[:, CCH:]
            qre, qim = s_pk[:, 0:CCH], s_pk[:, CCH:]
            for ft in range(2):
                omr, omi, sg = s_om[:, ft:ft + 1], s_om[:, 2 + ft:3 + ft], s_om[:, 4 + ft:5 + ft]
                w0, w1, w2, w3, w4, w5 = [w[:, :] for w in s_w]
                fre = s_fh[:, ft * CCH:(ft + 1) * CCH]
                fim = s_fh[:, (2 + ft) * CCH:(3 + ft) * CCH]
                v.tensor_scalar(w0, pre, omr, None, op0=ALU.mult)
                v.tensor_scalar(w1, pim, omi, None, op0=ALU.mult)
                v.tensor_tensor(w0, w0, w1, op=ALU.subtract)
                v.tensor_scalar(w0, w0, -1.0, 1.0, op0=ALU.mult, op1=ALU.add)
                v.tensor_scalar(w1, pim, omr, None, op0=ALU.mult)
                v.tensor_scalar(w2, pre, omi, None, op0=ALU.mult)
                v.tensor_tensor(w1, w1, w2, op=ALU.add)
                v.tensor_scalar(w1, w1, -1.0, None, op0=ALU.mult)
                v.tensor_tensor(w2, w0, w0, op=ALU.mult)
                v.tensor_tensor(w3, w1, w1, op=ALU.mult)
                v.tensor_tensor(w2, w2, w3, op=ALU.add)
                sy.selfedge(v, v.reciprocal(w2, w2))
                v.tensor_scalar(w3, qre, sg, None, op0=ALU.mult)
                v.tensor_scalar(w3, w3, -1.0, 1.0, op0=ALU.mult, op1=ALU.add)
                v.tensor_scalar(w4, qim, sg, None, op0=ALU.mult)
                v.tensor_scalar(w4, w4, -1.0, None, op0=ALU.mult)
                v.tensor_tensor(w5, w3, w0, op=ALU.mult)
                v.tensor_tensor(fre, w4, w1, op=ALU.mult)
                v.tensor_tensor(fre, fre, w5, op=ALU.add)
                v.tensor_tensor(fre, fre, w2, op=ALU.mult)
                v.tensor_tensor(w5, w4, w0, op=ALU.mult)
                v.tensor_tensor(w0, w3, w1, op=ALU.mult)
                v.tensor_tensor(fim, w5, w0, op=ALU.subtract)
                v.tensor_tensor(fim, fim, w2, op=ALU.mult)
            sy.inc(v.tensor_tensor(s_w[0][:, 0:4], s_fh[:, 0:4], s_fh[:, 252:256], op=ALU.bypass), "v", 1)
        VF = sy.val("v")

        @block.sync
        def _(q):
            sy.wait(q, "v", VF)
            for i in range(4):
                sy.inc(q.dma_start(out=s_fr[i][:, 0:CCH], in_=s_fh[:, i * CCH:(i + 1) * CCH]), "dq", 16)
            sy.wait(q, "dq")
            for i in range(4):
                w = CCH
                while w < 512:
                    sy.inc(q.dma_start(out=s_fr[i][:, w:2 * w], in_=s_fr[i][:, 0:w]), "dq", 16)
                    sy.wait(q, "dq")
                    w *= 2
        FREP = sy.val("dq")

        @block.tensor
        def _(t):
            sy.wait(t, "dg", LDG)
            i = 0
            for ft in range(2):
                dre = s_dre[:, ft * 128:(ft + 1) * 128]
                dim = s_dim[:, ft * 128:(ft + 1) * 128]
                for n in range(8):
                    if i >= 2:
                        sy.wait(t, "evx", 2 * (i - 1))
                    pa, pb = (p_a, p_b) if i % 2 == 0 else (p_c2, p_d)
                    rs = s_y[:, n * 512:(n + 1) * 512]
                    t.matmul(pa[:, :], dre, rs, start=True, stop=True)
                    t.matmul(pb[:, :], dim, rs, start=True, stop=True)
                    mmfence(t, "mmx", 1)
                    i += 1

        @block.scalar
        def _(sc):
            i = 0
            for ft in range(2):
                for n in range(8):
                    sy.wait(sc, "mmx", i + 1)
                    pa, pb = (p_a, p_b) if i % 2 == 0 else (p_c2, p_d)
                    j = i % 2
                    if i >= 2:
                        sy.wait(sc, "fm", 2 * (i - 1))
                        sy.wait(sc, "g", i - 1)
                    sc.activation(s_xr[j][:, :], pa[:, :], AF.Copy)
                    sy.inc(sc.activation(s_xi[j][:, :], pb[:, :], AF.Copy), "evx", 2)
                    i += 1

        @block.vector
        def _(v):
            sy.wait(v, "dq", FREP)
            for i in range(16):
                ft, n = divmod(i, 8)
                j = i % 2
                sl = slice(ft * 4096 + n * 512, ft * 4096 + (n + 1) * 512)
                sy.wait(v, "evx", 2 * (i + 1))
                if i > 0:
                    sy.wait(v, "g", i)
                fr, fi = s_fr[ft][:, :], s_fr[2 + ft][:, :]
                v.tensor_tensor(s_t1[:, :], s_xr[j][:, :], fr, op=ALU.mult)
                v.tensor_tensor(s_t2[:, :], s_xi[j][:, :], fi, op=ALU.mult)
                v.tensor_tensor(s_yhr[:, sl], s_t1[:, :], s_t2[:, :], op=ALU.subtract)
                sy.inc(v.tensor_tensor(s_t1[:, :], s_xr[j][:, :], fi, op=ALU.mult), "fm", 2)

        @block.gpsimd
        def _(g):
            for i in range(16):
                ft, n = divmod(i, 8)
                j = i % 2
                sl = slice(ft * 4096 + n * 512, ft * 4096 + (n + 1) * 512)
                sy.wait(g, "fm", 2 * (i + 1))
                fr = s_fr[ft][:, :]
                g.tensor_tensor(s_g1[:, :], s_xi[j][:, :], fr, op=ALU.mult)
                sy.inc(g.tensor_tensor(s_yhi[:, sl], s_t1[:, :], s_g1[:, :], op=ALU.add), "g", 1)
        YH_G = sy.val("g")

        @block.vector
        def _(v):
            sy.selfedge(v, v.memset(s_pg[:, :], 0.0))
            sy.selfedge(v, v.memset(s_pg[0:1, 0:CCH], 1.0))
            sy.inc(v.tensor_copy(s_pgb[:, :], s_pg[:, :]), "v", 1)
        PG0 = sy.val("v")

        @block.tensor
        def _(t):
            for r in range(7):
                sy.wait(t, "v", PG0 + r)
                shm = s_shp[:, r * 128:(r + 1) * 128]
                t.matmul(p_g[:, 0:CCH], shm, s_pgb[:, 0:CCH], start=True, stop=True)
                sy.inc(t.matmul(p_g[:, CCH:2 * CCH], shm, s_pgb[:, CCH:], start=True, stop=True), "mmp", 1)

        @block.vector
        def _(v):
            for r in range(7):
                sy.wait(v, "mmp", r + 1)
                pw = s_pow[:, 2 * r * CCH:(2 * r + 2) * CCH]
                v.tensor_tensor(s_w[0][:, :], p_g[:, 0:CCH], pw[:, 0:CCH], op=ALU.mult)
                v.tensor_tensor(s_w[1][:, :], p_g[:, CCH:2 * CCH], pw[:, CCH:], op=ALU.mult)
                v.tensor_tensor(s_w[0][:, :], s_w[0][:, :], s_w[1][:, :], op=ALU.subtract)
                v.tensor_tensor(s_w[2][:, :], p_g[:, 0:CCH], pw[:, CCH:], op=ALU.mult)
                v.tensor_tensor(s_w[3][:, :], p_g[:, CCH:2 * CCH], pw[:, 0:CCH], op=ALU.mult)
                v.tensor_tensor(s_w[2][:, :], s_w[2][:, :], s_w[3][:, :], op=ALU.add)
                v.tensor_tensor(s_pg[:, 0:CCH], s_pg[:, 0:CCH], s_w[0][:, :], op=ALU.add)
                v.tensor_tensor(s_pg[:, CCH:], s_pg[:, CCH:], s_w[2][:, :], op=ALU.add)
                sy.inc(v.tensor_copy(s_pgb[:, :], s_pg[:, :]), "v", 1)
            cmul(v, s_pgp[:, 0:CCH], s_pgp[:, CCH:], s_pg[:, 0:CCH], s_pg[:, CCH:],
                 s_p[:, 0:CCH], s_p[:, CCH:], s_w[0][:, :], s_w[1][:, :])
            sy.inc(v.tensor_tensor(s_w[0][:, 0:4], s_pgp[:, 0:4], s_pgp[:, 124:128], op=ALU.bypass), "v", 1)
        PGB = sy.val("v")

        @block.sync
        def _(q):
            sy.wait(q, "v", PGB)
            sy.inc(q.dma_start(out=s_pgr[:, 0:CCH], in_=s_pgp[:, 0:CCH]), "dq", 16)
            sy.inc(q.dma_start(out=s_pgi[:, 0:CCH], in_=s_pgp[:, CCH:2 * CCH]), "dq", 16)
            sy.wait(q, "dq")
            for dst in (s_pgr, s_pgi):
                w = CCH
                while w < 512:
                    sy.inc(q.dma_start(out=dst[:, w:2 * w], in_=dst[:, 0:w]), "dq", 16)
                    sy.wait(q, "dq")
                    w *= 2
        PGR = sy.val("dq")

        @block.tensor
        def _(t):
            sy.wait(t, "fm", 32)
            sy.wait(t, "g", YH_G)
            for n in range(8):
                if n > 0:
                    sy.wait(t, "evt", n)
                sl = slice(n * 512, (n + 1) * 512)
                sl1 = slice(4096 + n * 512, 4096 + (n + 1) * 512)
                t.matmul(p_t[:, :], s_tl[:, 0:2], s_yhr[:, sl], start=True, stop=False)
                t.matmul(p_t[:, :], s_tl[:, 2:4], s_yhi[:, sl], start=False, stop=False)
                t.matmul(p_t[:, :], s_tl[:, 4:6], s_yhr[:, sl1], start=False, stop=False)
                t.matmul(p_t[:, :], s_tl[:, 6:8], s_yhi[:, sl1], start=False, stop=True)
                mmfence(t, "mmt", 1)

        @block.vector
        def _(v):
            for n in range(8):
                sy.wait(v, "mmt", n + 1)
                sy.inc(v.tensor_copy(s_tr[0:2, n * 512:(n + 1) * 512], p_t[:, :]), "evt", 1)
            sy.inc(v.tensor_copy(s_cw[0][0:2, 0:8], s_tr[0:2, 0:4096:512]), "evt", 1)
        EVT = sy.val("evt")

        @block.gpsimd
        def _(g):
            sy.wait(g, "evt", EVT)
            sy.inc(g.dma_start(out=dbt.ap(), in_=s_tr[:, :]), "dg", 16)
            sy.wait(g, "dg")
            sy.inc(g.dma_start(out=s_T[:, 0:64], in_=dbt.ap()[0:1, :].rearrange("p (a b) -> (p a) b", a=64)), "dg", 16)
            sy.inc(g.dma_start(out=s_T[:, 64:128], in_=dbt.ap()[1:2, :].rearrange("p (a b) -> (p a) b", a=64)), "dg", 16)
        TLD = sy.val("dg")

        @block.vector
        def _(v):
            sy.wait(v, "dg", TLD)
            v.tensor_tensor(s_cw[0][0:1, :], s_pk[0:1, 0:CCH], s_lc[0:1, 0:CCH], op=ALU.mult)
            v.tensor_tensor(s_cw[1][0:1, :], s_pk[0:1, CCH:], s_lc[0:1, CCH:], op=ALU.mult)
            v.tensor_tensor(s_cw[0][0:1, :], s_cw[0][0:1, :], s_cw[1][0:1, :], op=ALU.subtract)
            v.tensor_tensor(s_T[0:1, 0:64], s_T[0:1, 0:64], s_cw[0][0:1, :], op=ALU.add)
            v.tensor_tensor(s_cw[0][0:1, :], s_pk[0:1, 0:CCH], s_lc[0:1, CCH:], op=ALU.mult)
            v.tensor_tensor(s_cw[1][0:1, :], s_pk[0:1, CCH:], s_lc[0:1, 0:CCH], op=ALU.mult)
            v.tensor_tensor(s_cw[0][0:1, :], s_cw[0][0:1, :], s_cw[1][0:1, :], op=ALU.add)
            sy.inc(v.tensor_tensor(s_T[0:1, 64:128], s_T[0:1, 64:128], s_cw[0][0:1, :], op=ALU.add), "v", 1)
        CH0 = sy.val("v")

        @block.tensor
        def _(t):
            for r in range(7):
                sy.wait(t, "v", CH0 + r)
                shm = s_shc[:, r * 64:(r + 1) * 64]
                t.matmul(p_ch[:, 0:64], shm, s_T[:, 0:64], start=True, stop=True)
                sy.inc(t.matmul(p_ch[:, 64:128], shm, s_T[:, 64:128], start=True, stop=True), "mmc", 1)

        @block.vector
        def _(v):
            for r in range(6):
                sy.wait(v, "mmc", r + 1)
                v.tensor_copy(s_shv[:, :], p_ch[:, 0:128])
                qre = s_qpow[0:64, 2 * r * CCH:(2 * r + 1) * CCH]
                qim = s_qpow[0:64, (2 * r + 1) * CCH:2 * (r + 1) * CCH]
                v.tensor_tensor(s_cw[0][:, :], qre, s_shv[:, 0:64], op=ALU.mult)
                v.tensor_tensor(s_cw[1][:, :], qim, s_shv[:, 64:128], op=ALU.mult)
                v.tensor_tensor(s_cw[0][:, :], s_cw[0][:, :], s_cw[1][:, :], op=ALU.subtract)
                v.tensor_tensor(s_T[:, 0:64], s_T[:, 0:64], s_cw[0][:, :], op=ALU.add)
                v.tensor_tensor(s_cw[0][:, :], qre, s_shv[:, 64:128], op=ALU.mult)
                v.tensor_tensor(s_cw[1][:, :], qim, s_shv[:, 0:64], op=ALU.mult)
                v.tensor_tensor(s_cw[0][:, :], s_cw[0][:, :], s_cw[1][:, :], op=ALU.add)
                sy.inc(v.tensor_tensor(s_T[:, 64:128], s_T[:, 64:128], s_cw[0][:, :], op=ALU.add), "v", 1)
            sy.wait(v, "mmc", 7)
            v.tensor_copy(s_ci[:, :], p_ch[:, 0:128])
            v.tensor_copy(s_ci[0:1, :], s_lc[0:1, :])
            sy.inc(v.tensor_tensor(s_cw[0][:, 0:2], s_ci[:, 0:2], s_ci[:, 126:128], op=ALU.bypass), "v", 1)
        CIV = sy.val("v")

        @block.gpsimd
        def _(g):
            sy.wait(g, "v", CIV)
            sy.inc(g.dma_start(out=dci.ap()[0:1, :].rearrange("p (a b) -> (p a) b", a=64), in_=s_ci[:, 0:64]), "dg", 16)
            sy.inc(g.dma_start(out=dci.ap()[1:2, :].rearrange("p (a b) -> (p a) b", a=64), in_=s_ci[:, 64:128]), "dg", 16)
            sy.wait(g, "dg")
            sy.inc(g.dma_start(out=s_y[0:1, :], in_=dci[0:1, :]), "ci", 16)
            sy.wait(g, "cie", 8)
            sy.inc(g.dma_start(out=s_y[0:1, :], in_=dci[1:2, :]), "ci", 16)

        @block.tensor
        def _(t):
            for i in range(16):
                sy.wait(t, "ci", 16 if i < 8 else 32)
                if i >= 1:
                    sy.wait(t, "cie", i)
                t.matmul(p_g[:, :], s_onesk[:, :], s_y[0:1, (i % 8) * 512:(i % 8 + 1) * 512], start=True, stop=True)
                mmfence(t, "cib", 1)

        @block.vector
        def _(v):
            for i in range(16):
                sy.wait(v, "cib", i + 1)
                dst = s_crr if i < 8 else s_cri
                sl = slice((i % 8) * 512, (i % 8 + 1) * 512)
                sy.inc(v.tensor_copy(dst[:, sl], p_g[:, :]), "cie", 1)
        CIB = sy.val("cie")

        @block.tensor
        def _(t):
            sy.wait(t, "cie", CIB)
            sy.wait(t, "dq", PGR)
            icre0, icre1 = s_ic[:, 0:128], s_ic[:, 128:256]
            icim0, icim1 = s_ic[:, 256:384], s_ic[:, 384:512]
            for n in range(8):
                if n >= 2:
                    sy.wait(t, "evy", n - 1)
                pp = p_a if n % 2 == 0 else p_b
                sl = slice(n * 512, (n + 1) * 512)
                sl1 = slice(4096 + n * 512, 4096 + (n + 1) * 512)
                t.matmul(pp[:, :], icre0, s_yhr[:, sl], start=True, stop=False)
                t.matmul(pp[:, :], icim0, s_yhi[:, sl], start=False, stop=False)
                t.matmul(pp[:, :], icre1, s_yhr[:, sl1], start=False, stop=False)
                t.matmul(pp[:, :], icim1, s_yhi[:, sl1], start=False, stop=True)
                mmfence(t, "mmy", 1)

        @block.vector
        def _(v):
            for n in range(8):
                sy.wait(v, "mmy", n + 1)
                if n >= 2:
                    sy.wait(v, "os", 16 * (n - 1))
                pp = p_a if n % 2 == 0 else p_b
                sl = slice(n * 512, (n + 1) * 512)
                ob = s_os[n % 2]
                v.tensor_tensor(s_t1[:, :], s_pgr[:, :], s_crr[:, sl], op=ALU.mult)
                v.tensor_tensor(s_t2[:, :], s_pgi[:, :], s_cri[:, sl], op=ALU.mult)
                v.tensor_tensor(s_t1[:, :], s_t1[:, :], s_t2[:, :], op=ALU.subtract)
                sy.inc(v.tensor_tensor(ob[:, :], pp[:, :], s_t1[:, :], op=ALU.add), "evy", 1)

        @block.sync
        def _(q):
            for n in range(8):
                sy.wait(q, "evy", n + 1)
                sy.inc(q.dma_start(out=out_d.ap()[:, n * 512:(n + 1) * 512], in_=s_os[n % 2][:, :]), "os", 16)
            sy.wait(q, "os", 16 * 8)
            if debug:
                sy.inc(q.dma_start(out=dbg_T.ap(), in_=s_T[:, :]), "dq", 16)
                sy.inc(q.dma_start(out=dbg_ci.ap(), in_=s_ci[:, :]), "dq", 16)
                sy.inc(q.dma_start(out=dbg_tr.ap(), in_=s_tr[:, :]), "dq", 16)
                sy.inc(q.dma_start(out=dbg_pg.ap(), in_=s_pgp[:, :]), "dq", 16)
                sy.inc(q.dma_start(out=dbg_crr.ap()[:, 0:4096], in_=s_crr[:, :]), "dq", 16)
                sy.inc(q.dma_start(out=dbg_crr.ap()[:, 4096:8192], in_=s_cri[:, :]), "dq", 16)
                sy.inc(q.dma_start(out=dbg_pgr.ap()[:, 0:512], in_=s_pgr[:, :]), "dq", 16)
                sy.inc(q.dma_start(out=dbg_pgr.ap()[:, 512:1024], in_=s_pgi[:, :]), "dq", 16)
                sy.inc(q.dma_start(out=dbg_yh.ap()[:, 0:512], in_=s_yhr[:, 512:1024]), "dq", 16)
                sy.inc(q.dma_start(out=dbg_yh.ap()[:, 512:1024], in_=s_yhi[:, 512:1024]), "dq", 16)
                sy.wait(q, "dq")

    return nc


def conv_consts():
    f = np.arange(CF)
    t = np.arange(CT)
    ang = 2 * np.pi * np.outer(f, t) / CF
    dftre = np.cos(ang).T.astype(np.float32)
    dftim = (-np.sin(ang)).T.astype(np.float32)
    icre = (np.cos(ang) / CF).astype(np.float32)
    icim = (-np.sin(ang) / CF).astype(np.float32)
    ic = np.concatenate([kstack(icre), kstack(icim)], axis=1)
    tl = np.stack([icre[:, 127], -icim[:, 127], icim[:, 127], icre[:, 127]], axis=1)
    tl = kstack(tl)
    shc = np.zeros((64, 64 * 7), np.float32)
    for r in range(6):
        s = 1 << r
        m = np.zeros((64, 64), np.float32)
        for i in range(64 - s):
            m[i, i + s] = 1.0
        shc[:, r * 64:(r + 1) * 64] = m
    m = np.zeros((64, 64), np.float32)
    for i in range(63):
        m[i, i + 1] = 1.0
    shc[:, 6 * 64:7 * 64] = m
    shp = np.zeros((128, 128 * 7), np.float32)
    for r in range(7):
        s = 1 << r
        m = np.zeros((128, 128), np.float32)
        for i in range(128 - s):
            m[i, i + s] = 1.0
        shp[:, r * 128:(r + 1) * 128] = m
    om = np.zeros((128, 6), np.float32)
    fr = np.arange(128)
    om[:, 0] = np.cos(2 * np.pi * fr / CF)
    om[:, 1] = np.cos(2 * np.pi * (fr + 128) / CF)
    om[:, 2] = -np.sin(2 * np.pi * fr / CF)
    om[:, 3] = -np.sin(2 * np.pi * (fr + 128) / CF)
    om[:, 4] = 1.0
    om[:, 5] = -1.0
    om[:, 4] = np.where(fr % 2 == 0, 1.0, -1.0)
    om[:, 5] = np.where((fr + 128) % 2 == 0, 1.0, -1.0)
    return dict(dftre=dftre, dftim=dftim, ic=ic, tl=tl, shc=shc, shp=shp, om=om,
                onesk=np.ones((1, 128), np.float32))


def mstack(wT):
    K, M = wT.shape
    nk, nm = K // 128, M // 128
    return np.ascontiguousarray(
        wT.reshape(nk, 128, nm, 128).transpose(2, 0, 1, 3).reshape(nm * nk, 128, 128)
        .transpose(1, 0, 2).reshape(128, nm * nk * 128))


def build_mid():
    nc = bass.Bass("TRN2", target_bir_lowering=False)
    dI = lambda n, s: nc.dram_tensor(n, s, F32, kind="ExternalInput")
    h_d = dI("h", [128, 4096])
    cv_d = dI("cv", [128, 4096])
    pc_d = dI("pc", [128, 24])
    b1_d = dI("b1c", [128, 16])
    onesk_d = dI("onesk", [1, 128])
    w1_d = dI("w1t", [128, 8192])
    w2_d = dI("w2t", [128, 8192])
    ho_d = nc.dram_tensor("hout", [128, 4096], F32, kind="ExternalOutput")
    y_d = nc.dram_tensor("y", [128, 4096], F32, kind="ExternalOutput")
    dmid = nc.dram_tensor("dmid", [2, 1024], F32)

    with contextlib.ExitStack() as st:
        T = lambda n, s, d=F32: st.enter_context(nc.sbuf_tensor(n, s, d))
        PS = lambda n, s: st.enter_context(nc.psum_tensor(n, s, F32))
        sy = Sy(nc, st)
        block = st.enter_context(nc.Block())

        s_h = T("s_h", [128, 4096])
        s_cv = T("s_cv", [128, 4096])
        s_pc = T("s_pc", [128, 24])
        s_b1 = T("s_b1", [128, 16])
        s_onesk = T("s_onesk", [1, 128])
        s_w1 = T("s_w1", [128, 8192], BF16)
        s_w2 = T("s_w2", [128, 8192], BF16)
        s_ones1 = T("s_ones1", [128, 1], BF16)
        s_eps = T("s_eps", [128, 1])
        s_hb = T("s_hb", [128, 4096], BF16)
        s_sq = T("s_sq", [128, 4096], BF16)
        s_yln = T("s_yln", [128, 4096], BF16)
        s_u = T("s_u", [128, 16384], BF16)
        s_mu = T("s_mu", [1, 1024])
        s_m2 = T("s_m2", [1, 1024])
        s_var = T("s_var", [1, 1024])
        s_rs = T("s_rs", [1, 1024])
        s_kb = T("s_kb", [1, 128], BF16)
        s_rowb = T("s_rowb", [1, 2048], BF16)
        s_crow = s_rowb[0:1, 0:512]
        s_mur = T("s_mur", [128, 1024])
        s_rsr = T("s_rsr", [128, 1024])
        s_a = T("s_a", [128, 1024])
        s_b = T("s_b", [128, 1024])
        s_y = T("s_y", [128, 4096])

        p_a = PS("p_a", [128, 512])
        p_b = PS("p_b", [128, 512])
        p_s = PS("p_s", [1, 1024])
        p_q = PS("p_q", [1, 1024])
        p_bc = PS("p_bc", [128, 512])
        p_f = PS("p_f", [1, 16])

        def mmfence(t, name, n=1):
            sy.inc(t.matmul(p_f[0:1, 0:1], s_eps[:, 0:1], s_eps[:, 0:1], start=True, stop=True), name, n)

        @block.sync
        def _(q):
            for dst, src in [(s_h, h_d), (s_cv, cv_d), (s_pc, pc_d), (s_b1, b1_d), (s_onesk, onesk_d)]:
                sy.inc(q.dma_start(out=dst[:, :], in_=src.ap()), "dq", 16)

        @block.gpsimd
        def _(g):
            for dst, src in [(s_w1, w1_d), (s_w2, w2_d)]:
                sy.inc(g.dma_start(out=dst[:, :], in_=src.ap()), "dg", 16)
            sy.inc(g.dma_start(out=s_kb[:, :], in_=onesk_d.ap()), "dg", 16)
            sy.inc(g.dma_start(out=s_ones1[:, :], in_=onesk_d.ap().rearrange("a b -> b a")), "dg", 16)
        LDQ, LDG = sy.val("dq"), sy.val("dg")

        C = lambda i: slice(i * 1024, (i + 1) * 1024)
        col = lambda base, j: s_pc[:, base * 4 + j:base * 4 + j + 1]

        def stats_phase(v_first_marker):
            pass

        @block.vector
        def _(v):
            sy.wait(v, "dq", LDQ)
            sy.wait(v, "dg", LDG)
            sy.selfedge(v, v.memset(s_eps[:, :], 1e-5))
            for i in range(4):
                v.scalar_tensor_tensor(s_h[:, C(i)], s_cv[:, C(i)], col(0, i), s_h[:, C(i)],
                                       op0=ALU.mult, op1=ALU.add)
                sy.inc(v.tensor_copy(s_hb[:, C(i)], s_h[:, C(i)]), "v", 1)

        @block.scalar
        def _(sc):
            for i in range(4):
                sy.wait(sc, "v", i + 1)
                sy.inc(sc.activation(s_sq[:, C(i)], s_h[:, C(i)], AF.Square), "s", 1)

        @block.tensor
        def _(t):
            sy.wait(t, "v", 4)
            sy.wait(t, "s", 4)
            for half in range(2):
                for i in range(4):
                    t.matmul(p_s[0:1, half * 512:(half + 1) * 512], s_ones1[:, :],
                             s_hb[:, i * 1024 + half * 512:i * 1024 + (half + 1) * 512],
                             start=(i == 0), stop=(i == 3))
            for half in range(2):
                for i in range(4):
                    t.matmul(p_q[0:1, half * 512:(half + 1) * 512], s_ones1[:, :],
                             s_sq[:, i * 1024 + half * 512:i * 1024 + (half + 1) * 512],
                             start=(i == 0), stop=(i == 3))
            mmfence(t, "mm1", 1)
        ST1 = sy.val("mm1")

        def rows_and_bcast(tag, stat_wait):
            @block.vector
            def _(v):
                sy.wait(v, "mm1" if tag == "A" else "mm6", stat_wait)
                v.tensor_copy(s_mu[:, 0:512], p_s[:, 0:512])
                v.tensor_copy(s_mu[:, 512:1024], p_s[:, 512:1024])
                v.tensor_copy(s_m2[:, 0:512], p_q[:, 0:512])
                v.tensor_copy(s_m2[:, 512:1024], p_q[:, 512:1024])
                v.tensor_scalar(s_mu[:, :], s_mu[:, :], 1.0 / 512, None, op0=ALU.mult)
                v.tensor_scalar(s_m2[:, :], s_m2[:, :], 1.0 / 512, None, op0=ALU.mult)
                v.tensor_tensor(s_var[:, :], s_mu[:, :], s_mu[:, :], op=ALU.mult)
                sy.inc(v.tensor_tensor(s_var[:, :], s_m2[:, :], s_var[:, :], op=ALU.subtract), "v", 1)

            @block.scalar
            def _(sc):
                sy.wait(sc, "v")
                sy.inc(sc.activation(s_rs[:, :], s_var[:, :], AF.Sqrt, bias=s_eps[0:1, :]), "s", 1)

            @block.vector
            def _(v):
                sy.wait(v, "s")
                sy.selfedge(v, v.reciprocal(s_rs[:, :], s_rs[:, :]))
                sy.inc(v.tensor_tensor(s_a[0:1, 0:4], s_rs[0:1, 0:4], s_mu[0:1, 0:4], op=ALU.bypass), "v", 1)

            @block.vector
            def _(v):
                v.tensor_copy(s_rowb[0:1, 0:1024], s_mu[:, :])
                sy.inc(v.tensor_copy(s_rowb[0:1, 1024:2048], s_rs[:, :]), "v", 1)

            @block.tensor
            def _(t):
                sy.wait(t, "v")
                for half in range(2):
                    for n in range(2):
                        if sy.val("bc") > 0:
                            sy.wait(t, "bce", sy.val("bc"))
                        sy.inc(t.matmul(p_bc[:, :], s_kb[:, :],
                                        s_rowb[0:1, half * 1024 + n * 512:half * 1024 + (n + 1) * 512],
                                        start=True, stop=True), "bc", 1)

            @block.vector
            def _(v):
                for half, dst in [(0, s_mur), (1, s_rsr)]:
                    for n in range(2):
                        sy.wait(v, "bc", sy.val("bce") + 1)
                        sy.inc(v.tensor_copy(dst[:, n * 512:(n + 1) * 512], p_bc[:, :]), "bce", 1)

        rows_and_bcast("A", ST1)
        BC1 = sy.val("bce")

        @block.vector
        def _(v):
            sy.wait(v, "bce", BC1)
            for i in range(4):
                v.tensor_tensor(s_a[:, :], s_h[:, C(i)], s_mur[:, :], op=ALU.subtract)
                v.tensor_tensor(s_b[:, :], s_a[:, :], s_rsr[:, :], op=ALU.mult)
                sy.inc(v.tensor_scalar(s_yln[:, C(i)], s_b[:, :], col(1, i), col(2, i),
                                       op0=ALU.mult, op1=ALU.add), "yl", 1)
        YL = sy.val("yl")

        @block.tensor
        def _(t):
            sy.wait(t, "dg", LDG)
            sy.wait(t, "yl", YL)
            for m in range(16):
                for n in range(2):
                    idx = m * 2 + n
                    pp = p_a if idx % 2 == 0 else p_b
                    if idx >= 2:
                        sy.wait(t, "ue", idx - 1)
                    for k in range(4):
                        t.matmul(pp[:, :], s_w1[:, (m * 4 + k) * 128:(m * 4 + k + 1) * 128],
                                 s_yln[:, k * 1024 + n * 512:k * 1024 + (n + 1) * 512],
                                 start=(k == 0), stop=(k == 3))
                    mmfence(t, "um", 1)

        @block.scalar
        def _(sc):
            for m in range(16):
                for n in range(2):
                    idx = m * 2 + n
                    pp = p_a if idx % 2 == 0 else p_b
                    sy.wait(sc, "um", idx + 1)
                    sy.inc(sc.activation(s_u[:, m * 1024 + n * 512:m * 1024 + (n + 1) * 512],
                                         pp[:, :], AF.Silu, bias=s_b1[:, m:m + 1]), "ue", 1)
        UE = sy.val("ue")

        @block.tensor
        def _(t):
            sy.wait(t, "ue", UE)
            for j in range(4):
                for n in range(2):
                    idx = j * 2 + n
                    pp = p_a if idx % 2 == 0 else p_b
                    if idx >= 2:
                        sy.wait(t, "he", idx - 1)
                    for k in range(16):
                        t.matmul(pp[:, :], s_w2[:, (j * 16 + k) * 128:(j * 16 + k + 1) * 128],
                                 s_u[:, k * 1024 + n * 512:k * 1024 + (n + 1) * 512],
                                 start=(k == 0), stop=(k == 15))
                    mmfence(t, "hm", 1)

        @block.vector
        def _(v):
            for j in range(4):
                for n in range(2):
                    idx = j * 2 + n
                    pp = p_a if idx % 2 == 0 else p_b
                    sy.wait(v, "hm", idx + 1)
                    sl = slice(j * 1024 + n * 512, j * 1024 + (n + 1) * 512)
                    sy.inc(v.scalar_tensor_tensor(s_h[:, sl], pp[:, :], col(5, j), s_h[:, sl],
                                                  op0=ALU.add, op1=ALU.add), "he", 1)
        HE = sy.val("he")

        @block.vector
        def _(v):
            sy.wait(v, "he", HE)
            for i in range(4):
                sy.inc(v.tensor_copy(s_hb[:, C(i)], s_h[:, C(i)]), "v2", 1)

        @block.scalar
        def _(sc):
            sy.wait(sc, "he", HE)
            for i in range(4):
                sy.wait(sc, "v2", i + 1)
                sy.inc(sc.activation(s_sq[:, C(i)], s_h[:, C(i)], AF.Square), "s2", 1)

        @block.tensor
        def _(t):
            sy.wait(t, "v2", 4)
            sy.wait(t, "s2", 4)
            for half in range(2):
                for i in range(4):
                    t.matmul(p_s[0:1, half * 512:(half + 1) * 512], s_ones1[:, :],
                             s_hb[:, i * 1024 + half * 512:i * 1024 + (half + 1) * 512],
                             start=(i == 0), stop=(i == 3))
            for half in range(2):
                for i in range(4):
                    t.matmul(p_q[0:1, half * 512:(half + 1) * 512], s_ones1[:, :],
                             s_sq[:, i * 1024 + half * 512:i * 1024 + (half + 1) * 512],
                             start=(i == 0), stop=(i == 3))
            mmfence(t, "mm6", 1)
        ST6 = sy.val("mm6")

        rows_and_bcast("B", ST6)
        BC2 = sy.val("bce")

        @block.vector
        def _(v):
            sy.wait(v, "bce", BC2)
            for i in range(4):
                if i > 0:
                    sy.wait(v, "ye", i)
                v.tensor_tensor(s_a[:, :], s_h[:, C(i)], s_mur[:, :], op=ALU.subtract)
                sy.inc(v.tensor_tensor(s_b[:, :], s_a[:, :], s_rsr[:, :], op=ALU.mult), "yb", 1)

        @block.scalar
        def _(sc):
            for i in range(4):
                sy.wait(sc, "yb", i + 1)
                sy.inc(sc.activation(s_y[:, C(i)], s_b[:, :], AF.Silu,
                                     scale=col(3, i), bias=col(4, i)), "ye", 1)

        @block.sync
        def _(q):
            sy.wait(q, "he", HE)
            sy.inc(q.dma_start(out=ho_d.ap(), in_=s_h[:, :]), "os", 16)
            sy.wait(q, "ye", 4)
            sy.inc(q.dma_start(out=y_d.ap(), in_=s_y[:, :]), "os", 16)
            sy.wait(q, "os")

    return nc


def build_biga(upto=9):
    nc = bass.Bass("TRN2", target_bir_lowering=False)
    dI = lambda n, s: nc.dram_tensor(n, s, F32, kind="ExternalInput")
    xin_d = dI("xin", [1536, 4096])
    wq_d = dI("wqT", [128, 32 * 4096])
    wk_d = dI("wkT", [128, 4096])
    wv_d = dI("wvT", [128, 4096])
    wo_d = dI("woT", [128, 32 * 4096])
    cosq_d = dI("cosq", [128, 1024])
    sinq_d = dI("sinq", [128, 1024])
    cosk_d = dI("cosk", [128, 1536])
    sink_d = dI("sink", [128, 1536])
    aA_d = dI("aA", [1, 1024])
    aB_d = dI("aB", [1, 1024])
    mb_d = dI("mb", [128, 12])
    bw_d = dI("bwT", [128, 32 * 4352])
    gw_d = dI("gwT", [128, 32 * 28])
    la_d = dI("laT", [128, 32 * 256])
    win_d = dI("winT", [128, 32 * 512])
    pc_d = dI("pc", [128, 12])
    onesk_d = dI("onesk", [1, 128])
    cstE_d = dI("cstE", [128, 1])
    cstO_d = dI("cstO", [128, 1])
    ident_d = dI("ident", [128, 128])
    rot_d = dI("rot", [128, 128])
    xn2_d = nc.dram_tensor("xn2", [128, 32768], BF16, kind="ExternalOutput")
    musig_d = nc.dram_tensor("musig", [1, 2048], F32, kind="ExternalOutput")
    h_d = nc.dram_tensor("h", [128, 4096], F32, kind="ExternalOutput")
    y0_d = nc.dram_tensor("y0", [128, 4096], F32, kind="ExternalOutput")
    dmu = nc.dram_tensor("dmu", [2, 1024], F32)
    dmu1 = nc.dram_tensor("dmu1", [2, 1024], F32)

    SCALE = 1.0 / float(np.sqrt(128.0))

    with contextlib.ExitStack() as st:
        T = lambda n, s, d=F32: st.enter_context(nc.sbuf_tensor(n, s, d))
        PS = lambda n, s, d=F32: st.enter_context(nc.psum_tensor(n, s, d))
        sy = Sy(nc, st)
        block = st.enter_context(nc.Block())

        s_xnT = T("s_xnT", [128, 32768], BF16)
        s_big = T("s_big", [128, 32768], BF16)
        s_str = T("s_str", [128, 12288], BF16)
        s_xin = s_str[:, 0:8192].bitcast(F32)
        s_rows = T("s_rows", [1, 1024])
        s_kr = T("s_kr", [128, 1536], BF16)
        s_v = T("s_v", [128, 1536], BF16)
        s_qr = T("s_qr", [128, 1024], BF16)
        s_sig = T("s_sig", [128, 2048], BF16)
        s_avu = T("s_avu", [128, 2048], BF16)
        s_aAB = s_rows[0:1, :].bitcast(BF16)

        s_cab = T("s_cab", [128, 1024], BF16)
        s_mb = T("s_mb", [128, 12])
        s_cq = T("s_cq", [128, 2048], BF16)
        s_ck = T("s_ck", [128, 3072], BF16)
        s_gw = T("s_gw", [128, 896], BF16)
        s_id = T("s_id", [128, 128], BF16)
        s_rt = T("s_rt", [128, 128], BF16)
        s_onesk = T("s_onesk", [1, 128])
        s_kb = T("s_kb", [1, 128], BF16)
        s_ones1 = T("s_ones1", [128, 1], BF16)
        s_eps = T("s_eps", [128, 1])
        s_pc = T("s_pc", [128, 12])
        s_w = [T(f"s_w{i}", [128, 512]) for i in range(2)]
        s_sq = T("s_sq", [128, 512], BF16)
        s_rep = [T(f"s_rep{i}", [128, 512]) for i in range(3)]
        s_rowb = T("s_rowb", [1, 2048], BF16)
        s_crow = s_rowb[0:1, 0:512]
        s_lg = T("s_lg", [128, 256])
        s_gt = None
        s_gx = None
        s_hl = None
        s_stat = T("s_stat", [128, 64])
        s_den = T("s_den", [1, 2048], BF16)

        p_a = PS("p_a", [128, 512])
        p_b = PS("p_b", [128, 512])
        p_s = PS("p_s", [1, 1024])
        p_q = PS("p_q", [1, 1024])
        p_t1 = PS("p_t1", [128, 256], BF16)
        p_r = PS("p_r", [1, 512])
        p_tb = p_a[:, :].bitcast(BF16)

        s_aux = s_big[:, 0:16384]
        s_wk = s_str[:, 0:4096]
        s_wv = s_str[:, 4096:8192]
        s_gx = s_avu[:, :]
        s_gt = s_qr[0:28, :]
        s_hl = s_sig[:, :]
        C1 = lambda i: slice(i * 1024, (i + 1) * 1024)

        @block.sync
        def _(q):
            for dst, src in [(s_mb[:, :], mb_d.ap()), (s_onesk[:, :], onesk_d.ap()), (s_pc[:, :], pc_d.ap()),
                             (s_eps[:, :], cstE_d.ap())]:
                sy.inc(q.dma_start(out=dst, in_=src), "dq", 16)

        @block.gpsimd
        def _(g):
            for dst, src in [(s_gw, gw_d), (s_id, ident_d), (s_rt, rot_d)]:
                sy.inc(g.dma_start(out=dst[:, :], in_=src.ap()), "dg", 16)
            sy.inc(g.dma_start(out=s_ones1[:, :], in_=cstO_d.ap()), "dg", 16)
            sy.inc(g.dma_start(out=s_kb[:, :], in_=onesk_d.ap()), "dg", 16)
            sy.inc(g.dma_start(out=s_cq[:, 0:1024], in_=cosq_d.ap()), "dg", 16)
            sy.inc(g.dma_start(out=s_cq[:, 1024:], in_=sinq_d.ap()), "dg", 16)
            sy.inc(g.dma_start(out=s_ck[:, 0:1536], in_=cosk_d.ap()), "dg", 16)
            sy.inc(g.dma_start(out=s_ck[:, 1536:], in_=sink_d.ap()), "dg", 16)
        LDQ, LDG = sy.val("dq"), sy.val("dg")

        @block.vector
        def _(v):
            sy.wait(v, "dq", LDQ)
            sy.wait(v, "dg", LDG)
            sy.inc(v.nop(), "v", 1)

        for t in range(12):
            @block.sync
            def _(q, t=t):
                if t > 0:
                    sy.wait(q, "trA", t * 32)
                sy.inc(q.dma_start(out=s_xin, in_=xin_d[128 * t:128 * (t + 1), :]), "xt", 16)

            @block.scalar
            def _(sc, t=t):
                sy.wait(sc, "xt", 16 * (t + 1))
                if t == 0:
                    sy.wait(sc, "dq", LDQ)
                    sy.wait(sc, "dg", LDG)
                if t >= 1:
                    sy.wait(sc, "bnA", t)
                sy.inc(sc.activation(s_big[:, 16384:20480], s_xin[:, :], AF.Square), "sqA", 1)

            @block.vector
            def _(v, t=t):
                sy.wait(v, "xt", 16 * (t + 1))
                sy.wait(v, "v", 1)
                sy.wait(v, "sqA", t + 1)
                sy.selfedge(v, v.tensor_reduce(s_stat[:, 48:49], s_xin[:, :], axis=AX.X, op=ALU.add))
                sy.selfedge(v, v.tensor_reduce(s_stat[:, 49:50], s_big[:, 16384:20480], axis=AX.X, op=ALU.add))
                if 3 <= t <= 10:
                    sy.wait(v, "dmu", 32 * (t - 2))
                sy.selfedge(v, v.tensor_scalar(s_stat[:, 50:51], s_stat[:, 48:49], 1.0 / 4096, None, op0=ALU.mult))
                sy.selfedge(v, v.tensor_scalar(s_stat[:, 53:54], s_stat[:, 49:50], 1.0 / 4096, None, op0=ALU.mult))
                sy.selfedge(v, v.tensor_tensor(s_stat[:, 54:55], s_stat[:, 50:51], s_stat[:, 50:51], op=ALU.mult))
                sy.selfedge(v, v.tensor_tensor(s_stat[:, 51:52], s_stat[:, 53:54], s_stat[:, 54:55], op=ALU.subtract))
                sy.inc(v.nop(), "bnA", 1)

            @block.scalar
            def _(sc, t=t):
                sy.wait(sc, "bnA", t + 1)
                sy.selfedge(sc, sc.activation(s_stat[:, 51:52], s_stat[:, 51:52], AF.Sqrt, bias=s_eps[:, :]))
                sy.inc(sc.nop(), "sgA", 1)

            @block.vector
            def _(v, t=t):
                sy.wait(v, "sgA", t + 1)
                sy.selfedge(v, v.reciprocal(s_stat[:, 52:53], s_stat[:, 51:52]))
                sy.inc(v.tensor_scalar(s_str[:, 8192:12288], s_str[:, 0:8192].bitcast(F32), s_stat[:, 50:51], s_stat[:, 52:53],
                                       op0=ALU.subtract, op1=ALU.mult), "nrmA", 1)

            @block.gpsimd
            def _(g, t=t):
                if 2 <= t <= 9:
                    sy.wait(g, "nrmA", t + 1)
                    sy.inc(g.dma_start(out=dmu.ap()[0:1, 128 * (t - 2):128 * (t - 1)].rearrange("a b -> b a"),
                                       in_=s_stat[:, 50:51]), "dmu", 16)
                    sy.inc(g.dma_start(out=dmu.ap()[1:2, 128 * (t - 2):128 * (t - 1)].rearrange("a b -> b a"),
                                       in_=s_stat[:, 51:52]), "dmu", 16)

            @block.tensor
            def _(tn, t=t):
                sy.wait(tn, "nrmA", t + 1)
                for c in range(32):
                    k = sy.val("trA")
                    if k >= 2:
                        sy.wait(tn, "evA", k - 1)
                    pt = p_t1[:, 0:128] if k % 2 == 0 else p_tb[:, 0:128]
                    sy.inc(tn.transpose(pt, s_str[:, 8192 + c * 128:8192 + (c + 1) * 128], s_id[:, :]), "trA", 1)

            @block.vector
            def _(v, t=t):
                for c in range(32):
                    k = t * 32 + c
                    sy.wait(v, "trA", k + 1)
                    if 2 <= t <= 9:
                        dst = s_xnT[:, c * 1024 + 128 * (t - 2):c * 1024 + 128 * (t - 1)]
                    elif t < 2:
                        dst = s_aux[:, c * 512 + 128 * t:c * 512 + 128 * (t + 1)]
                    else:
                        dst = s_aux[:, c * 512 + 256 + 128 * (t - 10):c * 512 + 256 + 128 * (t - 9)]
                    pt = p_t1[:, 0:128] if k % 2 == 0 else p_tb[:, 0:128]
                    sy.inc(v.tensor_copy(dst, pt), "evA", 1)
        TRA = sy.val("evA")

        if upto == 0:
            @block.sync
            def _(q):
                sy.wait(q, "evA", TRA)
                for part in range(4):
                    sy.inc(q.dma_start(out=xn2_d.ap()[:, part * 8192:(part + 1) * 8192],
                                       in_=s_xnT[:, part * 8192:(part + 1) * 8192]), "out", 16)
                sy.inc(q.dma_start(out=h_d.ap(), in_=s_big[:, 0:8192].bitcast(F32)), "out", 16)
                sy.inc(q.dma_start(out=y0_d.ap(), in_=s_big[:, 8192:16384].bitcast(F32)), "out", 16)
                sy.wait(q, "out", 96)
            return nc

        @block.gpsimd
        def _(g):
            sy.wait(g, "evA", TRA)
            sy.inc(g.dma_start(out=s_str[:, 0:4096], in_=wk_d.ap()), "wkv", 16)
            sy.inc(g.dma_start(out=s_str[:, 4096:8192], in_=wv_d.ap()), "wkv", 16)

        def kbuf_rhs(c, lo, wid):
            out = []
            pos = lo
            while wid > 0:
                if pos < 256:
                    w = min(wid, 256 - pos)
                    out.append((s_aux[:, c * 512 + pos:c * 512 + pos + w], pos - lo, w))
                elif pos < 1280:
                    w = min(wid, 1280 - pos)
                    out.append((s_xnT[:, c * 1024 + pos - 256:c * 1024 + pos - 256 + w], pos - lo, w))
                else:
                    w = min(wid, 1536 - pos)
                    out.append((s_aux[:, c * 512 + 256 + pos - 1280:c * 512 + 256 + pos - 1280 + w], pos - lo, w))
                pos += w
                wid -= w
            return out

        @block.tensor
        def _(t):
            sy.wait(t, "wkv", 32)
            for grp in range(3):
                pp = p_a if grp % 2 == 0 else p_b
                if grp >= 2:
                    sy.wait(t, "krv", 1)
                nseg = len(kbuf_rhs(0, grp * 512, 512))
                for j in range(nseg):
                    for c in range(32):
                        srcs, off, w = kbuf_rhs(c, grp * 512, 512)[j]
                        mm = t.matmul(pp[:, off:off + w], s_wk[:, c * 128:(c + 1) * 128], srcs,
                                      start=(c == 0), stop=(c == 31))
                sy.inc(mm, "kmm", 1)
            for vt in range(12):
                pp = p_a if vt % 2 == 0 else p_b
                sy.wait(t, "krv", min(3, 1 + sy.val("vev")) if False else sy.val("vev"))
                if vt >= 1:
                    sy.wait(t, "vev", vt - 1)
                if vt == 0:
                    sy.wait(t, "krv", 3)
                for c in range(32):
                    for srcs, off, w in kbuf_rhs(c, 0, 1536):
                        pass
                for c in range(32):
                    if vt < 2:
                        lh = s_aux[:, c * 512 + 128 * vt:c * 512 + 128 * (vt + 1)]
                    elif vt < 10:
                        lh = s_xnT[:, c * 1024 + 128 * (vt - 2):c * 1024 + 128 * (vt - 1)]
                    else:
                        lh = s_aux[:, c * 512 + 256 + 128 * (vt - 10):c * 512 + 256 + 128 * (vt - 9)]
                    mm = t.matmul(pp[:, 0:128], lh, s_wv[:, c * 128:(c + 1) * 128],
                                  start=(c == 0), stop=(c == 31))
                sy.inc(mm, "vmm", 1)

        @block.vector
        def _(v):
            for grp in range(3):
                sy.wait(v, "kmm", grp + 1)
                pp = p_a if grp % 2 == 0 else p_b
                sl = slice(grp * 512, (grp + 1) * 512)
                ck, sk2 = s_ck[:, sl], s_ck[:, 1536 + grp * 512:1536 + (grp + 1) * 512]
                sy.inc(v.tensor_copy(s_kr[:, sl], pp[:, :]), "krv", 1)

        @block.tensor
        def _(t):
            sy.wait(t, "krv", 3)
            for grp in range(3):
                pp = p_a if grp % 2 == 0 else p_b
                sy.wait(t, "vev", 12)
                if grp >= 2:
                    sy.wait(t, "krf", grp - 1)
                sy.inc(t.matmul(pp[:, :], s_rt[:, :], s_kr[:, grp * 512:(grp + 1) * 512],
                                start=True, stop=True), "krot", 1)

        @block.scalar
        def _(sc):
            for vt in range(12):
                sy.wait(sc, "vmm", vt + 1)
                pp = p_a if vt % 2 == 0 else p_b
                sy.inc(sc.activation(s_v[:, vt * 128:(vt + 1) * 128], pp[:, 0:128], AF.Copy), "vev", 1)

        @block.vector
        def _(v):
            for grp in range(3):
                sy.wait(v, "krot", grp + 1)
                pp = p_a if grp % 2 == 0 else p_b
                sl = slice(grp * 512, (grp + 1) * 512)
                v.tensor_tensor(s_w[0][:, :], s_kr[:, sl], s_ck[:, sl], op=ALU.mult)
                v.tensor_tensor(s_w[1][:, :], pp[:, :], s_ck[:, 1536 + grp * 512:1536 + (grp + 1) * 512], op=ALU.mult)
                sy.inc(v.tensor_tensor(s_kr[:, sl], s_w[0][:, :], s_w[1][:, :], op=ALU.add), "krf", 1)
        KRF = sy.val("krf")

        if upto == 1:
            @block.sync
            def _(q):
                sy.wait(q, "krf", KRF)
                sy.inc(q.dma_start(out=xn2_d.ap()[:, 0:1536], in_=s_kr[:, :]), "out", 16)
                sy.inc(q.dma_start(out=xn2_d.ap()[:, 2048:3584], in_=s_v[:, :]), "out", 16)
                sy.wait(q, "out", 32)
            return nc

        W_OFF = [0, 256, 768, 1280, 1792]
        W_SPAN = [256, 512, 512, 512, 256]
        W_Q0 = [0, 0, 256, 512, 768]
        @block.gpsimd
        def _(g):
            sy.wait(g, "krf", KRF)
            sy.inc(g.dma_start(out=s_aAB[0:1, 0:1024], in_=aA_d.ap()), "ab", 16)
            sy.inc(g.dma_start(out=s_aAB[0:1, 1024:2048], in_=aB_d.ap()), "ab", 16)

        for h in range(32):
            @block.gpsimd
            def _(g, h=h):
                if h >= 2:
                    sy.wait(g, "qmm", 2 * (h - 1))
                sy.inc(g.dma_start(out=s_str[:, (h % 2) * 4096:(h % 2 + 1) * 4096],
                                   in_=wq_d.ap()[:, h * 4096:(h + 1) * 4096]), "wq", 16)

            @block.tensor
            def _(t, h=h):
                sy.wait(t, "wq", 16 * (h + 1))
                if h == 0:
                    sy.wait(t, "krf", KRF)
                    sy.wait(t, "ab", 32)
                else:
                    sy.wait(t, "bl", h * 4)
                wq = s_str[:, (h % 2) * 4096:(h % 2 + 1) * 4096]
                for ng in range(2):
                    pp = p_a if ng == 0 else p_b
                    for c in range(32):
                        mm = t.matmul(pp[:, :], wq[:, c * 128:(c + 1) * 128],
                                      s_xnT[:, c * 1024 + ng * 512:c * 1024 + (ng + 1) * 512],
                                      start=(c == 0), stop=(c == 31))
                    sy.inc(mm, "qmm", 1)
                sy.wait(t, "qev", 2 * (h + 1))
                t.matmul(p_a[:, 0:512], s_rt[:, :], s_qr[:, 0:512], start=True, stop=True)
                sy.inc(t.matmul(p_b[:, 0:512], s_rt[:, :], s_qr[:, 512:1024], start=True, stop=True), "qrot", 1)
                sy.wait(t, "qrf", h + 1)
                for w in range(5):
                    span, q0, s0 = W_SPAN[w], W_Q0[w], 256 * w
                    for j in range(4):
                        if not (w == 0 or (w == 1 and j >= 2) or w >= 2):
                            pass
                        sy.wait(t, "sgev", max(0, sy.val("smm") - 1))
                        pp = p_a if j % 2 == 0 else p_b
                        sy.inc(t.matmul(pp[:, 0:span], s_kr[:, s0 + j * 128:s0 + (j + 1) * 128],
                                        s_qr[:, q0:q0 + span], start=True, stop=True), "smm", 1)
                    sy.wait(t, "sgev", sy.val("smm"))
                    for j in range(4):
                        mm = t.matmul(p_r[0:1, 0:span], s_ones1[:, :], s_sig[:, j * 512:j * 512 + span],
                                      start=(j == 0), stop=(j == 3))
                    sy.inc(mm, "dmm", 1)
                    sy.wait(t, "dev", sy.val("dmm"))
                    for j in range(4):
                        mm = t.matmul(p_a[:, 0:span], s_v[:, (s0 // 128 + j) * 128:(s0 // 128 + j + 1) * 128],
                                      s_sig[:, j * 512:j * 512 + span], start=(j == 0), stop=(j == 3))
                    sy.inc(mm, "amm", 1)
                    sy.wait(t, "aev", sy.val("amm"))

            @block.scalar
            def _(sc, h=h):
                sy.wait(sc, "qmm", 2 * h + 1)
                sc.activation(s_qr[:, 0:512], p_a[:, :], AF.Copy)
                sy.wait(sc, "qmm", 2 * h + 2)
                sy.inc(sc.activation(s_qr[:, 512:1024], p_b[:, :], AF.Copy), "qev", 2)
                for w in range(5):
                    span = W_SPAN[w]
                    base = sy.val("sgev")
                    for j in range(4):
                        sy.wait(sc, "smm", 20 * h + 4 * w + j + 1)
                        pp = p_a if j % 2 == 0 else p_b
                        kt = (256 * w) // 128 + j
                        sy.inc(sc.activation(s_sig[:, j * 512:j * 512 + span], pp[:, 0:span], AF.Sigmoid,
                                             scale=SCALE, bias=s_mb[:, kt:kt + 1]), "sgev", 1)
                    sy.wait(sc, "dmm", 5 * h + w + 1)
                    sy.selfedge(sc, sc.activation(s_den[0:1, W_OFF[w]:W_OFF[w] + span],
                                                  p_r[0:1, 0:span], AF.Copy))
                    sy.inc(sc.nop(), "dev", 1)
                    sy.wait(sc, "amm", 5 * h + w + 1)
                    sy.inc(sc.activation(s_avu[:, W_OFF[w]:W_OFF[w] + span], p_a[:, 0:span], AF.Copy), "aev", 1)

            @block.vector
            def _(v, h=h):
                sy.wait(v, "qrot", h + 1)
                v.tensor_tensor(s_w[0][:, :], s_qr[:, 0:512], s_cq[:, 0:512], op=ALU.mult)
                v.tensor_tensor(s_w[1][:, :], p_a[:, 0:512], s_cq[:, 1024:1536], op=ALU.mult)
                v.tensor_tensor(s_qr[:, 0:512], s_w[0][:, :], s_w[1][:, :], op=ALU.add)
                v.tensor_tensor(s_w[0][:, :], s_qr[:, 512:1024], s_cq[:, 512:1024], op=ALU.mult)
                v.tensor_tensor(s_w[1][:, :], p_b[:, 0:512], s_cq[:, 1536:2048], op=ALU.mult)
                sy.inc(v.tensor_tensor(s_qr[:, 512:1024], s_w[0][:, :], s_w[1][:, :], op=ALU.add), "qrf", 1)

            @block.vector
            def _(v, h=h):
                for m in range(4):
                    sy.wait(v, "aev", 5 * h + m + 2)
                    wA, wB = m, m + 1
                    offA = W_OFF[wA] + (256 * m - W_Q0[wA])
                    offB = W_OFF[wB] + (256 * m - W_Q0[wB])
                    sy.selfedge(v, v.reciprocal(s_w[0][0:1, 0:256], s_den[0:1, offA:offA + 256]))
                    sy.selfedge(v, v.reciprocal(s_w[0][0:1, 256:512], s_den[0:1, offB:offB + 256]))
                    v.tensor_tensor(s_crow[0:1, 0:256], s_aAB[0:1, 256 * m:256 * (m + 1)], s_w[0][0:1, 0:256], op=ALU.mult)
                    sy.inc(v.tensor_tensor(s_crow[0:1, 256:512], s_aAB[0:1, 1024 + 256 * m:1024 + 256 * (m + 1)], s_w[0][0:1, 256:512], op=ALU.mult), "crw", 1)
                    sy.wait(v, "cbc", sy.val("crw"))
                    v.tensor_tensor(s_w[0][:, 0:256], s_avu[:, offA:offA + 256], p_b[:, 0:256], op=ALU.mult)
                    v.tensor_tensor(s_w[1][:, 0:256], s_avu[:, offB:offB + 256], p_b[:, 256:512], op=ALU.mult)
                    sy.inc(v.tensor_tensor(s_big[:, h * 1024 + 256 * m:h * 1024 + 256 * (m + 1)],
                                           s_w[0][:, 0:256], s_w[1][:, 0:256], op=ALU.add), "bl", 1)

            @block.tensor
            def _(t, h=h):
                for m in range(4):
                    sy.wait(t, "crw", 4 * h + m + 1)
                    t.matmul(p_b[:, 0:256], s_kb[:, :], s_crow[0:1, 0:256], start=True, stop=True)
                    sy.inc(t.matmul(p_b[:, 256:512], s_kb[:, :], s_crow[0:1, 256:512], start=True, stop=True), "cbc", 1)
        ATTN = sy.val("bl")

        if upto == 2:
            @block.sync
            def _(q):
                sy.wait(q, "bl", ATTN)
                for part in range(4):
                    sy.inc(q.dma_start(out=xn2_d.ap()[:, part * 8192:(part + 1) * 8192],
                                       in_=s_big[:, part * 8192:(part + 1) * 8192]), "out", 16)
                sy.wait(q, "out", 64)

            @block.gpsimd
            def _(g):
                sy.wait(g, "bl", ATTN)
                sy.inc(g.dma_start(out=musig_d.ap(), in_=s_den[0:1, :]), "outd", 16)
                sy.inc(g.dma_start(out=y0_d.ap()[:, 0:2048], in_=s_avu[:, :]), "outd", 16)
                sy.inc(g.dma_start(out=h_d.ap()[0:1, 0:2048], in_=s_aAB[0:1, :]), "outd", 16)
                sy.inc(g.dma_start(out=h_d.ap()[1:2, 0:128], in_=s_kb[0:1, :]), "outd", 16)
                sy.inc(g.dma_start(out=h_d.ap()[2:3, 0:512], in_=s_crow[0:1, :]), "outd", 16)
                sy.inc(g.dma_start(out=h_d.ap()[3:4, 0:128], in_=s_onesk[0:1, :]), "outd", 16)
                sy.wait(g, "outd", 96)
            return nc

        fview = []
        for buf, ntile in [(s_sig, 4), (s_avu, 4), (s_qr, 2), (s_kr, 3), (s_v, 3),
                           (s_cq, 4), (s_ck, 6), (s_str[:, 8192:12288], 8)]:
            for i2 in range(ntile):
                fview.append(buf[:, i2 * 512:(i2 + 1) * 512])
        fview = fview[:32]
        s_musv = s_den[0:1, :].bitcast(F32)

        for half in range(2):
            @block.gpsimd
            def _(g, half=half):
                if half == 0:
                    sy.wait(g, "bl", ATTN)
                else:
                    sy.wait(g, "ep0h", 32)
                sy.wait(g, "dmu", 256)
                sy.inc(g.dma_start(out=s_musv[0:1, 0:512], in_=dmu.ap()[0:1, half * 512:(half + 1) * 512]), "mus", 16)
                sy.inc(g.dma_start(out=s_musv[0:1, 512:1024], in_=dmu.ap()[1:2, half * 512:(half + 1) * 512]), "mus", 16)
                for o in range(32):
                    if half * 32 + o >= 2:
                        sy.wait(g, "fev", 2 * (half * 32 + o - 1))
                    sy.inc(g.dma_start(out=s_str[:, (o % 2) * 4096:(o % 2 + 1) * 4096],
                                       in_=wo_d.ap()[:, o * 4096:(o + 1) * 4096]), "wo", 16)

            @block.tensor
            def _(t, half=half):
                if half == 0:
                    sy.wait(t, "bl", ATTN)
                else:
                    sy.wait(t, "ep0h", 32)
                for o in range(32):
                    sy.wait(t, "wo", 16 * (half * 32 + o + 1))
                    if half * 32 + o >= 1:
                        sy.wait(t, "fev", 2 * (half * 32 + o))
                    wob = s_str[:, (o % 2) * 4096:(o % 2 + 1) * 4096]
                    for c in range(32):
                        mm = t.matmul(p_a[:, :], wob[:, c * 128:(c + 1) * 128],
                                      s_big[:, c * 1024 + half * 512:c * 1024 + (half + 1) * 512],
                                      start=(c == 0), stop=(c == 31))
                    sy.inc(mm, "fmm", 1)
                    sy.wait(t, "fev", 2 * (half * 32 + o) + 2)
                    t.matmul(p_s[0:1, 0:512], s_ones1[:, :], fview[o], start=(o == 0), stop=(o == 31))
                    sy.inc(t.matmul(p_q[0:1, 0:512], s_ones1[:, :], s_sq[:, 0:512], start=(o == 0), stop=(o == 31)), "sqm", 1)

            @block.scalar
            def _(sc, half=half):
                for o in range(32):
                    sy.wait(sc, "fmm", half * 32 + o + 1)
                    if half * 32 + o >= 1:
                        sy.wait(sc, "sqm", half * 32 + o)
                    sc.activation(fview[o], p_a[:, :], AF.Copy)
                    sy.inc(sc.activation(s_sq[:, 0:512], p_a[:, :], AF.Square), "fev", 2)

            @block.vector
            def _(v, half=half):
                sy.wait(v, "sqm", 32 * (half + 1))
                sy.wait(v, "mus", 32 * (half + 1))
                v.tensor_copy(s_rows[0:1, 0:512], p_s[0:1, 0:512])
                v.tensor_copy(s_w[0][0:1, 0:512], p_q[0:1, 0:512])
                v.tensor_scalar(s_rows[0:1, 0:512], s_rows[0:1, 0:512], 1.0 / 4096, None, op0=ALU.mult)
                v.tensor_scalar(s_w[0][0:1, 0:512], s_w[0][0:1, 0:512], 1.0 / 4096, None, op0=ALU.mult)
                v.tensor_tensor(s_w[1][0:1, 0:512], s_rows[0:1, 0:512], s_rows[0:1, 0:512], op=ALU.mult)
                sy.inc(v.tensor_tensor(s_w[0][0:1, 0:512], s_w[0][0:1, 0:512], s_w[1][0:1, 0:512], op=ALU.subtract), "vq0", 1)

            @block.scalar
            def _(sc, half=half):
                sy.wait(sc, "vq0", half + 1)
                sy.inc(sc.activation(s_w[0][0:1, 0:512], s_w[0][0:1, 0:512], AF.Sqrt, bias=s_eps[0:1, :]), "sq0", 1)

            @block.vector
            def _(v, half=half):
                sy.wait(v, "sq0", half + 1)
                sy.selfedge(v, v.reciprocal(s_w[0][0:1, 0:512], s_w[0][0:1, 0:512]))
                v.tensor_tensor(s_w[1][0:1, 0:512], s_rows[0:1, 0:512], s_w[0][0:1, 0:512], op=ALU.mult)
                v.tensor_tensor(s_w[1][0:1, 0:512], s_musv[0:1, 0:512], s_w[1][0:1, 0:512], op=ALU.subtract)
                v.tensor_copy(s_rowb[0:1, 0:512], s_w[0][0:1, 0:512])
                v.tensor_copy(s_rowb[0:1, 512:1024], s_musv[0:1, 512:1024])
                sy.inc(v.tensor_copy(s_rowb[0:1, 1024:1536], s_w[1][0:1, 0:512]), "rb", 1)

            @block.tensor
            def _(t, half=half):
                sy.wait(t, "rb", sy.val("rb"))
                for r in range(3):
                    if sy.val("bc3") > 0:
                        sy.wait(t, "bce3", sy.val("bc3"))
                    sy.inc(t.matmul(p_b[:, :], s_kb[:, :], s_rowb[0:1, r * 512:(r + 1) * 512],
                                    start=True, stop=True), "bc3", 1)

            @block.vector
            def _(v, half=half):
                for r in range(3):
                    sy.wait(v, "bc3", sy.val("bce3") + 1)
                    sy.inc(v.tensor_copy(s_rep[r][:, :], p_b[:, :]), "bce3", 1)

            @block.vector
            def _(v, half=half):
                sy.wait(v, "bce3", sy.val("bce3"))
                for o in range(32):
                    dst = s_xnT[:, o * 1024 + half * 512:o * 1024 + (half + 1) * 512]
                    v.tensor_tensor(s_w[0][:, 0:512], fview[o], s_rep[0][:, :], op=ALU.mult)
                    v.tensor_tensor(s_w[1][:, 0:512], dst, s_rep[1][:, :], op=ALU.mult)
                    v.tensor_tensor(s_w[0][:, 0:512], s_w[0][:, 0:512], s_w[1][:, 0:512], op=ALU.add)
                    sy.inc(v.tensor_tensor(dst, s_w[0][:, 0:512], s_rep[2][:, :], op=ALU.add), "xn1", 1)

            @block.scalar
            def _(sc, half=half):
                for o in range(32):
                    sy.wait(sc, "xn1", half * 32 + o + 1)
                    if sy.val("xq1") > 0:
                        sy.wait(sc, "xqm", sy.val("xq1"))
                    sy.inc(sc.activation(s_sq[:, 0:512], s_xnT[:, o * 1024 + half * 512:o * 1024 + (half + 1) * 512], AF.Square), "xq1", 1)

            @block.tensor
            def _(t, half=half):
                for o in range(32):
                    sy.wait(t, "xq1", half * 32 + o + 1)
                    sy.inc(t.matmul(p_q[0:1, 512:1024], s_ones1[:, :], s_sq[:, 0:512], start=(o == 0), stop=(o == 31)), "xqm", 1)

            @block.vector
            def _(v, half=half):
                sy.wait(v, "xqm", 32 * (half + 1))
                v.tensor_copy(s_w[0][0:1, 0:512], p_q[0:1, 512:1024])
                v.tensor_scalar(s_w[0][0:1, 0:512], s_w[0][0:1, 0:512], 1.0 / 4096, None, op0=ALU.mult)
                v.tensor_tensor(s_w[1][0:1, 0:512], s_musv[0:1, 0:512], s_musv[0:1, 0:512], op=ALU.mult)
                sy.inc(v.tensor_tensor(s_w[0][0:1, 0:512], s_w[0][0:1, 0:512], s_w[1][0:1, 0:512], op=ALU.subtract), "vv1", 1)

            @block.scalar
            def _(sc, half=half):
                sy.wait(sc, "vv1", half + 1)
                sy.inc(sc.activation(s_w[0][0:1, 0:512], s_w[0][0:1, 0:512], AF.Sqrt, bias=s_eps[0:1, :]), "sg1s", 1)

            @block.gpsimd
            def _(g, half=half):
                sy.wait(g, "sg1v", half + 1)
                sy.inc(g.dma_start(out=dmu1.ap()[1:2, half * 512:(half + 1) * 512], in_=s_rows[0:1, 512:1024]), "dm1", 16)
                sy.inc(g.dma_start(out=dmu1.ap()[0:1, half * 512:(half + 1) * 512], in_=s_musv[0:1, 0:512]), "dm1", 16)

            @block.vector
            def _(v, half=half):
                sy.wait(v, "sg1s", half + 1)
                v.tensor_copy(s_rows[0:1, 512:1024], s_w[0][0:1, 0:512])
                sy.selfedge(v, v.reciprocal(s_w[0][0:1, 0:512], s_w[0][0:1, 0:512]))
                v.tensor_copy(s_rowb[0:1, 0:512], s_w[0][0:1, 0:512])
                v.tensor_copy(s_rowb[0:1, 512:1024], s_musv[0:1, 0:512])
                sy.inc(v.tensor_copy(s_w[1][0:1, 0:4], s_rowb[0:1, 0:4]), "sg1v", 1)

            @block.tensor
            def _(t, half=half):
                sy.wait(t, "sg1v", half + 1)
                for r in range(2):
                    if sy.val("bc1") > 0:
                        sy.wait(t, "bce1", sy.val("bc1"))
                    sy.inc(t.matmul(p_b[:, :], s_kb[:, :], s_rowb[0:1, r * 512:(r + 1) * 512],
                                    start=True, stop=True), "bc1", 1)

            @block.vector
            def _(v, half=half):
                for r in range(2):
                    sy.wait(v, "bc1", sy.val("bce1") + 1)
                    sy.inc(v.tensor_copy(s_rep[r][:, :], p_b[:, :]), "bce1", 1)

            @block.vector
            def _(v, half=half):
                sy.wait(v, "bce1", sy.val("bce1"))
                for o in range(32):
                    dst = s_xnT[:, o * 1024 + half * 512:o * 1024 + (half + 1) * 512]
                    v.tensor_tensor(s_w[0][:, 0:512], dst, s_rep[1][:, :], op=ALU.subtract)
                    sy.inc(v.tensor_tensor(dst, s_w[0][:, 0:512], s_rep[0][:, :], op=ALU.mult), "ep0h", 1)


        EP0 = 64

        if upto == 3:
            @block.sync
            def _(q):
                sy.wait(q, "ep0h", EP0)
                for part in range(4):
                    sy.inc(q.dma_start(out=xn2_d.ap()[:, part * 8192:(part + 1) * 8192],
                                       in_=s_xnT[:, part * 8192:(part + 1) * 8192]), "out", 16)
                sy.wait(q, "out", 64)
            return nc

        for _new, _old in [("lge", "xt"), ("lgm", "bnA"), ("gtb", "sgA"), ("gte", "nrmA"),
                           ("gtt", "trA"), ("gm", "evA"), ("gsg", "wkv"), ("gxd", "krv"),
                           ("lad", "kmm"), ("hle", "vev"), ("hlm", "vmm"), ("hg", "krot"),
                           ("f1e", "krf"), ("bwd", "ab"), ("f1m", "qmm"), ("f1s", "wq"),
                           ("mu1", "qev"), ("vq2", "qrot"), ("sq2", "qrf"), ("rb2", "sgev"),
                           ("bc4", "smm"), ("bce4", "dmm"), ("xn2", "dev"), ("xq2", "amm"),
                           ("xqm2", "aev"), ("st2", "crw"), ("vv2", "cbc"), ("sg2s", "mus"),
                           ("sg2v", "fev"), ("msd", "wo"), ("bc5", "fmm"), ("bce5", "sqm"),
                           ("ep1h", "dmu")]:
            sy.alias(_new, _old)
        s_lgb = s_cab[:, 0:224]
        s_sq2 = s_lg[:, :].bitcast(BF16)

        @block.tensor
        def _(t):
            sy.wait(t, "ep0h", EP0)
            for tt in range(8):
                if tt >= 1:
                    sy.wait(t, "lge", tt)
                for c in range(32):
                    mm = t.matmul(p_a[:, 0:28], s_xnT[:, c * 1024 + tt * 128:c * 1024 + (tt + 1) * 128],
                                  s_gw[:, c * 28:(c + 1) * 28], start=(c == 0), stop=(c == 31))
                sy.inc(mm, "lgm", 1)
            for tt in range(8):
                sy.wait(t, "gtb", tt + 1)
                if tt >= 1:
                    sy.wait(t, "gte", tt)
                sy.inc(t.transpose(p_t1[0:28, 0:128], s_lgb[:, tt * 28:(tt + 1) * 28], s_id[:, :]), "gtt", 1)

        @block.scalar
        def _(sc):
            for tt in range(8):
                sy.wait(sc, "lgm", tt + 1)
                sy.inc(sc.activation(s_lg[:, tt * 28:(tt + 1) * 28], p_a[:, 0:28], AF.Copy), "lge", 1)

        @block.vector
        def _(v):
            for tt in range(8):
                sy.wait(v, "lge", tt + 1)
                lg = s_lg[:, tt * 28:(tt + 1) * 28]
                c0 = s_stat[:, 0:1]
                sy.selfedge(v, v.tensor_reduce(c0, lg, axis=AX.X, op=ALU.max))
                sy.selfedge(v, v.tensor_scalar(s_w[0][:, 0:28], lg, c0, None, op0=ALU.is_equal))
                sy.selfedge(v, v.scalar_tensor_tensor(s_w[1][:, 0:28], s_w[0][:, 0:28], -1e9, lg, op0=ALU.mult, op1=ALU.add))
                sy.selfedge(v, v.tensor_reduce(s_stat[:, 1:2], s_w[1][:, 0:28], axis=AX.X, op=ALU.max))
                sy.selfedge(v, v.tensor_scalar(s_w[1][:, 0:28], lg, s_stat[:, 1:2], None, op0=ALU.is_equal))
                sy.selfedge(v, v.tensor_tensor(s_stat[:, 3:4], s_stat[:, 1:2], c0, op=ALU.subtract))
                sy.inc(v.nop(), "gm", 1)
                sy.wait(v, "gsg", tt + 1)
                sy.selfedge(v, v.tensor_scalar(s_stat[:, 5:6], s_stat[:, 4:5], -1.0, 1.0, op0=ALU.mult, op1=ALU.add))
                sy.selfedge(v, v.tensor_scalar(s_w[0][:, 0:28], s_w[0][:, 0:28], s_stat[:, 5:6], None, op0=ALU.mult))
                sy.selfedge(v, v.tensor_scalar(s_w[1][:, 0:28], s_w[1][:, 0:28], s_stat[:, 4:5], None, op0=ALU.mult))
                sy.selfedge(v, v.tensor_tensor(s_w[0][:, 0:28], s_w[0][:, 0:28], s_w[1][:, 0:28], op=ALU.add))
                sy.selfedge(v, v.tensor_copy(s_lgb[:, tt * 28:(tt + 1) * 28], s_w[0][:, 0:28]))
                sy.inc(v.nop(), "gtb", 1)
            for tt in range(8):
                sy.wait(v, "gtt", tt + 1)
                sy.inc(v.tensor_copy(s_gt[0:28, tt * 128:(tt + 1) * 128], p_t1[0:28, 0:128]), "gte", 1)

        @block.scalar
        def _(sc):
            for tt in range(8):
                sy.wait(sc, "gm", tt + 1)
                sy.selfedge(sc, sc.activation(s_stat[:, 4:5], s_stat[:, 3:4], AF.Sigmoid))
                sy.inc(sc.nop(), "gsg", 1)

        @block.gpsimd
        def _(g):
            sy.wait(g, "gte", 8)
            for rr in range(8):
                dst = s_gx[32 * (rr % 4):32 * (rr % 4) + 28, (rr // 4) * 1024:(rr // 4 + 1) * 1024]
                sy.inc(g.dma_start(out=dst, in_=s_gt[0:28, :]), "gxd", 16)
            sy.inc(g.dma_start(out=s_str[:, 0:8192], in_=la_d.ap()), "lad", 16)

        @block.tensor
        def _(t):
            sy.wait(t, "lad", 16)
            for m2 in range(2):
                for ng in range(2):
                    idx = m2 * 2 + ng
                    if idx >= 1:
                        sy.wait(t, "hle", idx)
                    for c in range(32):
                        mm = t.matmul(p_a[:, :], s_str[:, c * 256 + m2 * 128:c * 256 + (m2 + 1) * 128],
                                      s_xnT[:, c * 1024 + ng * 512:c * 1024 + (ng + 1) * 512],
                                      start=(c == 0), stop=(c == 31))
                    sy.inc(mm, "hlm", 1)

        @block.scalar
        def _(sc):
            for idx in range(4):
                sy.wait(sc, "hlm", idx + 1)
                m2, ng = divmod(idx, 2)
                sy.inc(sc.activation(s_hl[:, m2 * 1024 + ng * 512:m2 * 1024 + (ng + 1) * 512],
                                     p_a[:, :], AF.Copy), "hle", 1)

        @block.vector
        def _(v):
            sy.wait(v, "hle", 4)
            sy.wait(v, "gxd", 16 * 8)
            v.scalar_tensor_tensor(s_hl[:, 0:1024], s_hl[:, 0:1024], 2.0, s_gx[:, 0:1024], op0=ALU.mult, op1=ALU.mult)
            sy.inc(v.scalar_tensor_tensor(s_hl[:, 1024:2048], s_hl[:, 1024:2048], 2.0, s_gx[:, 1024:2048], op0=ALU.mult, op1=ALU.mult), "hg", 1)
        HG = sy.val("hg")

        @block.gpsimd
        def _(g):
            sy.wait(g, "hg", HG)
            for o in range(32):
                if o >= 2:
                    sy.wait(g, "f1e", 4 * (o - 1))
                sy.inc(g.dma_start(out=s_str[:, (o % 2) * 6144:(o % 2) * 6144 + 4352],
                                   in_=bw_d.ap()[:, o * 4352:(o + 1) * 4352]), "bwd", 16)

        @block.tensor
        def _(t):
            sy.wait(t, "hg", HG)
            for o in range(32):
                sy.wait(t, "bwd", 16 * (o + 1))
                if o >= 1:
                    sy.wait(t, "f1e", 4 * o)
                blk = s_str[:, (o % 2) * 6144:(o % 2) * 6144 + 4352]
                for ng in range(2):
                    pp = p_a if ng == 0 else p_b
                    for c in range(32):
                        t.matmul(pp[:, :], blk[:, c * 128:(c + 1) * 128],
                                 s_xnT[:, c * 1024 + ng * 512:c * 1024 + (ng + 1) * 512],
                                 start=(c == 0), stop=False)
                    t.matmul(pp[:, :], blk[:, 4096:4224], s_hl[:, 0 * 1024 + ng * 512:0 * 1024 + (ng + 1) * 512], start=False, stop=False)
                    mm = t.matmul(pp[:, :], blk[:, 4224:4352], s_hl[:, 1 * 1024 + ng * 512:1 * 1024 + (ng + 1) * 512], start=False, stop=True)
                sy.inc(mm, "f1m", 1)
                sy.wait(t, "f1e", 4 * o + 2)
                for ng in range(2):
                    t.matmul(p_s[0:1, ng * 512:(ng + 1) * 512], s_ones1[:, :],
                             s_big[:, o * 1024 + ng * 512:o * 1024 + (ng + 1) * 512], start=(o == 0), stop=(o == 31))
                    mm = t.matmul(p_q[0:1, ng * 512:(ng + 1) * 512], s_ones1[:, :],
                                  s_sq[:, 0:512] if ng == 0 else s_sq2[:, 0:512], start=(o == 0), stop=(o == 31))
                sy.inc(mm, "f1s", 1)

        @block.scalar
        def _(sc):
            for o in range(32):
                sy.wait(sc, "f1m", o + 1)
                if o >= 1:
                    sy.wait(sc, "f1s", o)
                sc.activation(s_big[:, o * 1024:o * 1024 + 512], p_a[:, :], AF.Copy)
                sc.activation(s_sq[:, 0:512], p_a[:, :], AF.Square)
                sc.activation(s_big[:, o * 1024 + 512:(o + 1) * 1024], p_b[:, :], AF.Copy)
                sy.inc(sc.activation(s_sq2[:, 0:512], p_b[:, :], AF.Square), "f1e", 4)
        F1S = 32

        for half in range(2):
            @block.gpsimd
            def _(g, half=half):
                sy.wait(g, "f1s", F1S)
                sy.wait(g, "dm1", 64)
                if half == 1:
                    sy.wait(g, "ep1h", 32)
                sy.inc(g.dma_start(out=s_musv[0:1, 0:512], in_=dmu1.ap()[0:1, half * 512:(half + 1) * 512]), "mu1", 16)
                sy.inc(g.dma_start(out=s_musv[0:1, 512:1024], in_=dmu1.ap()[1:2, half * 512:(half + 1) * 512]), "mu1", 16)

            @block.vector
            def _(v, half=half):
                sy.wait(v, "f1s", F1S)
                sy.wait(v, "mu1", 32 * (half + 1))
                if half == 1:
                    sy.wait(v, "ep1h", 32)
                v.tensor_copy(s_rows[0:1, 0:512], p_s[0:1, half * 512:(half + 1) * 512])
                v.tensor_copy(s_w[0][0:1, 0:512], p_q[0:1, half * 512:(half + 1) * 512])
                v.tensor_scalar(s_rows[0:1, 0:512], s_rows[0:1, 0:512], 1.0 / 4096, None, op0=ALU.mult)
                v.tensor_scalar(s_w[0][0:1, 0:512], s_w[0][0:1, 0:512], 1.0 / 4096, None, op0=ALU.mult)
                v.tensor_tensor(s_w[1][0:1, 0:512], s_rows[0:1, 0:512], s_rows[0:1, 0:512], op=ALU.mult)
                sy.inc(v.tensor_tensor(s_w[0][0:1, 0:512], s_w[0][0:1, 0:512], s_w[1][0:1, 0:512], op=ALU.subtract), "vq2", 1)

            @block.scalar
            def _(sc, half=half):
                sy.wait(sc, "vq2", half + 1)
                sy.inc(sc.activation(s_w[0][0:1, 0:512], s_w[0][0:1, 0:512], AF.Sqrt, bias=s_eps[0:1, :]), "sq2", 1)

            @block.vector
            def _(v, half=half):
                sy.wait(v, "sq2", half + 1)
                sy.selfedge(v, v.reciprocal(s_w[0][0:1, 0:512], s_w[0][0:1, 0:512]))
                v.tensor_tensor(s_w[1][0:1, 0:512], s_rows[0:1, 0:512], s_w[0][0:1, 0:512], op=ALU.mult)
                v.tensor_tensor(s_w[1][0:1, 0:512], s_musv[0:1, 0:512], s_w[1][0:1, 0:512], op=ALU.subtract)
                v.tensor_copy(s_rowb[0:1, 0:512], s_w[0][0:1, 0:512])
                v.tensor_copy(s_rowb[0:1, 512:1024], s_musv[0:1, 512:1024])
                sy.inc(v.tensor_copy(s_rowb[0:1, 1024:1536], s_w[1][0:1, 0:512]), "rb2", 1)

            @block.tensor
            def _(t, half=half):
                sy.wait(t, "rb2", half + 1)
                for r in range(3):
                    if sy.val("bc4") > 0:
                        sy.wait(t, "bce4", sy.val("bc4"))
                    sy.inc(t.matmul(p_b[:, :], s_kb[:, :], s_rowb[0:1, r * 512:(r + 1) * 512],
                                    start=True, stop=True), "bc4", 1)

            @block.vector
            def _(v, half=half):
                for r in range(3):
                    sy.wait(v, "bc4", sy.val("bce4") + 1)
                    sy.inc(v.tensor_copy(s_rep[r][:, :], p_b[:, :]), "bce4", 1)

            @block.vector
            def _(v, half=half):
                sy.wait(v, "bce4", sy.val("bce4"))
                for o in range(32):
                    dst = s_xnT[:, o * 1024 + half * 512:o * 1024 + (half + 1) * 512]
                    fx = s_big[:, o * 1024 + half * 512:o * 1024 + (half + 1) * 512]
                    v.tensor_tensor(s_w[0][:, 0:512], fx, s_rep[0][:, :], op=ALU.mult)
                    v.tensor_tensor(s_w[1][:, 0:512], dst, s_rep[1][:, :], op=ALU.mult)
                    v.tensor_tensor(s_w[0][:, 0:512], s_w[0][:, 0:512], s_w[1][:, 0:512], op=ALU.add)
                    sy.inc(v.tensor_tensor(dst, s_w[0][:, 0:512], s_rep[2][:, :], op=ALU.add), "xn2", 1)

            @block.scalar
            def _(sc, half=half):
                for o in range(32):
                    sy.wait(sc, "xn2", half * 32 + o + 1)
                    if sy.val("xq2") > 0:
                        sy.wait(sc, "xqm2", sy.val("xq2"))
                    sy.inc(sc.activation(s_sq[:, 0:512], s_xnT[:, o * 1024 + half * 512:o * 1024 + (half + 1) * 512], AF.Square), "xq2", 1)

            @block.tensor
            def _(t, half=half):
                for o in range(32):
                    sy.wait(t, "xq2", half * 32 + o + 1)
                    sy.inc(t.matmul(p_q[0:1, half * 512:(half + 1) * 512], s_ones1[:, :], s_sq[:, 0:512],
                                    start=(o == 0), stop=(o == 31)), "xqm2", 1)

            @block.vector
            def _(v, half=half):
                sy.wait(v, "xqm2", 32 * (half + 1))
                v.tensor_copy(s_w[0][0:1, 0:512], p_q[0:1, half * 512:(half + 1) * 512])
                v.tensor_scalar(s_w[0][0:1, 0:512], s_w[0][0:1, 0:512], 1.0 / 4096, None, op0=ALU.mult)
                v.tensor_tensor(s_w[1][0:1, 0:512], s_musv[0:1, 0:512], s_musv[0:1, 0:512], op=ALU.mult)
                sy.inc(v.tensor_tensor(s_w[0][0:1, 0:512], s_w[0][0:1, 0:512], s_w[1][0:1, 0:512], op=ALU.subtract), "vv2", 1)

            @block.scalar
            def _(sc, half=half):
                sy.wait(sc, "vv2", half + 1)
                sy.inc(sc.activation(s_w[0][0:1, 0:512], s_w[0][0:1, 0:512], AF.Sqrt, bias=s_eps[0:1, :]), "sg2s", 1)

            @block.vector
            def _(v, half=half):
                sy.wait(v, "sg2s", half + 1)
                v.tensor_copy(s_rows[0:1, 512:1024], s_w[0][0:1, 0:512])
                sy.selfedge(v, v.reciprocal(s_w[0][0:1, 0:512], s_w[0][0:1, 0:512]))
                v.tensor_copy(s_rowb[0:1, 0:512], s_w[0][0:1, 0:512])
                v.tensor_copy(s_rowb[0:1, 512:1024], s_musv[0:1, 0:512])
                sy.inc(v.tensor_copy(s_w[1][0:1, 0:4], s_rowb[0:1, 0:4]), "sg2v", 1)

            @block.gpsimd
            def _(g, half=half):
                sy.wait(g, "sg2v", half + 1)
                sy.inc(g.dma_start(out=musig_d.ap()[0:1, half * 512:(half + 1) * 512], in_=s_musv[0:1, 0:512]), "msd", 16)
                sy.inc(g.dma_start(out=musig_d.ap()[0:1, 1024 + half * 512:1024 + (half + 1) * 512], in_=s_rows[0:1, 512:1024]), "msd", 16)

            @block.tensor
            def _(t, half=half):
                sy.wait(t, "sg2v", half + 1)
                for r in range(2):
                    if sy.val("bc5") > 0:
                        sy.wait(t, "bce5", sy.val("bc5"))
                    sy.inc(t.matmul(p_b[:, :], s_kb[:, :], s_rowb[0:1, r * 512:(r + 1) * 512],
                                    start=True, stop=True), "bc5", 1)

            @block.vector
            def _(v, half=half):
                for r in range(2):
                    sy.wait(v, "bc5", sy.val("bce5") + 1)
                    sy.inc(v.tensor_copy(s_rep[r][:, :], p_b[:, :]), "bce5", 1)

            @block.vector
            def _(v, half=half):
                sy.wait(v, "bce5", sy.val("bce5"))
                for o in range(32):
                    dst = s_xnT[:, o * 1024 + half * 512:o * 1024 + (half + 1) * 512]
                    v.tensor_tensor(s_w[0][:, 0:512], dst, s_rep[1][:, :], op=ALU.subtract)
                    sy.inc(v.tensor_tensor(dst, s_w[0][:, 0:512], s_rep[0][:, :], op=ALU.mult), "ep1h", 1)
        EP1 = 64

        if upto == 4:
            @block.sync
            def _(q):
                sy.wait(q, "ep1h", EP1)
                for part in range(4):
                    sy.inc(q.dma_start(out=xn2_d.ap()[:, part * 8192:(part + 1) * 8192],
                                       in_=s_xnT[:, part * 8192:(part + 1) * 8192]), "out", 16)
                sy.wait(q, "msd", 64)
                sy.wait(q, "out", 64)
            return nc

        for _new, _old in [("whe", "vq0"), ("wnd", "sq0"), ("whm", "rb"), ("gsm", "bc3"),
                           ("gsq", "bce3"), ("ghb", "xn1"), ("gst", "xq1"), ("gvv", "xqm"),
                           ("gsr", "bl"), ("grb", "vv1"), ("gbc", "sg1s"), ("gbe", "sg1v"),
                           ("gy", "bc1"), ("gye", "bce1"), ("out", "dq")]:
            sy.alias(_new, _old)
        s_hF = s_big[:, 0:8192].bitcast(F32)
        s_yF = s_big[:, 8192:16384].bitcast(F32)

        @block.gpsimd
        def _(g):
            sy.wait(g, "ep1h", EP1)
            for m in range(4):
                if m >= 2:
                    sy.wait(g, "whe", 2 * (m - 1))
                sy.inc(g.dma_start(out=s_str[:, (m % 2) * 4096:(m % 2 + 1) * 4096],
                                   in_=win_d.ap()[:, m * 4096:(m + 1) * 4096]), "wnd", 16)

        @block.tensor
        def _(t):
            sy.wait(t, "ep1h", EP1)
            for m in range(4):
                sy.wait(t, "wnd", 16 * (m + 1))
                if m >= 1:
                    sy.wait(t, "whe", 2 * m)
                blk = s_str[:, (m % 2) * 4096:(m % 2 + 1) * 4096]
                for ng in range(2):
                    pp = p_a if ng == 0 else p_b
                    for c in range(32):
                        mm = t.matmul(pp[:, :], blk[:, c * 128:(c + 1) * 128],
                                      s_xnT[:, c * 1024 + ng * 512:c * 1024 + (ng + 1) * 512],
                                      start=(c == 0), stop=(c == 31))
                sy.inc(mm, "whm", 1)

        @block.vector
        def _(v):
            for m in range(4):
                sy.wait(v, "whm", m + 1)
                v.tensor_scalar(s_hF[:, m * 1024:m * 1024 + 512], p_a[:, :], s_pc[:, m:m + 1], None, op0=ALU.add)
                sy.inc(v.tensor_scalar(s_hF[:, m * 1024 + 512:(m + 1) * 1024], p_b[:, :], s_pc[:, m:m + 1], None, op0=ALU.add), "whe", 2)
        WHE = 8

        s_gtb = s_big[:, 16384:20480]

        @block.scalar
        def _(sc):
            sy.wait(sc, "whe", WHE)
            for i2 in range(4):
                if i2 >= 1:
                    sy.wait(sc, "gsm", 2 * i2)
                sc.activation(s_sq[:, 0:512], s_hF[:, i2 * 1024:i2 * 1024 + 512], AF.Square)
                sy.inc(sc.activation(s_sq2[:, 0:512], s_hF[:, i2 * 1024 + 512:(i2 + 1) * 1024], AF.Square), "gsq", 2)

        @block.vector
        def _(v):
            sy.wait(v, "whe", WHE)
            for i2 in range(4):
                sy.inc(v.tensor_copy(s_gtb[:, i2 * 1024:(i2 + 1) * 1024], s_hF[:, i2 * 1024:(i2 + 1) * 1024]), "ghb", 1)

        @block.tensor
        def _(t):
            for i2 in range(4):
                sy.wait(t, "ghb", i2 + 1)
                sy.wait(t, "gsq", 2 * (i2 + 1))
                t.matmul(p_s[0:1, 0:512], s_ones1[:, :], s_gtb[:, i2 * 1024:i2 * 1024 + 512], start=(i2 == 0), stop=(i2 == 3))
                t.matmul(p_s[0:1, 512:1024], s_ones1[:, :], s_gtb[:, i2 * 1024 + 512:(i2 + 1) * 1024], start=(i2 == 0), stop=(i2 == 3))
                t.matmul(p_q[0:1, 0:512], s_ones1[:, :], s_sq[:, 0:512], start=(i2 == 0), stop=(i2 == 3))
                sy.inc(t.matmul(p_q[0:1, 512:1024], s_ones1[:, :], s_sq2[:, 0:512], start=(i2 == 0), stop=(i2 == 3)), "gsm", 2)

        @block.vector
        def _(v):
            sy.wait(v, "gsm", 8)
            for part in range(2):
                v.tensor_copy(s_rows[0:1, part * 512:(part + 1) * 512], p_s[0:1, part * 512:(part + 1) * 512])
                v.tensor_copy(s_w[part][0:1, 0:512], p_q[0:1, part * 512:(part + 1) * 512])
            v.tensor_scalar(s_rows[0:1, 0:1024], s_rows[0:1, 0:1024], 1.0 / 512, None, op0=ALU.mult)
            v.tensor_scalar(s_w[0][0:1, 0:512], s_w[0][0:1, 0:512], 1.0 / 512, None, op0=ALU.mult)
            v.tensor_scalar(s_w[1][0:1, 0:512], s_w[1][0:1, 0:512], 1.0 / 512, None, op0=ALU.mult)
            v.tensor_tensor(s_rowb[0:1, 0:1024].bitcast(F32)[0:1, 0:512], s_rows[0:1, 0:512], s_rows[0:1, 0:512], op=ALU.mult)
            v.tensor_tensor(s_w[0][0:1, 0:512], s_w[0][0:1, 0:512], s_rowb[0:1, 0:1024].bitcast(F32)[0:1, 0:512], op=ALU.subtract)
            v.tensor_tensor(s_rowb[0:1, 0:1024].bitcast(F32)[0:1, 0:512], s_rows[0:1, 512:1024], s_rows[0:1, 512:1024], op=ALU.mult)
            sy.inc(v.tensor_tensor(s_w[1][0:1, 0:512], s_w[1][0:1, 0:512], s_rowb[0:1, 0:1024].bitcast(F32)[0:1, 0:512], op=ALU.subtract), "gvv", 1)

        @block.scalar
        def _(sc):
            sy.wait(sc, "gvv", 1)
            sc.activation(s_w[0][0:1, 0:512], s_w[0][0:1, 0:512], AF.Sqrt, bias=s_eps[0:1, :])
            sy.inc(sc.activation(s_w[1][0:1, 0:512], s_w[1][0:1, 0:512], AF.Sqrt, bias=s_eps[0:1, :]), "gsr", 1)

        @block.vector
        def _(v):
            sy.wait(v, "gsr", 1)
            sy.selfedge(v, v.reciprocal(s_w[0][0:1, 0:512], s_w[0][0:1, 0:512]))
            sy.selfedge(v, v.reciprocal(s_w[1][0:1, 0:512], s_w[1][0:1, 0:512]))
            v.tensor_copy(s_rowb[0:1, 0:512], s_rows[0:1, 0:512])
            v.tensor_copy(s_rowb[0:1, 512:1024], s_rows[0:1, 512:1024])
            v.tensor_copy(s_rowb[0:1, 1024:1536], s_w[0][0:1, 0:512])
            sy.inc(v.tensor_copy(s_rowb[0:1, 1536:2048], s_w[1][0:1, 0:512]), "grb", 1)

        @block.tensor
        def _(t):
            sy.wait(t, "grb", 1)
            for r in range(4):
                if sy.val("gbc") > 0:
                    sy.wait(t, "gbe", sy.val("gbc"))
                sy.inc(t.matmul(p_b[:, :], s_kb[:, :], s_rowb[0:1, r * 512:(r + 1) * 512],
                                start=True, stop=True), "gbc", 1)

        @block.vector
        def _(v):
            dsts = [s_rep[0], s_rep[1], s_rep[2], s_w[0]]
            for r in range(4):
                sy.wait(v, "gbc", sy.val("gbe") + 1)
                sy.inc(v.tensor_copy(dsts[r][:, 0:512], p_b[:, :]), "gbe", 1)

        @block.vector
        def _(v):
            sy.wait(v, "gbe", 4)
            for i2 in range(4):
                for part in range(2):
                    idx = i2 * 2 + part
                    if idx >= 1:
                        sy.wait(v, "gye", idx)
                    sl = slice(i2 * 1024 + part * 512, i2 * 1024 + (part + 1) * 512)
                    mu = s_rep[part]
                    rr2 = s_rep[2] if part == 0 else s_w[0]
                    v.tensor_tensor(s_w[1][:, 0:512], s_hF[:, sl], mu[:, 0:512], op=ALU.subtract)
                    sy.inc(v.tensor_tensor(s_w[1][:, 0:512], s_w[1][:, 0:512], rr2[:, 0:512], op=ALU.mult), "gy", 1)

        @block.scalar
        def _(sc):
            for i2 in range(4):
                for part in range(2):
                    sy.wait(sc, "gy", i2 * 2 + part + 1)
                    sl = slice(i2 * 1024 + part * 512, i2 * 1024 + (part + 1) * 512)
                    sy.inc(sc.activation(s_yF[:, sl], s_w[1][:, 0:512], AF.Silu,
                                         scale=s_pc[:, 4 + i2:5 + i2], bias=s_pc[:, 8 + i2:9 + i2]), "gye", 1)
        GYE = 8

        @block.sync
        def _(q):
            sy.wait(q, "ep1h", EP1)
            for part in range(4):
                sy.inc(q.dma_start(out=xn2_d.ap()[:, part * 8192:(part + 1) * 8192],
                                   in_=s_xnT[:, part * 8192:(part + 1) * 8192]), "out", 16)
            sy.wait(q, "whe", WHE)
            sy.inc(q.dma_start(out=h_d.ap(), in_=s_hF[:, :]), "out", 16)
            sy.wait(q, "gye", GYE)
            sy.inc(q.dma_start(out=y0_d.ap(), in_=s_yF[:, :]), "out", 16)
            sy.wait(q, "msd", 64)
            sy.wait(q, "out", 96)


    return nc


def rope_cache_ext():
    inv_freq = 1.0 / 10000.0 ** (np.arange(0, 128, 2, dtype=np.float32) / 128)
    t = np.arange(-256, 8448, dtype=np.float32)
    f = np.outer(t, inv_freq)
    cos = np.zeros((8704, 128), np.float32)
    sin = np.zeros((8704, 128), np.float32)
    cos[:, 0::2] = np.cos(f); cos[:, 1::2] = np.cos(f)
    sin[:, 0::2] = np.sin(f); sin[:, 1::2] = np.sin(f)
    return cos, sin


def biga_shared(Wq, Wk, Wv, Wo, base_W, gate_W, lora_A, lora_B, Win, bin_, ln1w0, ln1b0):
    f32 = np.float32
    kc = lambda W, o: kstack(np.ascontiguousarray(W.T[:, 128 * o:128 * (o + 1)], dtype=f32))
    d = {}
    d["wqT"] = np.concatenate([kc(Wq, h) for h in range(32)], axis=1)
    d["wkT"] = kstack(np.ascontiguousarray(Wk.T, dtype=f32))
    d["wvT"] = kstack(np.ascontiguousarray(Wv.T, dtype=f32))
    d["woT"] = np.concatenate([kc(Wo, o) for o in range(32)], axis=1)
    lbT = np.zeros((2, 128, 4096), f32)
    for e in range(28):
        for r in range(8):
            m2, rr = divmod(r, 4)
            lbT[m2, 32 * rr + e, :] = lora_B[e, :, r]
    blocks = []
    for o in range(32):
        blocks.append(np.concatenate(
            [kc(base_W, o), lbT[0][:, 128 * o:128 * (o + 1)], lbT[1][:, 128 * o:128 * (o + 1)]], axis=1))
    d["bwT"] = np.ascontiguousarray(np.concatenate(blocks, axis=1))
    d["gwT"] = kstack(np.ascontiguousarray(gate_W.T, dtype=f32))
    laP = np.zeros((4096, 256), f32)
    for e in range(28):
        for r in range(8):
            m2, rr = divmod(r, 4)
            laP[:, m2 * 128 + 32 * rr + e] = lora_A[e, r, :]
    d["laT"] = kstack(laP)
    d["winT"] = np.concatenate([kc(Win, m) for m in range(4)], axis=1)
    pc = np.zeros((128, 12), f32)
    for j in range(4):
        pc[:, j] = bin_[128 * j:128 * (j + 1)]
        pc[:, 4 + j] = ln1w0[128 * j:128 * (j + 1)]
        pc[:, 8 + j] = ln1b0[128 * j:128 * (j + 1)]
    d["pc"] = pc
    d["onesk"] = np.ones((1, 128), f32)
    d["cstE"] = np.full((128, 1), 1e-5, f32)
    d["cstO"] = np.ones((128, 1), f32)
    d["ident"] = np.eye(128, dtype=f32)
    R = np.zeros((128, 128), f32)
    for i in range(64):
        R[i + 64, i] = -1.0
        R[i, i + 64] = 1.0
    d["rot"] = R
    d["_cos"], d["_sin"] = rope_cache_ext()
    return d


def biga_core_inputs(c, x0, sh):
    f32 = np.float32
    im = {k: v for k, v in sh.items() if not k.startswith("_")}
    xin = np.zeros((1536, 4096), f32)
    lo, hi = 1024 * c - 256, 1024 * c + 1280
    slo, shi = max(lo, 0), min(hi, 8192)
    xin[slo - lo:shi - lo, :] = x0[slo:shi]
    im["xin"] = xin
    cos, sin = sh["_cos"], sh["_sin"]
    im["cosq"] = np.ascontiguousarray(cos[256 + 1024 * c:256 + 1024 * c + 1024].T)
    im["sinq"] = np.ascontiguousarray(sin[256 + 1024 * c:256 + 1024 * c + 1024].T)
    im["cosk"] = np.ascontiguousarray(cos[1024 * c:1024 * c + 1536].T)
    im["sink"] = np.ascontiguousarray(sin[1024 * c:1024 * c + 1536].T)
    al = np.arange(256, dtype=f32) / 255.0
    aA = np.zeros((1, 1024), f32)
    aB = np.zeros((1, 1024), f32)
    for qb in range(4):
        if 4 * c + qb == 0:
            aA[0, 256 * qb:256 * (qb + 1)] = 0.0
            aB[0, 256 * qb:256 * (qb + 1)] = 1.0
        else:
            aA[0, 256 * qb:256 * (qb + 1)] = 1.0 - al
            aB[0, 256 * qb:256 * (qb + 1)] = al
    im["aA"], im["aB"] = aA, aB
    mb = np.zeros((128, 12), f32)
    if c == 0:
        mb[:, 0:2] = -60.0
    if c == NCORE - 1:
        mb[:, 10:12] = -60.0
    im["mb"] = mb
    return im


def build_bigb(upto=9):
    nc = bass.Bass("TRN2", target_bir_lowering=False)
    dI = lambda n, s: nc.dram_tensor(n, s, F32, kind="ExternalInput")
    dB = lambda n, s: nc.dram_tensor(n, s, BF16, kind="ExternalInput")
    xn2_d = dB("xn2", [128, 32768])
    musig_d = dI("musig", [1, 2048])
    h_d = dI("h", [128, 4096])
    pc_d = dI("pc", [128, 40])
    wout_d = dB("woutT", [128, 16384])
    a13_d = dB("a13T", [128, 128])
    w13_d = dB("w13", [128, 86 * 2432])
    w2s_d = dB("w2s", [128, 32 * 2944])
    onesk_d = dI("onesk", [1, 128])
    cstE_d = dI("cstE", [128, 1])
    cstO_d = dI("cstO", [128, 1])
    out_d = nc.dram_tensor("out", [128, 32768], F32, kind="ExternalOutput")
    dbg_d = nc.dram_tensor("dbg", [4, 2048], F32, kind="ExternalOutput")

    def fold_segs(m):
        f0 = (128 * m) % 2752
        j = (128 * m) // 2752
        raw = []
        if f0 + 128 <= 2752:
            raw.append((0, f0, 128, j == 0))
        else:
            l1 = 2752 - f0
            raw.append((0, f0, l1, j == 0))
            raw.append((l1, 0, 128 - l1, False))
        segs = []
        for (sl, dp, ln, fw) in raw:
            while ln > 0:
                take = min(ln, 128 - (dp % 128))
                segs.append((sl, dp, take, fw))
                sl += take
                dp += take
                ln -= take
        return segs

    with contextlib.ExitStack() as st:
        T = lambda n, s, d=F32: st.enter_context(nc.sbuf_tensor(n, s, d))
        PS = lambda n, s, d=F32: st.enter_context(nc.psum_tensor(n, s, d))
        sy = Sy(nc, st)
        block = st.enter_context(nc.Block())

        s_xnT = T("s_xnT", [128, 32768], BF16)
        s_big = T("s_big", [128, 16384], BF16)
        s_guf = T("s_guf", [128, 22528], BF16)
        s_str = T("s_str", [128, 5888], BF16)
        s_gud = T("s_gud", [128, 2048], BF16)
        s_ms = T("s_ms", [1, 2048], BF16)
        s_sg3 = s_ms[0:1, 1024:2048]
        s_rows = T("s_rows", [1, 1024])
        s_w = [T(f"s_w{i}", [128, 512]) for i in range(2)]
        s_rep = [T(f"s_rep{i}", [128, 512]) for i in range(3)]
        s_rowb = T("s_rowb", [1, 2048], BF16)
        s_t13 = T("s_t13", [2, 2048], BF16)
        s_t2 = T("s_t2", [2, 1024], BF16)
        s_a13 = T("s_a13", [128, 128], BF16)
        s_sq = T("s_sq", [128, 512], BF16)
        s_gsh = T("s_gsh", [128, 1024], BF16)
        s_kb = T("s_kb", [1, 128], BF16)
        s_ones1 = T("s_ones1", [128, 1], BF16)
        s_eps = T("s_eps", [128, 1])
        s_pc = T("s_pc", [128, 40])

        s_h = s_guf[:, 0:8192].bitcast(F32)
        s_hn = s_guf[:, 8192:12288]
        s_hb = s_guf[:, 12288:16384]
        s_sq2 = s_guf[:, 16384:16896]
        s_gate = s_str[:, 4864:5888]
        s_gf = s_gud[:, :].bitcast(F32)
        s_xs4 = s_big[:, 0:8192]

        p_a = PS("p_a", [128, 512])
        p_b = PS("p_b", [128, 512])
        p_c = PS("p_c", [128, 512])
        p_d = PS("p_d", [128, 512])
        p_s = PS("p_s", [1, 1024])
        p_t2 = PS("p_t2", [4, 1024])
        p_sq = p_t2[0:1, :]

        @block.sync
        def _(q):
            for dst, src in [(s_pc[:, :], pc_d.ap()), (s_eps[:, :], cstE_d.ap()),
                             (s_a13[:, :], a13_d.ap())]:
                sy.inc(q.dma_start(out=dst, in_=src), "dq", 16)
            for part in range(4):
                sy.inc(q.dma_start(out=s_xnT[:, part * 8192:(part + 1) * 8192],
                                   in_=xn2_d.ap()[:, part * 8192:(part + 1) * 8192]), "dq", 16)

        @block.gpsimd
        def _(g):
            sy.inc(g.dma_start(out=s_ones1[:, :], in_=cstO_d.ap()), "dg", 16)
            sy.inc(g.dma_start(out=s_ms[:, :], in_=musig_d.ap()), "dg", 16)
            sy.inc(g.dma_start(out=s_kb[:, :], in_=onesk_d.ap()), "dg", 16)
            sy.inc(g.dma_start(out=s_h[:, :], in_=h_d.ap()), "dg", 16)
        LDQ, LDG = sy.val("dq"), sy.val("dg")

        if upto != 85:
            @block.vector
            def _(v):
                sy.wait(v, "dq", LDQ)
                sy.wait(v, "dg", LDG)
                for i2 in range(4):
                    sy.inc(v.tensor_copy(s_hb[:, i2 * 1024:(i2 + 1) * 1024], s_h[:, i2 * 1024:(i2 + 1) * 1024]), "hbv", 1)

            @block.scalar
            def _(sc):
                sy.wait(sc, "dq", LDQ)
                sy.wait(sc, "dg", LDG)
                for i2 in range(4):
                    if i2 >= 1:
                        sy.wait(sc, "hsm", 2 * i2)
                    sc.activation(s_sq[:, 0:512], s_h[:, i2 * 1024:i2 * 1024 + 512], AF.Square)
                    sy.inc(sc.activation(s_sq2[:, 0:512], s_h[:, i2 * 1024 + 512:(i2 + 1) * 1024], AF.Square), "hsq", 2)

            @block.tensor
            def _(t):
                for i2 in range(4):
                    sy.wait(t, "hbv", i2 + 1)
                    sy.wait(t, "hsq", 2 * (i2 + 1))
                    t.matmul(p_s[0:1, 0:512], s_ones1[:, :], s_hb[:, i2 * 1024:i2 * 1024 + 512], start=(i2 == 0), stop=(i2 == 3))
                    t.matmul(p_s[0:1, 512:1024], s_ones1[:, :], s_hb[:, i2 * 1024 + 512:(i2 + 1) * 1024], start=(i2 == 0), stop=(i2 == 3))
                    t.matmul(p_sq[0:1, 0:512], s_ones1[:, :], s_sq[:, 0:512], start=(i2 == 0), stop=(i2 == 3))
                    sy.inc(t.matmul(p_sq[0:1, 512:1024], s_ones1[:, :], s_sq2[:, 0:512], start=(i2 == 0), stop=(i2 == 3)), "hsm", 2)

            @block.vector
            def _(v):
                sy.wait(v, "hsm", 8)
                for part in range(2):
                    v.tensor_copy(s_rows[0:1, part * 512:(part + 1) * 512], p_s[0:1, part * 512:(part + 1) * 512])
                    v.tensor_copy(s_w[part][0:1, 0:512], p_sq[0:1, part * 512:(part + 1) * 512])
                v.tensor_scalar(s_rows[0:1, 0:1024], s_rows[0:1, 0:1024], 1.0 / 512, None, op0=ALU.mult)
                v.tensor_scalar(s_w[0][0:1, 0:512], s_w[0][0:1, 0:512], 1.0 / 512, None, op0=ALU.mult)
                v.tensor_scalar(s_w[1][0:1, 0:512], s_w[1][0:1, 0:512], 1.0 / 512, None, op0=ALU.mult)
                v.tensor_tensor(s_gf[0:1, 0:512], s_rows[0:1, 0:512], s_rows[0:1, 0:512], op=ALU.mult)
                v.tensor_tensor(s_w[0][0:1, 0:512], s_w[0][0:1, 0:512], s_gf[0:1, 0:512], op=ALU.subtract)
                v.tensor_scalar(s_w[0][0:1, 0:512], s_w[0][0:1, 0:512], 1e-5, None, op0=ALU.add)
                v.tensor_tensor(s_gf[0:1, 0:512], s_rows[0:1, 512:1024], s_rows[0:1, 512:1024], op=ALU.mult)
                v.tensor_tensor(s_w[1][0:1, 0:512], s_w[1][0:1, 0:512], s_gf[0:1, 0:512], op=ALU.subtract)
                sy.inc(v.tensor_scalar(s_w[1][0:1, 0:512], s_w[1][0:1, 0:512], 1e-5, None, op0=ALU.add), "hvv", 1)

            @block.scalar
            def _(sc):
                sy.wait(sc, "hvv", 1)
                sc.activation(s_w[0][0:1, 0:512], s_w[0][0:1, 0:512], AF.Sqrt)
                sy.inc(sc.activation(s_w[1][0:1, 0:512], s_w[1][0:1, 0:512], AF.Sqrt), "hsr", 1)

            @block.vector
            def _(v):
                sy.wait(v, "hsr", 1)
                sy.selfedge(v, v.reciprocal(s_w[0][0:1, 0:512], s_w[0][0:1, 0:512]))
                sy.selfedge(v, v.reciprocal(s_w[1][0:1, 0:512], s_w[1][0:1, 0:512]))
                v.tensor_copy(s_rowb[0:1, 0:512], s_rows[0:1, 0:512])
                v.tensor_copy(s_rowb[0:1, 512:1024], s_rows[0:1, 512:1024])
                v.tensor_copy(s_rowb[0:1, 1024:1536], s_w[0][0:1, 0:512])
                sy.inc(v.tensor_copy(s_rowb[0:1, 1536:2048], s_w[1][0:1, 0:512]), "hrb", 1)

            @block.tensor
            def _(t):
                sy.wait(t, "hrb", 1)
                for r in range(4):
                    if sy.val("hbc") > 0:
                        sy.wait(t, "hbe", sy.val("hbc"))
                    sy.inc(t.matmul(p_b[:, :], s_kb[:, :], s_rowb[0:1, r * 512:(r + 1) * 512],
                                    start=True, stop=True), "hbc", 1)

            @block.vector
            def _(v):
                dsts = [s_rep[0], s_rep[1], s_rep[2], s_gf[:, 0:512]]
                for r in range(4):
                    sy.wait(v, "hbc", sy.val("hbe") + 1)
                    sy.inc(v.tensor_copy(dsts[r][:, 0:512], p_b[:, :]), "hbe", 1)

            @block.vector
            def _(v):
                sy.wait(v, "hbe", 4)
                for i2 in range(4):
                    for part in range(2):
                        sl = slice(i2 * 1024 + part * 512, i2 * 1024 + (part + 1) * 512)
                        mu = s_rep[part]
                        rr2 = s_rep[2] if part == 0 else s_gf[:, 0:512]
                        v.tensor_tensor(s_w[0][:, 0:512], s_h[:, sl], mu[:, 0:512], op=ALU.subtract)
                        v.tensor_tensor(s_w[0][:, 0:512], s_w[0][:, 0:512], rr2[:, 0:512], op=ALU.mult)
                        sy.inc(v.tensor_scalar(s_hn[:, sl], s_w[0][:, 0:512], s_pc[:, i2:i2 + 1], s_pc[:, 4 + i2:5 + i2],
                                               op0=ALU.mult, op1=ALU.add), "hnv", 1)
            HNV = 8

            for half in range(2):
                @block.gpsimd
                def _(g, half=half):
                    if half == 0:
                        sy.wait(g, "hnv", HNV)
                    else:
                        sy.wait(g, "ep2", 32)
                    for o in range(32):
                        if half * 32 + o >= 2:
                            sy.wait(g, "f2e", half * 32 + o - 1)
                        sy.inc(g.dma_start(out=s_str[:, (o % 2) * 512:(o % 2) * 512 + 512],
                                           in_=wout_d.ap()[:, o * 512:(o + 1) * 512]), "wod", 16)

                @block.tensor
                def _(t, half=half):
                    if half == 0:
                        sy.wait(t, "hnv", HNV)
                    else:
                        sy.wait(t, "ep2", 32)
                    for o in range(32):
                        sy.wait(t, "wod", 16 * (half * 32 + o + 1))
                        if half * 32 + o >= 1:
                            sy.wait(t, "f2e", half * 32 + o)
                        blk = s_str[:, (o % 2) * 512:(o % 2) * 512 + 512]
                        for k in range(4):
                            mm = t.matmul(p_a[:, :], blk[:, k * 128:(k + 1) * 128],
                                          s_hn[:, k * 1024 + half * 512:k * 1024 + (half + 1) * 512],
                                          start=(k == 0), stop=(k == 3))
                        sy.inc(mm, "f2m", 1)
                        sy.wait(t, "f2q", half * 32 + o + 1)
                        t.matmul(p_s[0:1, half * 512:(half + 1) * 512], s_ones1[:, :],
                                 s_big[:, o * 512:(o + 1) * 512], start=(o == 0), stop=(o == 31))
                        sy.inc(t.matmul(p_sq[0:1, half * 512:(half + 1) * 512], s_ones1[:, :],
                                        s_sq[:, 0:512], start=(o == 0), stop=(o == 31)), "f2s", 1)

                @block.vector
                def _(v, half=half):
                    for o in range(32):
                        sy.wait(v, "f2m", half * 32 + o + 1)
                        sy.inc(v.tensor_scalar(s_big[:, o * 512:(o + 1) * 512], p_a[:, :],
                                               s_pc[:, 8 + o:9 + o], None, op0=ALU.add), "f2e", 1)

                @block.scalar
                def _(sc, half=half):
                    for o in range(32):
                        sy.wait(sc, "f2e", half * 32 + o + 1)
                        if half * 32 + o >= 1:
                            sy.wait(sc, "f2s", half * 32 + o)
                        sy.inc(sc.activation(s_sq[:, 0:512], s_big[:, o * 512:(o + 1) * 512], AF.Square), "f2q", 1)

                @block.vector
                def _(v, half=half):
                    sy.wait(v, "f2s", 32 * (half + 1))
                    v.tensor_copy(s_rows[0:1, 0:512], p_s[0:1, half * 512:(half + 1) * 512])
                    v.tensor_copy(s_w[0][0:1, 0:512], p_sq[0:1, half * 512:(half + 1) * 512])
                    v.tensor_scalar(s_rows[0:1, 0:512], s_rows[0:1, 0:512], 1.0 / 4096, None, op0=ALU.mult)
                    v.tensor_scalar(s_w[0][0:1, 0:512], s_w[0][0:1, 0:512], 1.0 / 4096, None, op0=ALU.mult)
                    v.tensor_tensor(s_w[1][0:1, 0:512], s_rows[0:1, 0:512], s_rows[0:1, 0:512], op=ALU.mult)
                    v.tensor_tensor(s_w[0][0:1, 0:512], s_w[0][0:1, 0:512], s_w[1][0:1, 0:512], op=ALU.subtract)
                    sy.inc(v.tensor_scalar(s_w[0][0:1, 0:512], s_w[0][0:1, 0:512], 1e-5, None, op0=ALU.add), "vq2", 1)

                @block.scalar
                def _(sc, half=half):
                    sy.wait(sc, "vq2", half + 1)
                    sy.inc(sc.activation(s_w[0][0:1, 0:512], s_w[0][0:1, 0:512], AF.Sqrt), "sq2", 1)

                @block.vector
                def _(v, half=half):
                    sy.wait(v, "sq2", half + 1)
                    sy.selfedge(v, v.reciprocal(s_w[0][0:1, 0:512], s_w[0][0:1, 0:512]))
                    v.tensor_tensor(s_w[1][0:1, 0:512], s_rows[0:1, 0:512], s_w[0][0:1, 0:512], op=ALU.mult)
                    v.tensor_tensor(s_w[1][0:1, 0:512], s_ms[0:1, half * 512:(half + 1) * 512], s_w[1][0:1, 0:512], op=ALU.subtract)
                    v.tensor_copy(s_rowb[0:1, 0:512], s_w[0][0:1, 0:512])
                    v.tensor_copy(s_rowb[0:1, 512:1024], s_ms[0:1, 1024 + half * 512:1024 + (half + 1) * 512])
                    sy.inc(v.tensor_copy(s_rowb[0:1, 1024:1536], s_w[1][0:1, 0:512]), "rb2", 1)

                @block.tensor
                def _(t, half=half):
                    sy.wait(t, "rb2", half + 1)
                    for r in range(3):
                        if sy.val("bc2") > 0:
                            sy.wait(t, "be2", sy.val("bc2"))
                        sy.inc(t.matmul(p_b[:, :], s_kb[:, :], s_rowb[0:1, r * 512:(r + 1) * 512],
                                        start=True, stop=True), "bc2", 1)

                @block.vector
                def _(v, half=half):
                    for r in range(3):
                        sy.wait(v, "bc2", sy.val("be2") + 1)
                        sy.inc(v.tensor_copy(s_rep[r][:, :], p_b[:, :]), "be2", 1)

                @block.vector
                def _(v, half=half):
                    sy.wait(v, "be2", sy.val("be2"))
                    for o in range(32):
                        dst = s_xnT[:, o * 1024 + half * 512:o * 1024 + (half + 1) * 512]
                        fx = s_big[:, o * 512:(o + 1) * 512]
                        v.tensor_tensor(s_w[0][:, 0:512], fx, s_rep[0][:, :], op=ALU.mult)
                        v.tensor_tensor(s_w[1][:, 0:512], dst, s_rep[1][:, :], op=ALU.mult)
                        v.tensor_tensor(s_w[0][:, 0:512], s_w[0][:, 0:512], s_w[1][:, 0:512], op=ALU.add)
                        sy.inc(v.tensor_tensor(dst, s_w[0][:, 0:512], s_rep[2][:, :], op=ALU.add), "x3v", 1)

                @block.scalar
                def _(sc, half=half):
                    for o in range(32):
                        sy.wait(sc, "x3v", half * 32 + o + 1)
                        if sy.val("x3q") > 0:
                            sy.wait(sc, "x3m", sy.val("x3q"))
                        sy.inc(sc.activation(s_sq[:, 0:512], s_xnT[:, o * 1024 + half * 512:o * 1024 + (half + 1) * 512], AF.Square), "x3q", 1)

                @block.tensor
                def _(t, half=half):
                    for o in range(32):
                        sy.wait(t, "x3q", half * 32 + o + 1)
                        sy.inc(t.matmul(p_sq[0:1, half * 512:(half + 1) * 512], s_ones1[:, :], s_sq[:, 0:512],
                                        start=(o == 0), stop=(o == 31)), "x3m", 1)

                @block.vector
                def _(v, half=half):
                    sy.wait(v, "x3m", 32 * (half + 1))
                    v.tensor_copy(s_w[0][0:1, 0:512], p_sq[0:1, half * 512:(half + 1) * 512])
                    v.tensor_scalar(s_w[0][0:1, 0:512], s_w[0][0:1, 0:512], 1.0 / 4096, None, op0=ALU.mult)
                    v.tensor_tensor(s_w[1][0:1, 0:512], s_ms[0:1, half * 512:(half + 1) * 512],
                                    s_ms[0:1, half * 512:(half + 1) * 512], op=ALU.mult)
                    v.tensor_tensor(s_w[0][0:1, 0:512], s_w[0][0:1, 0:512], s_w[1][0:1, 0:512], op=ALU.subtract)
                    sy.inc(v.tensor_scalar(s_w[0][0:1, 0:512], s_w[0][0:1, 0:512], 1e-5, None, op0=ALU.add), "vv3", 1)

                @block.scalar
                def _(sc, half=half):
                    sy.wait(sc, "vv3", half + 1)
                    sy.inc(sc.activation(s_w[0][0:1, 0:512], s_w[0][0:1, 0:512], AF.Sqrt), "sg3s", 1)

                @block.vector
                def _(v, half=half):
                    sy.wait(v, "sg3s", half + 1)
                    v.tensor_copy(s_sg3[0:1, half * 512:(half + 1) * 512], s_w[0][0:1, 0:512])
                    sy.selfedge(v, v.reciprocal(s_w[0][0:1, 0:512], s_w[0][0:1, 0:512]))
                    v.tensor_copy(s_rowb[0:1, 0:512], s_w[0][0:1, 0:512])
                    v.tensor_copy(s_rowb[0:1, 512:1024], s_ms[0:1, half * 512:(half + 1) * 512])
                    sy.inc(v.tensor_copy(s_w[1][0:1, 0:4], s_rowb[0:1, 0:4]), "sg3v", 1)

                @block.tensor
                def _(t, half=half):
                    sy.wait(t, "sg3v", half + 1)
                    for r in range(2):
                        if sy.val("bc3") > 0:
                            sy.wait(t, "be3", sy.val("bc3"))
                        sy.inc(t.matmul(p_b[:, :], s_kb[:, :], s_rowb[0:1, r * 512:(r + 1) * 512],
                                        start=True, stop=True), "bc3", 1)

                @block.vector
                def _(v, half=half):
                    for r in range(2):
                        sy.wait(v, "bc3", sy.val("be3") + 1)
                        sy.inc(v.tensor_copy(s_rep[r][:, :], p_b[:, :]), "be3", 1)

                @block.vector
                def _(v, half=half):
                    sy.wait(v, "be3", sy.val("be3"))
                    for o in range(32):
                        dst = s_xnT[:, o * 1024 + half * 512:o * 1024 + (half + 1) * 512]
                        v.tensor_tensor(s_w[0][:, 0:512], dst, s_rep[1][:, :], op=ALU.subtract)
                        sy.inc(v.tensor_tensor(dst, s_w[0][:, 0:512], s_rep[0][:, :], op=ALU.mult), "ep2", 1)
            EP2 = 64

            if upto == 0:
                @block.gpsimd
                def _(g):
                    sy.wait(g, "ep2", EP2)
                    sy.inc(g.dma_start(out=out_d.ap(), in_=s_xnT[:, :]), "out", 16)
                    sy.inc(g.dma_start(out=dbg_d.ap()[0:1, 0:1024], in_=s_sg3[0:1, :]), "out", 16)
                    sy.wait(g, "out", 32)
                return nc



        else:
            EP2 = 64
        if upto != 85:
            @block.vector
            def _(v):
                sy.wait(v, "ep2", EP2)
                for fc in range(8):
                    v.tensor_copy(s_xs4[:, fc * 1024:(fc + 1) * 1024], s_xnT[:, fc * 1024:(fc + 1) * 1024])
                    for j in range(1, 4):
                        cc = fc + 8 * j
                        mm = v.tensor_tensor(s_xs4[:, fc * 1024:(fc + 1) * 1024], s_xs4[:, fc * 1024:(fc + 1) * 1024],
                                             s_xnT[:, cc * 1024:(cc + 1) * 1024], op=ALU.add)
                    sy.inc(mm, "xs4", 1)
                if upto == 84:
                    sy.inc(v.tensor_tensor(s_guf[0:128, 20 * 1024:21 * 1024], s_xnT[0:128, 0:1024],
                                           s_xnT[0:128, 0:1024], op=ALU.subtract), "gz", 1)
                else:
                    sy.inc(v.tensor_tensor(s_guf[64:128, 21 * 1024:22 * 1024], s_xnT[64:128, 0:1024],
                                           s_xnT[64:128, 0:1024], op=ALU.subtract), "gz", 1)

            if upto == 4:
                @block.gpsimd
                def _(g):
                    sy.wait(g, "xs4", 8)
                    sy.wait(g, "gz", 1)
                    sy.inc(g.dma_start(out=out_d.ap()[:, 0:8192], in_=s_xs4[:, :]), "out", 16)
                    sy.wait(g, "out", 16)
                return nc

            @block.tensor
            def _(t):
                sy.wait(t, "ep2", EP2)
                for ng in range(2):
                    pp = p_a if ng == 0 else p_b
                    for c in range(32):
                        mm = t.matmul(pp[0:4, :], s_a13[:, c * 4:(c + 1) * 4],
                                      s_xnT[:, c * 1024 + ng * 512:c * 1024 + (ng + 1) * 512],
                                      start=(c == 0), stop=(c == 31))
                    sy.inc(mm, "t13m", 1)

            @block.vector
            def _(v):
                sy.wait(v, "t13m", 2)
                s_t13s = s_gud[0:4, 0:1024]
                sy.selfedge(v, v.tensor_scalar(s_t13s[:, 0:512], p_a[0:4, :], 0.5, None, op0=ALU.mult))
                sy.selfedge(v, v.tensor_scalar(s_t13s[:, 512:1024], p_b[0:4, :], 0.5, None, op0=ALU.mult))
                sy.inc(v.nop(), "t13v", 1)

            @block.gpsimd
            def _(g):
                sy.wait(g, "t13v", 1)
                sy.inc(g.dma_start(out=s_t13[0:2, 0:1024], in_=s_gud[0:2, 0:1024]), "t13d", 16)
                sy.inc(g.dma_start(out=s_t13[0:2, 1024:2048], in_=s_gud[2:4, 0:1024]), "t13d", 16)

            @block.gpsimd
            def _(g):
                sy.wait(g, "t13d", 32)
                sy.inc(g.nop(), "t13e", 1)

            if upto == 5:
                @block.gpsimd
                def _(g):
                    sy.wait(g, "t13e", 1)
                    sy.wait(g, "xs4", 8)
                    sy.wait(g, "gz", 1)
                    sy.inc(g.dma_start(out=dbg_d.ap()[0:2, 0:2048], in_=s_t13[:, :]), "out", 16)
                    sy.wait(g, "out", 16)
                return nc


        else:
            @block.vector
            def _(v):
                sy.wait(v, "dq", LDQ)
                sy.wait(v, "dg", LDG)
                for _ in range(8):
                    sy.inc(v.nop(), "xs4", 1)
                sy.inc(v.nop(), "gz", 1)
                sy.inc(v.nop(), "t13v", 1)

            @block.gpsimd
            def _(g):
                sy.wait(g, "t13v", 1)
                sy.inc(g.nop(), "t13e", 1)
                sy.inc(g.nop(), "t13m", 0)

        W1O, W3O, B1O, B3O, A2O = 0, 1024, 2048, 2176, 2304

        def seg_src(m):
            out = []
            for (sl, dp, ln, fw) in fold_segs(m):
                off = dp % 128
                if sl % 128 == off:
                    out.append(("gud", sl, dp, ln, fw))
                else:
                    assert (sl + 64) % 128 == off
                    out.append(("gsh", off, dp, ln, fw))
            return out

        needs_gsh = [any(s[0] == "gsh" for s in seg_src(m)) for m in range(86)]
        cum_fld = [0]
        for m in range(86):
            cum_fld.append(cum_fld[-1] + len(fold_segs(m)))
        cum_gsh = [0]
        for m in range(86):
            cum_gsh.append(cum_gsh[-1] + (1 if needs_gsh[m] else 0))

        DO_T2 = upto != 6
        DO_FOLD = upto not in (6, 7)
        DO_GSH = upto not in (8, 87, 88)
        if upto in (81, 84, 85, 87):
            pass

        def emit_gsh(g, m):
            sy.wait(g, "guv", 2 * m + 2)
            sy.wait(g, "fld", cum_fld[m])
            sl = (m % 2) * 1024
            sy.inc(g.dma_start(out=s_gsh[64:128, :], in_=s_gud[0:64, sl:sl + 1024]), "gshd", 16)
            sy.inc(g.dma_start(out=s_gsh[0:64, :], in_=s_gud[64:128, sl:sl + 1024]), "gshd", 16)

        @block.gpsimd
        def _(g):
            sy.wait(g, "t13e", 1)
            sy.wait(g, "xs4", 8)
            for m in range(86):
                if m >= 2:
                    if DO_T2:
                        sy.wait(g, "t2m", m - 1)
                    else:
                        sy.wait(g, "gum", 4 * (m - 1))
                sy.inc(g.dma_start(out=s_str[:, (m % 2) * 2432:(m % 2) * 2432 + 2432],
                                   in_=w13_d.ap()[:, m * 2432:(m + 1) * 2432]), "wsd", 16)
                mg = m - 1
                if DO_FOLD and DO_GSH and mg >= 0 and needs_gsh[mg]:
                    emit_gsh(g, mg)
            if DO_FOLD and DO_GSH and needs_gsh[85]:
                emit_gsh(g, 85)

        @block.tensor
        def _(t):
            sy.wait(t, "t13e", 1)
            sy.wait(t, "xs4", 8)
            for m in range(86):
                sy.wait(t, "wsd", 16 * (m + 1))
                blk = s_str[:, (m % 2) * 2432:(m % 2) * 2432 + 2432]
                if m >= 1:
                    sy.wait(t, "gue", 2 * m)
                for ng in range(2):
                    pg = p_a if ng == 0 else p_c
                    pu = p_b if ng == 0 else p_d
                    for k in range(8):
                        t.matmul(pg[:, :], blk[:, W1O + k * 128:W1O + (k + 1) * 128],
                                 s_xs4[:, k * 1024 + ng * 512:k * 1024 + (ng + 1) * 512],
                                 start=(k == 0), stop=False)
                    sy.inc(t.matmul(pg[:, :], blk[0:2, B1O:B1O + 128], s_t13[0:2, ng * 512:(ng + 1) * 512],
                                    start=False, stop=True), "gum", 1)
                    if m >= 1:
                        sy.wait(t, "guv", 2 * m)
                    for k in range(8):
                        t.matmul(pu[:, :], blk[:, W3O + k * 128:W3O + (k + 1) * 128],
                                 s_xs4[:, k * 1024 + ng * 512:k * 1024 + (ng + 1) * 512],
                                 start=(k == 0), stop=False)
                    sy.inc(t.matmul(pu[:, :], blk[0:2, B3O:B3O + 128], s_t13[0:2, 1024 + ng * 512:1024 + (ng + 1) * 512],
                                    start=False, stop=True), "gum", 1)
                if DO_T2:
                    sy.wait(t, "guv", 2 * m + 2)
                    for ng in range(2):
                        mm = t.matmul(p_t2[0:2, ng * 512:(ng + 1) * 512], blk[:, A2O:A2O + 2],
                                      s_gud[:, (m % 2) * 1024 + ng * 512:(m % 2) * 1024 + (ng + 1) * 512],
                                      start=(m == 0), stop=(m == 85))
                    sy.inc(mm, "t2m", 1)

        @block.scalar
        def _(sc):
            for m in range(86):
                for ng in range(2):
                    sy.wait(sc, "gum", 4 * m + 2 * ng + 1)
                    if 2 * m + ng >= 2:
                        sy.wait(sc, "guv", 2 * m + ng - 1)
                    pg = p_a if ng == 0 else p_c
                    sy.inc(sc.activation(s_gate[:, ng * 512:(ng + 1) * 512], pg[:, :], AF.Silu), "gue", 1)

        @block.vector
        def _(v):
            for m in range(86):
                for ng in range(2):
                    sy.wait(v, "gue", 2 * m + ng + 1)
                    sy.wait(v, "gum", 4 * m + 2 * ng + 2)
                    pu = p_b if ng == 0 else p_d
                    sy.inc(v.tensor_tensor(s_gud[:, (m % 2) * 1024 + ng * 512:(m % 2) * 1024 + (ng + 1) * 512],
                                           s_gate[:, ng * 512:(ng + 1) * 512], pu[:, :], op=ALU.mult), "guv", 1)
                if not DO_FOLD:
                    continue
                if m == 0:
                    sy.wait(v, "gz", 1)
                if needs_gsh[m] and DO_GSH:
                    sy.wait(v, "gshd", 32 * cum_gsh[m + 1])
                for (kind, sl, dst_pos, ln, first) in seg_src(m):
                    if upto == 88:
                        continue
                    if upto == 87:
                        sy.inc(v.nop(), "fld", 1)
                        continue
                    if kind == "gsh" and not DO_GSH:
                        sy.inc(v.nop(), "fld", 1)
                        continue
                    dst_chunk, dst_off = divmod(dst_pos, 128)
                    dst = s_guf[dst_off:dst_off + ln, dst_chunk * 1024:(dst_chunk + 1) * 1024]
                    if kind == "gud":
                        srcv = s_gud[sl:sl + ln, (m % 2) * 1024:(m % 2) * 1024 + 1024]
                    else:
                        srcv = s_gsh[sl:sl + ln, :]
                    if first:
                        sy.inc(v.tensor_copy(dst, srcv), "fld", 1)
                    else:
                        sy.inc(v.tensor_tensor(dst, dst, srcv, op=ALU.add), "fld", 1)
        FLD = sy.val("fld")

        if upto in (6, 7):
            @block.gpsimd
            def _(g):
                sy.wait(g, "guv", 172)
                if upto == 7:
                    sy.wait(g, "t2m", 86)
                sy.inc(g.dma_start(out=out_d.ap()[:, 0:2048], in_=s_gud[:, :]), "out", 16)
                sy.wait(g, "out", 16)
            return nc

        @block.vector
        def _(v):
            sy.wait(v, "t2m", 86)
            v.tensor_scalar(s_t2[0:2, 0:512], p_t2[0:2, 0:512], 0.5, None, op0=ALU.mult)
            sy.inc(v.tensor_scalar(s_t2[0:2, 512:1024], p_t2[0:2, 512:1024], 0.5, None, op0=ALU.mult), "t2e", 1)

        if upto in (81, 84, 85, 87, 88):
            @block.gpsimd
            def _(g):
                if upto != 88:
                    sy.wait(g, "fld", FLD)
                sy.wait(g, "guv", 172)
                sy.wait(g, "t2m", 86)
                sy.inc(g.dma_start(out=out_d.ap()[:, 0:2048], in_=s_guf[:, 0:2048]), "out", 16)
                sy.wait(g, "out", 16)
            return nc

        if upto in (1, 8):
            @block.gpsimd
            def _(g):
                sy.wait(g, "fld", FLD)
                sy.wait(g, "t2e", 1)
                sy.inc(g.dma_start(out=out_d.ap()[:, 0:22528], in_=s_guf[:, :]), "out", 16)
                sy.inc(g.dma_start(out=dbg_d.ap()[0:2, 0:1024], in_=s_t2[:, :]), "out", 16)
                sy.wait(g, "out", 32)
            return nc

        for half in range(2):
            @block.gpsimd
            def _(g, half=half):
                if half == 0:
                    sy.wait(g, "fld", FLD)
                    sy.wait(g, "t2e", 1)
                else:
                    sy.wait(g, "x4d", 16 * 32)
                for o in range(32):
                    if half * 32 + o >= 2:
                        sy.wait(g, "f3q", half * 32 + o - 1)
                    sy.inc(g.dma_start(out=s_str[:, (o % 2) * 2944:(o % 2) * 2944 + 2944],
                                       in_=w2s_d.ap()[:, o * 2944:(o + 1) * 2944]), "w2d", 16)

            @block.tensor
            def _(t, half=half):
                if half == 0:
                    sy.wait(t, "fld", FLD)
                    sy.wait(t, "t2e", 1)
                else:
                    sy.wait(t, "ep3", 32)
                for o in range(32):
                    sy.wait(t, "w2d", 16 * (half * 32 + o + 1))
                    if half * 32 + o >= 1:
                        sy.wait(t, "f3q", half * 32 + o)
                    blk = s_str[:, (o % 2) * 2944:(o % 2) * 2944 + 2944]
                    for k in range(22):
                        t.matmul(p_a[:, :], blk[:, k * 128:(k + 1) * 128],
                                 s_guf[:, k * 1024 + half * 512:k * 1024 + (half + 1) * 512],
                                 start=(k == 0), stop=False)
                    sy.inc(t.matmul(p_a[:, :], blk[0:2, 2816:2944], s_t2[0:2, half * 512:(half + 1) * 512],
                                    start=False, stop=True), "f3m", 1)
                    sy.wait(t, "f3q", half * 32 + o + 1)
                    t.matmul(p_s[0:1, half * 512:(half + 1) * 512], s_ones1[:, :],
                             s_big[:, o * 512:(o + 1) * 512], start=(o == 0), stop=(o == 31))
                    sy.inc(t.matmul(p_sq[0:1, half * 512:(half + 1) * 512], s_ones1[:, :],
                                    s_sq[:, 0:512], start=(o == 0), stop=(o == 31)), "f3s", 1)

            @block.scalar
            def _(sc, half=half):
                for o in range(32):
                    sy.wait(sc, "f3m", half * 32 + o + 1)
                    if half * 32 + o >= 1:
                        sy.wait(sc, "f3s", half * 32 + o)
                    sc.activation(s_big[:, o * 512:(o + 1) * 512], p_a[:, :], AF.Copy)
                    sy.inc(sc.activation(s_sq[:, 0:512], p_a[:, :], AF.Square), "f3q", 1)

            @block.vector
            def _(v, half=half):
                sy.wait(v, "f3s", 32 * (half + 1))
                v.tensor_copy(s_rows[0:1, 0:512], p_s[0:1, half * 512:(half + 1) * 512])
                v.tensor_copy(s_w[0][0:1, 0:512], p_sq[0:1, half * 512:(half + 1) * 512])
                v.tensor_scalar(s_rows[0:1, 0:512], s_rows[0:1, 0:512], 1.0 / 4096, None, op0=ALU.mult)
                v.tensor_scalar(s_w[0][0:1, 0:512], s_w[0][0:1, 0:512], 1.0 / 4096, None, op0=ALU.mult)
                v.tensor_tensor(s_w[1][0:1, 0:512], s_rows[0:1, 0:512], s_rows[0:1, 0:512], op=ALU.mult)
                v.tensor_tensor(s_w[0][0:1, 0:512], s_w[0][0:1, 0:512], s_w[1][0:1, 0:512], op=ALU.subtract)
                sy.inc(v.tensor_scalar(s_w[0][0:1, 0:512], s_w[0][0:1, 0:512], 1e-5, None, op0=ALU.add), "vq4", 1)

            @block.scalar
            def _(sc, half=half):
                sy.wait(sc, "vq4", half + 1)
                sy.inc(sc.activation(s_w[0][0:1, 0:512], s_w[0][0:1, 0:512], AF.Sqrt), "sq4", 1)

            @block.vector
            def _(v, half=half):
                sy.wait(v, "sq4", half + 1)
                sy.selfedge(v, v.reciprocal(s_w[0][0:1, 0:512], s_w[0][0:1, 0:512]))
                v.tensor_tensor(s_w[1][0:1, 0:512], s_rows[0:1, 0:512], s_w[0][0:1, 0:512], op=ALU.mult)
                v.tensor_tensor(s_w[1][0:1, 0:512], s_ms[0:1, half * 512:(half + 1) * 512], s_w[1][0:1, 0:512], op=ALU.subtract)
                v.tensor_copy(s_rowb[0:1, 0:512], s_w[0][0:1, 0:512])
                v.tensor_copy(s_rowb[0:1, 512:1024], s_sg3[0:1, half * 512:(half + 1) * 512])
                sy.inc(v.tensor_copy(s_rowb[0:1, 1024:1536], s_w[1][0:1, 0:512]), "rb4", 1)

            @block.tensor
            def _(t, half=half):
                sy.wait(t, "rb4", half + 1)
                for r in range(3):
                    if sy.val("bc4") > 0:
                        sy.wait(t, "be4", sy.val("bc4"))
                    sy.inc(t.matmul(p_b[:, :], s_kb[:, :], s_rowb[0:1, r * 512:(r + 1) * 512],
                                    start=True, stop=True), "bc4", 1)

            @block.vector
            def _(v, half=half):
                for r in range(3):
                    sy.wait(v, "bc4", sy.val("be4") + 1)
                    sy.inc(v.tensor_copy(s_rep[r][:, :], p_b[:, :]), "be4", 1)

            @block.vector
            def _(v, half=half):
                sy.wait(v, "be4", sy.val("be4"))
                for o in range(32):
                    idx = half * 32 + o
                    if idx >= 2:
                        sy.wait(v, "x4d", 16 * (idx - 1))
                    ob = s_gf[:, (o % 2) * 512:(o % 2) * 512 + 512]
                    v.tensor_tensor(ob, s_big[:, o * 512:(o + 1) * 512], s_rep[0][:, :], op=ALU.mult)
                    v.tensor_tensor(s_w[0][:, 0:512], s_xnT[:, o * 1024 + half * 512:o * 1024 + (half + 1) * 512],
                                    s_rep[1][:, :], op=ALU.mult)
                    v.tensor_tensor(ob, ob, s_w[0][:, 0:512], op=ALU.add)
                    sy.inc(v.tensor_tensor(ob, ob, s_rep[2][:, :], op=ALU.add), "ep3", 1)

            @block.sync
            def _(q, half=half):
                for o in range(32):
                    sy.wait(q, "ep3", half * 32 + o + 1)
                    sy.inc(q.dma_start(out=out_d.ap()[:, o * 1024 + half * 512:o * 1024 + (half + 1) * 512],
                                       in_=s_gf[:, (o % 2) * 512:(o % 2) * 512 + 512]), "x4d", 16)

        @block.sync
        def _(q):
            sy.wait(q, "x4d", 16 * 64)


    return nc


def bigb_shared(Wout, bout, lnlw, lnlb, w1q, a1, b1, w3q, a3, b3, w2q, a2, b2):
    import ml_dtypes
    bf = ml_dtypes.bfloat16
    f32 = np.float32
    d = {}
    WoT = np.ascontiguousarray(Wout.T, dtype=f32)
    d["woutT"] = np.concatenate([kstack(np.ascontiguousarray(WoT[:, 128 * o:128 * (o + 1)]))
                                 for o in range(32)], axis=1).astype(bf)
    a13 = np.concatenate([np.asarray(a1, f32).T, np.asarray(a3, f32).T], axis=1)
    d["a13T"] = kstack(np.ascontiguousarray(a13)).astype(bf)
    blocks = []
    for m in range(86):
        cols = np.zeros((128, 2432), f32)
        for k in range(8):
            cols[:, k * 128:(k + 1) * 128] = w1q[128 * m:128 * (m + 1), 128 * k:128 * (k + 1)].T
            cols[:, 1024 + k * 128:1024 + (k + 1) * 128] = w3q[128 * m:128 * (m + 1), 128 * k:128 * (k + 1)].T
        cols[0:2, 2048:2176] = b1[128 * m:128 * (m + 1), :].T
        cols[0:2, 2176:2304] = b3[128 * m:128 * (m + 1), :].T
        cols[:, 2304:2306] = a2.T[128 * m:128 * (m + 1), :]
        blocks.append(cols)
    d["w13"] = np.concatenate(blocks, axis=1).astype(bf)
    w2p = np.zeros((2816, 4096), f32)
    w2p[0:2752] = np.asarray(w2q, f32).T
    blocks = []
    for o in range(32):
        cols = np.zeros((128, 2944), f32)
        for k in range(22):
            cols[:, k * 128:(k + 1) * 128] = w2p[128 * k:128 * (k + 1), 128 * o:128 * (o + 1)]
        cols[0:2, 2816:2944] = b2[128 * o:128 * (o + 1), :].T
        blocks.append(cols)
    d["w2s"] = np.concatenate(blocks, axis=1).astype(bf)
    pc = np.zeros((128, 40), f32)
    for j in range(4):
        pc[:, j] = lnlw[128 * j:128 * (j + 1)]
        pc[:, 4 + j] = lnlb[128 * j:128 * (j + 1)]
    for o in range(32):
        pc[:, 8 + o] = bout[128 * o:128 * (o + 1)]
    d["pc"] = pc
    d["onesk"] = np.ones((1, 128), f32)
    d["cstE"] = np.full((128, 1), 1e-5, f32)
    d["cstO"] = np.ones((128, 1), f32)
    return d


def _assemble_seq_512(res, key):
    cols = []
    for c in range(NCORE):
        a = np.asarray(res.results[c][key], np.float32)
        cols.append(a.reshape(128, 4, 1024).transpose(1, 0, 2).reshape(512, 1024).T)
    return np.ascontiguousarray(np.concatenate(cols, axis=0))


def _run_conv(ncC, consts, yfull, pre, pim, lcre, lcim):
    in_maps = []
    for c in range(NCORE):
        sl = slice(64 * c, 64 * c + 64)
        im = dict(consts)
        im["y"] = np.ascontiguousarray(
            yfull[:, sl].reshape(64, 128, 64).transpose(1, 0, 2).reshape(128, 4096))
        im["phre"] = np.broadcast_to(pre[sl], (128, 64)).copy()
        im["phim"] = np.broadcast_to(pim[sl], (128, 64)).copy()
        im["lc"] = np.concatenate([lcre[sl], lcim[sl]])[None, :].astype(np.float32)
        in_maps.append(im)
    res = run_bass_kernel_spmd(ncC, in_maps, core_ids=list(range(NCORE)))
    out = np.zeros((8192, 512), np.float32)
    for c in range(NCORE):
        o = np.asarray(res.results[c]["out"], np.float32)
        out[:, 64 * c:64 * c + 64] = o.reshape(128, 64, 64).transpose(1, 0, 2).reshape(8192, 64)
    return out


def _seq_to_kstack(full):
    out = []
    for c in range(NCORE):
        t = np.ascontiguousarray(full[1024 * c:1024 * (c + 1)].T)
        out.append(np.ascontiguousarray(t.reshape(4, 128, 1024).transpose(1, 0, 2).reshape(128, 4096)))
    return out


def kernel(**inputs):
    f32 = np.float32
    g = lambda k: np.asarray(inputs[k], f32)
    x0 = g("L0_attn_x").reshape(8192, 4096)
    Wq, Wk, Wv, Wo = g("L0_attn_Wq"), g("L0_attn_Wk"), g("L0_attn_Wv"), g("L0_attn_Wo")
    base_W, gate_W = g("L0_ffn_base_W"), g("L0_ffn_gate_W")
    lora_A, lora_B = g("L0_ffn_lora_A"), g("L0_ffn_lora_B")
    Win, bin_ = g("L1_attn_Win"), g("L1_attn_bin")
    ln1w, ln1b = g("L1_attn_ln1_w"), g("L1_attn_ln1_b")
    phz_re, phz_im = g("L1_attn_phz_re"), g("L1_attn_phz_im")
    lc_re, lc_im = g("L1_attn_lc_re"), g("L1_attn_lc_im")
    sc_lin, ln2w, ln2b = g("L1_attn_sc_lin"), g("L1_attn_ln2_w"), g("L1_attn_ln2_b")
    W1, b1, W2, b2 = g("L1_attn_W1"), g("L1_attn_b1"), g("L1_attn_W2"), g("L1_attn_b2")
    lnlw, lnlb = g("L1_attn_lnl_w"), g("L1_attn_lnl_b")
    Wout, bout = g("L1_attn_Wout"), g("L1_attn_bout")
    w1q, a1, b1q = g("L1_ffn_w1q"), g("L1_ffn_a1"), g("L1_ffn_b1")
    w3q, a3, b3q = g("L1_ffn_w3q"), g("L1_ffn_a3"), g("L1_ffn_b3")
    w2q, a2, b2q = g("L1_ffn_w2q"), g("L1_ffn_a2"), g("L1_ffn_b2")

    shA = biga_shared(Wq, Wk, Wv, Wo, base_W, gate_W, lora_A, lora_B, Win, bin_, ln1w[0], ln1b[0])
    in_maps = [biga_core_inputs(c, x0, shA) for c in range(NCORE)]
    ncA = build_biga()
    resA = run_bass_kernel_spmd(ncA, in_maps, core_ids=list(range(NCORE)))
    xn2 = [np.asarray(resA.results[c]["xn2"]) for c in range(NCORE)]
    musig = [np.asarray(resA.results[c]["musig"], f32) for c in range(NCORE)]
    hcur = [np.asarray(resA.results[c]["h"], f32) for c in range(NCORE)]
    ycur = _assemble_seq_512(resA, "y0")

    ncC = build_conv()
    ncM = build_mid()
    consts = conv_consts()
    mid_shared = dict(onesk=np.ones((1, 128), f32))
    for l in range(4):
        cv = _run_conv(ncC, consts, ycur, phz_re[l], phz_im[l], lc_re[l], lc_im[l])
        cvk = _seq_to_kstack(cv)
        nl = l + 1 if l < 3 else 0
        pc = np.zeros((128, 24), f32)
        for bidx, arr in enumerate([sc_lin[l], ln2w[l], ln2b[l], ln1w[nl], ln1b[nl], b2[l]]):
            for j in range(4):
                pc[:, bidx * 4 + j] = arr[128 * j:128 * (j + 1)]
        b1c = np.zeros((128, 16), f32)
        for m in range(16):
            b1c[:, m] = b1[l][128 * m:128 * (m + 1)]
        im0 = dict(mid_shared, pc=pc, b1c=b1c,
                   w1t=mstack(np.ascontiguousarray(W1[l].T)),
                   w2t=mstack(np.ascontiguousarray(W2[l].T)))
        mid_maps = []
        for c in range(NCORE):
            im = dict(im0)
            im["h"] = hcur[c]
            im["cv"] = cvk[c]
            mid_maps.append(im)
        resM = run_bass_kernel_spmd(ncM, mid_maps, core_ids=list(range(NCORE)))
        hcur = [np.asarray(resM.results[c]["hout"], f32) for c in range(NCORE)]
        if l < 3:
            ycur = _assemble_seq_512(resM, "y")

    shB = bigb_shared(Wout, bout, lnlw, lnlb, w1q, a1, b1q, w3q, a3, b3q, w2q, a2, b2q)
    ncB = build_bigb()
    bigb_maps = []
    for c in range(NCORE):
        im = dict(shB)
        im["xn2"] = xn2[c]
        im["musig"] = musig[c]
        im["h"] = hcur[c]
        bigb_maps.append(im)
    resB = run_bass_kernel_spmd(ncB, bigb_maps, core_ids=list(range(NCORE)))
    out = np.zeros((8192, 4096), np.float32)
    for c in range(NCORE):
        a = np.asarray(resB.results[c]["out"], np.float32)
        out[1024 * c:1024 * (c + 1)] = a.reshape(128, 32, 1024).transpose(1, 0, 2).reshape(4096, 1024).T
    return out.reshape(1, 8192, 4096)


# version 03
# speedup vs baseline: 1.0105x; 1.0105x over previous
import sys
sys.path.insert(0, "/opt/trn_rl_repo")
import contextlib
import numpy as np
import ml_dtypes
import concourse.bass as bass
import concourse.mybir as mybir
from concourse.bass_utils import run_bass_kernel_spmd

F32 = mybir.dt.float32
BF16 = mybir.dt.bfloat16
AF = mybir.ActivationFunctionType
ALU = mybir.AluOpType
AX = mybir.AxisListType
BF = ml_dtypes.bfloat16

NCORE = 8
S, D, P = 8192, 4096, 1024
DIM, DFF = 512, 2048
CCH, CNC, CT, CF = 64, 64, 128, 256


def kstack(a):
    K, M = a.shape
    nk = K // 128
    return np.ascontiguousarray(a.reshape(nk, 128, M).transpose(1, 0, 2).reshape(128, nk * M))


class Sy:

    def __init__(self, nc, st):
        self.nc, self.st, self.sems, self.cnt, self.base = nc, st, {}, {}, {}

    def sem(self, name):
        if name not in self.sems:
            self.sems[name] = self.st.enter_context(self.nc.semaphore(name))
            self.cnt[name] = 0
            self.base[name] = 0
        return self.sems[name]

    def alias(self, name, old):
        assert name not in self.sems and old in self.sems
        self.sems[name] = self.sems[old]
        self.base[name] = self.base[old] + self.cnt[old]
        self.cnt[name] = 0

    def inc(self, instr, name, n=1):
        instr.then_inc(self.sem(name), n)
        self.cnt[name] += n
        return self.cnt[name]

    def wait(self, eng, name, val=None):
        v = self.cnt.get(name, 0) if val is None else val
        eng.wait_ge(self.sem(name), self.base.get(name, 0) + v)

    def val(self, name):
        return self.cnt.get(name, 0)

    def selfedge(self, eng, instr):
        nm = "selfb_" + type(eng).__name__
        n = self.inc(instr, nm, 1)
        eng.wait_ge(self.sem(nm), n)


def cmul(v, outre, outim, are, aim, bre, bim, w0, w1):
    v.tensor_tensor(w0, are, bre, op=ALU.mult)
    v.tensor_tensor(w1, aim, bim, op=ALU.mult)
    v.tensor_tensor(w0, w0, w1, op=ALU.subtract)
    v.tensor_tensor(w1, are, bim, op=ALU.mult)
    v.tensor_tensor(outim, aim, bre, op=ALU.mult)
    v.tensor_tensor(outim, outim, w1, op=ALU.add)
    v.tensor_copy(outre, w0)


def build_conv(debug=False):
    nc = bass.Bass("TRN2", target_bir_lowering=False)
    dI = lambda n, s: nc.dram_tensor(n, s, F32, kind="ExternalInput")
    y_d = dI("y", [128, 4096])
    phre_d, phim_d = dI("phre", [128, CCH]), dI("phim", [128, CCH])
    lc_d = dI("lc", [1, 2 * CCH])
    dftre_d, dftim_d = dI("dftre", [128, CF]), dI("dftim", [128, CF])
    ic_d = dI("ic", [128, 512])
    tl_d = dI("tl", [128, 8])
    shc_d = dI("shc", [64, 64 * 7])
    shp_d = dI("shp", [128, 128 * 7])
    om_d = dI("om", [128, 6])
    onesk_d = dI("onesk", [1, 128])
    out_d = nc.dram_tensor("out", [128, 4096], F32, kind="ExternalOutput")
    dbt = nc.dram_tensor("dbt", [2, 4096], F32)
    dci = nc.dram_tensor("dci", [2, 4096], F32)
    if debug:
        dbg_T = nc.dram_tensor("dbg_T", [64, 128], F32, kind="ExternalOutput")
        dbg_ci = nc.dram_tensor("dbg_ci", [64, 128], F32, kind="ExternalOutput")
        dbg_tr = nc.dram_tensor("dbg_tr", [2, 4096], F32, kind="ExternalOutput")
        dbg_pg = nc.dram_tensor("dbg_pg", [128, 128], F32, kind="ExternalOutput")
        dbg_crr = nc.dram_tensor("dbg_crr", [128, 2 * 4096], F32, kind="ExternalOutput")
        dbg_pgr = nc.dram_tensor("dbg_pgr", [128, 2 * 512], F32, kind="ExternalOutput")
        dbg_yh = nc.dram_tensor("dbg_yh", [128, 2 * 512], F32, kind="ExternalOutput")

    with contextlib.ExitStack() as st:
        T = lambda n, s, d=F32: st.enter_context(nc.sbuf_tensor(n, s, d))
        PS = lambda n, s: st.enter_context(nc.psum_tensor(n, s, F32))
        sy = Sy(nc, st)
        block = st.enter_context(nc.Block())

        s_y = T("s_y", [128, 4096])
        s_dre = T("s_dre", [128, CF])
        s_dim = T("s_dim", [128, CF])
        s_ic = T("s_ic", [128, 512])
        s_tl = T("s_tl", [128, 8])
        s_shc = T("s_shc", [64, 448])
        s_shp = T("s_shp", [128, 896], BF16)
        s_ph = T("s_ph", [128, 2 * CCH])
        s_lc = T("s_lc", [1, 2 * CCH])
        s_om = T("s_om", [128, 6])
        s_onesk = T("s_onesk", [1, 128])
        s_p = T("s_p", [128, 2 * CCH])
        s_pk = T("s_pk", [128, 2 * CCH])
        s_pow = T("s_pow", [128, 14 * CCH])
        s_qpow = T("s_qpow", [128, 12 * CCH])
        s_w = [T(f"s_w{i}", [128, CCH]) for i in range(6)]
        s_fh = T("s_fh", [128, 4 * CCH])
        s_fr = [T(f"s_fr{i}", [128, 512]) for i in range(4)]
        s_yhr = T("s_yhr", [128, 2 * 4096])
        s_yhi = T("s_yhi", [128, 2 * 4096])
        s_xr = [T(f"s_xr{i}", [128, 512]) for i in range(2)]
        s_xi = [T(f"s_xi{i}", [128, 512]) for i in range(2)]
        s_t1 = T("s_t1", [128, 512])
        s_t2 = T("s_t2", [128, 512])
        s_g1 = T("s_g1", [128, 512])
        s_pg = T("s_pg", [128, 2 * CCH])
        s_pgb = T("s_pgb", [128, 2 * CCH], BF16)
        s_pgp = T("s_pgp", [128, 2 * CCH])
        s_pgr = T("s_pgr", [128, 512])
        s_pgi = T("s_pgi", [128, 512])
        s_tr = T("s_tr", [2, 4096])
        s_T = T("s_T", [64, 128])
        s_shv = T("s_shv", [64, 128])
        s_ci = T("s_ci", [64, 128])
        s_cw = [T(f"s_cw{i}", [64, 64]) for i in range(2)]
        s_crr = T("s_crr", [128, 4096])
        s_cri = T("s_cri", [128, 4096])
        s_os = [T(f"s_os{i}", [128, 512]) for i in range(2)]

        p_a = PS("p_a", [128, 512])
        p_b = PS("p_b", [128, 512])
        p_c2 = PS("p_c2", [128, 512])
        p_d = PS("p_d", [128, 512])
        p_t = PS("p_t", [2, 512])
        p_g = PS("p_g", [128, 512])
        p_ch = PS("p_ch", [64, 512])
        p_f = PS("p_f", [1, 16])

        def mmfence(t, name, n=1):
            sy.inc(t.matmul(p_f[0:1, 0:1], s_tl[:, 0:1], s_tl[:, 0:1], start=True, stop=True), name, n)

        @block.sync
        def _(q):
            for dst, src in [(s_shc, shc_d), (s_om, om_d), (s_lc, lc_d),
                             (s_dre, dftre_d), (s_dim, dftim_d), (s_ic, ic_d), (s_tl, tl_d)]:
                sy.inc(q.dma_start(out=dst[:, :], in_=src.ap()), "dq", 16)
            sy.inc(q.dma_start(out=s_ph[:, 0:CCH], in_=phre_d.ap()), "dq", 16)
            sy.inc(q.dma_start(out=s_ph[:, CCH:], in_=phim_d.ap()), "dq", 16)
            sy.inc(q.dma_start(out=s_onesk[:, :], in_=onesk_d.ap()), "dq", 16)

        @block.gpsimd
        def _(g):
            for dst, src in [(s_y, y_d), (s_shp, shp_d)]:
                sy.inc(g.dma_start(out=dst[:, :], in_=src.ap()), "dg", 16)

        LDQ, LDG = sy.val("dq"), sy.val("dg")

        @block.vector
        def _(v):
            sy.wait(v, "dq", LDQ)
            a, b = s_ph[:, 0:CCH], s_ph[:, CCH:]
            v.tensor_tensor(s_w[0][:, :], a, a, op=ALU.mult)
            v.tensor_tensor(s_w[1][:, :], b, b, op=ALU.mult)
            sy.inc(v.tensor_tensor(s_w[0][:, :], s_w[0][:, :], s_w[1][:, :], op=ALU.add), "v", 1)

        @block.scalar
        def _(sc):
            sy.wait(sc, "v", 1)
            sc.activation(s_w[1][:, :], s_w[0][:, :], AF.Sqrt)
            sy.inc(sc.activation(s_w[2][:, :], s_w[1][:, :], AF.Tanh), "s", 1)

        def csq(v, cur):
            a, b = cur[:, 0:CCH], cur[:, CCH:2 * CCH]
            v.tensor_tensor(s_w[0][:, :], a, a, op=ALU.mult)
            v.tensor_tensor(s_w[1][:, :], b, b, op=ALU.mult)
            v.tensor_tensor(s_w[2][:, :], a, b, op=ALU.mult)
            v.tensor_tensor(a, s_w[0][:, :], s_w[1][:, :], op=ALU.subtract)
            v.tensor_scalar(b, s_w[2][:, :], 2.0, None, op0=ALU.mult)

        @block.vector
        def _(v):
            sy.wait(v, "s", 1)
            sy.selfedge(v, v.reciprocal(s_w[3][:, :], s_w[1][:, :]))
            v.tensor_tensor(s_w[3][:, :], s_w[3][:, :], s_w[2][:, :], op=ALU.mult)
            v.tensor_tensor(s_p[:, 0:CCH], s_w[3][:, :], s_ph[:, 0:CCH], op=ALU.mult)
            v.tensor_tensor(s_p[:, CCH:], s_w[3][:, :], s_ph[:, CCH:], op=ALU.mult)
            v.tensor_copy(s_pk[:, :], s_p[:, :])
            for r in range(7):
                v.tensor_copy(s_pow[:, 2 * r * CCH:(2 * r + 2) * CCH], s_pk[:, :])
                csq(v, s_pk)
            v.tensor_copy(s_qpow[:, 0:2 * CCH], s_pk[:, :])
            for r in range(1, 6):
                v.tensor_copy(s_qpow[:, 2 * r * CCH:(2 * r + 2) * CCH],
                              s_qpow[:, 2 * (r - 1) * CCH:2 * r * CCH])
                csq(v, s_qpow[:, 2 * r * CCH:(2 * r + 2) * CCH])
            pre, pim = s_p[:, 0:CCH], s_p[:, CCH:]
            qre, qim = s_pk[:, 0:CCH], s_pk[:, CCH:]
            for ft in range(2):
                omr, omi, sg = s_om[:, ft:ft + 1], s_om[:, 2 + ft:3 + ft], s_om[:, 4 + ft:5 + ft]
                w0, w1, w2, w3, w4, w5 = [w[:, :] for w in s_w]
                fre = s_fh[:, ft * CCH:(ft + 1) * CCH]
                fim = s_fh[:, (2 + ft) * CCH:(3 + ft) * CCH]
                v.tensor_scalar(w0, pre, omr, None, op0=ALU.mult)
                v.tensor_scalar(w1, pim, omi, None, op0=ALU.mult)
                v.tensor_tensor(w0, w0, w1, op=ALU.subtract)
                v.tensor_scalar(w0, w0, -1.0, 1.0, op0=ALU.mult, op1=ALU.add)
                v.tensor_scalar(w1, pim, omr, None, op0=ALU.mult)
                v.tensor_scalar(w2, pre, omi, None, op0=ALU.mult)
                v.tensor_tensor(w1, w1, w2, op=ALU.add)
                v.tensor_scalar(w1, w1, -1.0, None, op0=ALU.mult)
                v.tensor_tensor(w2, w0, w0, op=ALU.mult)
                v.tensor_tensor(w3, w1, w1, op=ALU.mult)
                v.tensor_tensor(w2, w2, w3, op=ALU.add)
                sy.selfedge(v, v.reciprocal(w2, w2))
                v.tensor_scalar(w3, qre, sg, None, op0=ALU.mult)
                v.tensor_scalar(w3, w3, -1.0, 1.0, op0=ALU.mult, op1=ALU.add)
                v.tensor_scalar(w4, qim, sg, None, op0=ALU.mult)
                v.tensor_scalar(w4, w4, -1.0, None, op0=ALU.mult)
                v.tensor_tensor(w5, w3, w0, op=ALU.mult)
                v.tensor_tensor(fre, w4, w1, op=ALU.mult)
                v.tensor_tensor(fre, fre, w5, op=ALU.add)
                v.tensor_tensor(fre, fre, w2, op=ALU.mult)
                v.tensor_tensor(w5, w4, w0, op=ALU.mult)
                v.tensor_tensor(w0, w3, w1, op=ALU.mult)
                v.tensor_tensor(fim, w5, w0, op=ALU.subtract)
                v.tensor_tensor(fim, fim, w2, op=ALU.mult)
            sy.inc(v.tensor_tensor(s_w[0][:, 0:4], s_fh[:, 0:4], s_fh[:, 252:256], op=ALU.bypass), "v", 1)
        VF = sy.val("v")

        @block.sync
        def _(q):
            sy.wait(q, "v", VF)
            for i in range(4):
                sy.inc(q.dma_start(out=s_fr[i][:, 0:CCH], in_=s_fh[:, i * CCH:(i + 1) * CCH]), "dq", 16)
            sy.wait(q, "dq")
            for i in range(4):
                w = CCH
                while w < 512:
                    sy.inc(q.dma_start(out=s_fr[i][:, w:2 * w], in_=s_fr[i][:, 0:w]), "dq", 16)
                    sy.wait(q, "dq")
                    w *= 2
        FREP = sy.val("dq")

        @block.tensor
        def _(t):
            sy.wait(t, "dg", LDG)
            i = 0
            for ft in range(2):
                dre = s_dre[:, ft * 128:(ft + 1) * 128]
                dim = s_dim[:, ft * 128:(ft + 1) * 128]
                for n in range(8):
                    if i >= 2:
                        sy.wait(t, "evx", 2 * (i - 1))
                    pa, pb = (p_a, p_b) if i % 2 == 0 else (p_c2, p_d)
                    rs = s_y[:, n * 512:(n + 1) * 512]
                    t.matmul(pa[:, :], dre, rs, start=True, stop=True)
                    t.matmul(pb[:, :], dim, rs, start=True, stop=True)
                    mmfence(t, "mmx", 1)
                    i += 1

        @block.scalar
        def _(sc):
            i = 0
            for ft in range(2):
                for n in range(8):
                    sy.wait(sc, "mmx", i + 1)
                    pa, pb = (p_a, p_b) if i % 2 == 0 else (p_c2, p_d)
                    j = i % 2
                    if i >= 2:
                        sy.wait(sc, "fm", 2 * (i - 1))
                        sy.wait(sc, "g", i - 1)
                    sc.activation(s_xr[j][:, :], pa[:, :], AF.Copy)
                    sy.inc(sc.activation(s_xi[j][:, :], pb[:, :], AF.Copy), "evx", 2)
                    i += 1

        @block.vector
        def _(v):
            sy.wait(v, "dq", FREP)
            for i in range(16):
                ft, n = divmod(i, 8)
                j = i % 2
                sl = slice(ft * 4096 + n * 512, ft * 4096 + (n + 1) * 512)
                sy.wait(v, "evx", 2 * (i + 1))
                if i > 0:
                    sy.wait(v, "g", i)
                fr, fi = s_fr[ft][:, :], s_fr[2 + ft][:, :]
                v.tensor_tensor(s_t1[:, :], s_xr[j][:, :], fr, op=ALU.mult)
                v.tensor_tensor(s_t2[:, :], s_xi[j][:, :], fi, op=ALU.mult)
                v.tensor_tensor(s_yhr[:, sl], s_t1[:, :], s_t2[:, :], op=ALU.subtract)
                sy.inc(v.tensor_tensor(s_t1[:, :], s_xr[j][:, :], fi, op=ALU.mult), "fm", 2)

        @block.gpsimd
        def _(g):
            for i in range(16):
                ft, n = divmod(i, 8)
                j = i % 2
                sl = slice(ft * 4096 + n * 512, ft * 4096 + (n + 1) * 512)
                sy.wait(g, "fm", 2 * (i + 1))
                fr = s_fr[ft][:, :]
                g.tensor_tensor(s_g1[:, :], s_xi[j][:, :], fr, op=ALU.mult)
                sy.inc(g.tensor_tensor(s_yhi[:, sl], s_t1[:, :], s_g1[:, :], op=ALU.add), "g", 1)
        YH_G = sy.val("g")

        @block.vector
        def _(v):
            sy.selfedge(v, v.memset(s_pg[:, :], 0.0))
            sy.selfedge(v, v.memset(s_pg[0:1, 0:CCH], 1.0))
            sy.inc(v.tensor_copy(s_pgb[:, :], s_pg[:, :]), "v", 1)
        PG0 = sy.val("v")

        @block.tensor
        def _(t):
            for r in range(7):
                sy.wait(t, "v", PG0 + r)
                shm = s_shp[:, r * 128:(r + 1) * 128]
                t.matmul(p_g[:, 0:CCH], shm, s_pgb[:, 0:CCH], start=True, stop=True)
                sy.inc(t.matmul(p_g[:, CCH:2 * CCH], shm, s_pgb[:, CCH:], start=True, stop=True), "mmp", 1)

        @block.vector
        def _(v):
            for r in range(7):
                sy.wait(v, "mmp", r + 1)
                pw = s_pow[:, 2 * r * CCH:(2 * r + 2) * CCH]
                v.tensor_tensor(s_w[0][:, :], p_g[:, 0:CCH], pw[:, 0:CCH], op=ALU.mult)
                v.tensor_tensor(s_w[1][:, :], p_g[:, CCH:2 * CCH], pw[:, CCH:], op=ALU.mult)
                v.tensor_tensor(s_w[0][:, :], s_w[0][:, :], s_w[1][:, :], op=ALU.subtract)
                v.tensor_tensor(s_w[2][:, :], p_g[:, 0:CCH], pw[:, CCH:], op=ALU.mult)
                v.tensor_tensor(s_w[3][:, :], p_g[:, CCH:2 * CCH], pw[:, 0:CCH], op=ALU.mult)
                v.tensor_tensor(s_w[2][:, :], s_w[2][:, :], s_w[3][:, :], op=ALU.add)
                v.tensor_tensor(s_pg[:, 0:CCH], s_pg[:, 0:CCH], s_w[0][:, :], op=ALU.add)
                v.tensor_tensor(s_pg[:, CCH:], s_pg[:, CCH:], s_w[2][:, :], op=ALU.add)
                sy.inc(v.tensor_copy(s_pgb[:, :], s_pg[:, :]), "v", 1)
            cmul(v, s_pgp[:, 0:CCH], s_pgp[:, CCH:], s_pg[:, 0:CCH], s_pg[:, CCH:],
                 s_p[:, 0:CCH], s_p[:, CCH:], s_w[0][:, :], s_w[1][:, :])
            sy.inc(v.tensor_tensor(s_w[0][:, 0:4], s_pgp[:, 0:4], s_pgp[:, 124:128], op=ALU.bypass), "v", 1)
        PGB = sy.val("v")

        @block.sync
        def _(q):
            sy.wait(q, "v", PGB)
            sy.inc(q.dma_start(out=s_pgr[:, 0:CCH], in_=s_pgp[:, 0:CCH]), "dq", 16)
            sy.inc(q.dma_start(out=s_pgi[:, 0:CCH], in_=s_pgp[:, CCH:2 * CCH]), "dq", 16)
            sy.wait(q, "dq")
            for dst in (s_pgr, s_pgi):
                w = CCH
                while w < 512:
                    sy.inc(q.dma_start(out=dst[:, w:2 * w], in_=dst[:, 0:w]), "dq", 16)
                    sy.wait(q, "dq")
                    w *= 2
        PGR = sy.val("dq")

        @block.tensor
        def _(t):
            sy.wait(t, "fm", 32)
            sy.wait(t, "g", YH_G)
            for n in range(8):
                if n > 0:
                    sy.wait(t, "evt", n)
                sl = slice(n * 512, (n + 1) * 512)
                sl1 = slice(4096 + n * 512, 4096 + (n + 1) * 512)
                t.matmul(p_t[:, :], s_tl[:, 0:2], s_yhr[:, sl], start=True, stop=False)
                t.matmul(p_t[:, :], s_tl[:, 2:4], s_yhi[:, sl], start=False, stop=False)
                t.matmul(p_t[:, :], s_tl[:, 4:6], s_yhr[:, sl1], start=False, stop=False)
                t.matmul(p_t[:, :], s_tl[:, 6:8], s_yhi[:, sl1], start=False, stop=True)
                mmfence(t, "mmt", 1)

        @block.vector
        def _(v):
            for n in range(8):
                sy.wait(v, "mmt", n + 1)
                sy.inc(v.tensor_copy(s_tr[0:2, n * 512:(n + 1) * 512], p_t[:, :]), "evt", 1)
            sy.inc(v.tensor_copy(s_cw[0][0:2, 0:8], s_tr[0:2, 0:4096:512]), "evt", 1)
        EVT = sy.val("evt")

        @block.gpsimd
        def _(g):
            sy.wait(g, "evt", EVT)
            sy.inc(g.dma_start(out=dbt.ap(), in_=s_tr[:, :]), "dg", 16)
            sy.wait(g, "dg")
            sy.inc(g.dma_start(out=s_T[:, 0:64], in_=dbt.ap()[0:1, :].rearrange("p (a b) -> (p a) b", a=64)), "dg", 16)
            sy.inc(g.dma_start(out=s_T[:, 64:128], in_=dbt.ap()[1:2, :].rearrange("p (a b) -> (p a) b", a=64)), "dg", 16)
        TLD = sy.val("dg")

        @block.vector
        def _(v):
            sy.wait(v, "dg", TLD)
            v.tensor_tensor(s_cw[0][0:1, :], s_pk[0:1, 0:CCH], s_lc[0:1, 0:CCH], op=ALU.mult)
            v.tensor_tensor(s_cw[1][0:1, :], s_pk[0:1, CCH:], s_lc[0:1, CCH:], op=ALU.mult)
            v.tensor_tensor(s_cw[0][0:1, :], s_cw[0][0:1, :], s_cw[1][0:1, :], op=ALU.subtract)
            v.tensor_tensor(s_T[0:1, 0:64], s_T[0:1, 0:64], s_cw[0][0:1, :], op=ALU.add)
            v.tensor_tensor(s_cw[0][0:1, :], s_pk[0:1, 0:CCH], s_lc[0:1, CCH:], op=ALU.mult)
            v.tensor_tensor(s_cw[1][0:1, :], s_pk[0:1, CCH:], s_lc[0:1, 0:CCH], op=ALU.mult)
            v.tensor_tensor(s_cw[0][0:1, :], s_cw[0][0:1, :], s_cw[1][0:1, :], op=ALU.add)
            sy.inc(v.tensor_tensor(s_T[0:1, 64:128], s_T[0:1, 64:128], s_cw[0][0:1, :], op=ALU.add), "v", 1)
        CH0 = sy.val("v")

        @block.tensor
        def _(t):
            for r in range(7):
                sy.wait(t, "v", CH0 + r)
                shm = s_shc[:, r * 64:(r + 1) * 64]
                t.matmul(p_ch[:, 0:64], shm, s_T[:, 0:64], start=True, stop=True)
                sy.inc(t.matmul(p_ch[:, 64:128], shm, s_T[:, 64:128], start=True, stop=True), "mmc", 1)

        @block.vector
        def _(v):
            for r in range(6):
                sy.wait(v, "mmc", r + 1)
                v.tensor_copy(s_shv[:, :], p_ch[:, 0:128])
                qre = s_qpow[0:64, 2 * r * CCH:(2 * r + 1) * CCH]
                qim = s_qpow[0:64, (2 * r + 1) * CCH:2 * (r + 1) * CCH]
                v.tensor_tensor(s_cw[0][:, :], qre, s_shv[:, 0:64], op=ALU.mult)
                v.tensor_tensor(s_cw[1][:, :], qim, s_shv[:, 64:128], op=ALU.mult)
                v.tensor_tensor(s_cw[0][:, :], s_cw[0][:, :], s_cw[1][:, :], op=ALU.subtract)
                v.tensor_tensor(s_T[:, 0:64], s_T[:, 0:64], s_cw[0][:, :], op=ALU.add)
                v.tensor_tensor(s_cw[0][:, :], qre, s_shv[:, 64:128], op=ALU.mult)
                v.tensor_tensor(s_cw[1][:, :], qim, s_shv[:, 0:64], op=ALU.mult)
                v.tensor_tensor(s_cw[0][:, :], s_cw[0][:, :], s_cw[1][:, :], op=ALU.add)
                sy.inc(v.tensor_tensor(s_T[:, 64:128], s_T[:, 64:128], s_cw[0][:, :], op=ALU.add), "v", 1)
            sy.wait(v, "mmc", 7)
            v.tensor_copy(s_ci[:, :], p_ch[:, 0:128])
            v.tensor_copy(s_ci[0:1, :], s_lc[0:1, :])
            sy.inc(v.tensor_tensor(s_cw[0][:, 0:2], s_ci[:, 0:2], s_ci[:, 126:128], op=ALU.bypass), "v", 1)
        CIV = sy.val("v")

        @block.gpsimd
        def _(g):
            sy.wait(g, "v", CIV)
            sy.inc(g.dma_start(out=dci.ap()[0:1, :].rearrange("p (a b) -> (p a) b", a=64), in_=s_ci[:, 0:64]), "dg", 16)
            sy.inc(g.dma_start(out=dci.ap()[1:2, :].rearrange("p (a b) -> (p a) b", a=64), in_=s_ci[:, 64:128]), "dg", 16)
            sy.wait(g, "dg")
            sy.inc(g.dma_start(out=s_y[0:1, :], in_=dci[0:1, :]), "ci", 16)
            sy.wait(g, "cie", 8)
            sy.inc(g.dma_start(out=s_y[0:1, :], in_=dci[1:2, :]), "ci", 16)

        @block.tensor
        def _(t):
            for i in range(16):
                sy.wait(t, "ci", 16 if i < 8 else 32)
                if i >= 1:
                    sy.wait(t, "cie", i)
                t.matmul(p_g[:, :], s_onesk[:, :], s_y[0:1, (i % 8) * 512:(i % 8 + 1) * 512], start=True, stop=True)
                mmfence(t, "cib", 1)

        @block.vector
        def _(v):
            for i in range(16):
                sy.wait(v, "cib", i + 1)
                dst = s_crr if i < 8 else s_cri
                sl = slice((i % 8) * 512, (i % 8 + 1) * 512)
                sy.inc(v.tensor_copy(dst[:, sl], p_g[:, :]), "cie", 1)
        CIB = sy.val("cie")

        @block.tensor
        def _(t):
            sy.wait(t, "cie", CIB)
            sy.wait(t, "dq", PGR)
            icre0, icre1 = s_ic[:, 0:128], s_ic[:, 128:256]
            icim0, icim1 = s_ic[:, 256:384], s_ic[:, 384:512]
            for n in range(8):
                if n >= 2:
                    sy.wait(t, "evy", n - 1)
                pp = p_a if n % 2 == 0 else p_b
                sl = slice(n * 512, (n + 1) * 512)
                sl1 = slice(4096 + n * 512, 4096 + (n + 1) * 512)
                t.matmul(pp[:, :], icre0, s_yhr[:, sl], start=True, stop=False)
                t.matmul(pp[:, :], icim0, s_yhi[:, sl], start=False, stop=False)
                t.matmul(pp[:, :], icre1, s_yhr[:, sl1], start=False, stop=False)
                t.matmul(pp[:, :], icim1, s_yhi[:, sl1], start=False, stop=True)
                mmfence(t, "mmy", 1)

        @block.vector
        def _(v):
            for n in range(8):
                sy.wait(v, "mmy", n + 1)
                if n >= 2:
                    sy.wait(v, "os", 16 * (n - 1))
                pp = p_a if n % 2 == 0 else p_b
                sl = slice(n * 512, (n + 1) * 512)
                ob = s_os[n % 2]
                v.tensor_tensor(s_t1[:, :], s_pgr[:, :], s_crr[:, sl], op=ALU.mult)
                v.tensor_tensor(s_t2[:, :], s_pgi[:, :], s_cri[:, sl], op=ALU.mult)
                v.tensor_tensor(s_t1[:, :], s_t1[:, :], s_t2[:, :], op=ALU.subtract)
                sy.inc(v.tensor_tensor(ob[:, :], pp[:, :], s_t1[:, :], op=ALU.add), "evy", 1)

        @block.sync
        def _(q):
            for n in range(8):
                sy.wait(q, "evy", n + 1)
                sy.inc(q.dma_start(out=out_d.ap()[:, n * 512:(n + 1) * 512], in_=s_os[n % 2][:, :]), "os", 16)
            sy.wait(q, "os", 16 * 8)
            if debug:
                sy.inc(q.dma_start(out=dbg_T.ap(), in_=s_T[:, :]), "dq", 16)
                sy.inc(q.dma_start(out=dbg_ci.ap(), in_=s_ci[:, :]), "dq", 16)
                sy.inc(q.dma_start(out=dbg_tr.ap(), in_=s_tr[:, :]), "dq", 16)
                sy.inc(q.dma_start(out=dbg_pg.ap(), in_=s_pgp[:, :]), "dq", 16)
                sy.inc(q.dma_start(out=dbg_crr.ap()[:, 0:4096], in_=s_crr[:, :]), "dq", 16)
                sy.inc(q.dma_start(out=dbg_crr.ap()[:, 4096:8192], in_=s_cri[:, :]), "dq", 16)
                sy.inc(q.dma_start(out=dbg_pgr.ap()[:, 0:512], in_=s_pgr[:, :]), "dq", 16)
                sy.inc(q.dma_start(out=dbg_pgr.ap()[:, 512:1024], in_=s_pgi[:, :]), "dq", 16)
                sy.inc(q.dma_start(out=dbg_yh.ap()[:, 0:512], in_=s_yhr[:, 512:1024]), "dq", 16)
                sy.inc(q.dma_start(out=dbg_yh.ap()[:, 512:1024], in_=s_yhi[:, 512:1024]), "dq", 16)
                sy.wait(q, "dq")

    return nc


def conv_consts():
    f = np.arange(CF)
    t = np.arange(CT)
    ang = 2 * np.pi * np.outer(f, t) / CF
    dftre = np.cos(ang).T.astype(np.float32)
    dftim = (-np.sin(ang)).T.astype(np.float32)
    icre = (np.cos(ang) / CF).astype(np.float32)
    icim = (-np.sin(ang) / CF).astype(np.float32)
    ic = np.concatenate([kstack(icre), kstack(icim)], axis=1)
    tl = np.stack([icre[:, 127], -icim[:, 127], icim[:, 127], icre[:, 127]], axis=1)
    tl = kstack(tl)
    shc = np.zeros((64, 64 * 7), np.float32)
    for r in range(6):
        s = 1 << r
        m = np.zeros((64, 64), np.float32)
        for i in range(64 - s):
            m[i, i + s] = 1.0
        shc[:, r * 64:(r + 1) * 64] = m
    m = np.zeros((64, 64), np.float32)
    for i in range(63):
        m[i, i + 1] = 1.0
    shc[:, 6 * 64:7 * 64] = m
    shp = np.zeros((128, 128 * 7), np.float32)
    for r in range(7):
        s = 1 << r
        m = np.zeros((128, 128), np.float32)
        for i in range(128 - s):
            m[i, i + s] = 1.0
        shp[:, r * 128:(r + 1) * 128] = m
    om = np.zeros((128, 6), np.float32)
    fr = np.arange(128)
    om[:, 0] = np.cos(2 * np.pi * fr / CF)
    om[:, 1] = np.cos(2 * np.pi * (fr + 128) / CF)
    om[:, 2] = -np.sin(2 * np.pi * fr / CF)
    om[:, 3] = -np.sin(2 * np.pi * (fr + 128) / CF)
    om[:, 4] = 1.0
    om[:, 5] = -1.0
    om[:, 4] = np.where(fr % 2 == 0, 1.0, -1.0)
    om[:, 5] = np.where((fr + 128) % 2 == 0, 1.0, -1.0)
    return dict(dftre=dftre, dftim=dftim, ic=ic, tl=tl, shc=shc, shp=shp, om=om,
                onesk=np.ones((1, 128), np.float32))


def mstack(wT):
    K, M = wT.shape
    nk, nm = K // 128, M // 128
    return np.ascontiguousarray(
        wT.reshape(nk, 128, nm, 128).transpose(2, 0, 1, 3).reshape(nm * nk, 128, 128)
        .transpose(1, 0, 2).reshape(128, nm * nk * 128))


def build_mid():
    nc = bass.Bass("TRN2", target_bir_lowering=False)
    dI = lambda n, s: nc.dram_tensor(n, s, F32, kind="ExternalInput")
    h_d = dI("h", [128, 4096])
    cv_d = dI("cv", [128, 4096])
    pc_d = dI("pc", [128, 24])
    b1_d = dI("b1c", [128, 16])
    onesk_d = dI("onesk", [1, 128])
    w1_d = dI("w1t", [128, 8192])
    w2_d = dI("w2t", [128, 8192])
    ho_d = nc.dram_tensor("hout", [128, 4096], F32, kind="ExternalOutput")
    y_d = nc.dram_tensor("y", [128, 4096], F32, kind="ExternalOutput")
    dmid = nc.dram_tensor("dmid", [2, 1024], F32)

    with contextlib.ExitStack() as st:
        T = lambda n, s, d=F32: st.enter_context(nc.sbuf_tensor(n, s, d))
        PS = lambda n, s: st.enter_context(nc.psum_tensor(n, s, F32))
        sy = Sy(nc, st)
        block = st.enter_context(nc.Block())

        s_h = T("s_h", [128, 4096])
        s_cv = T("s_cv", [128, 4096])
        s_pc = T("s_pc", [128, 24])
        s_b1 = T("s_b1", [128, 16])
        s_onesk = T("s_onesk", [1, 128])
        s_w1 = T("s_w1", [128, 8192], BF16)
        s_w2 = T("s_w2", [128, 8192], BF16)
        s_ones1 = T("s_ones1", [128, 1], BF16)
        s_eps = T("s_eps", [128, 1])
        s_hb = T("s_hb", [128, 4096], BF16)
        s_sq = T("s_sq", [128, 4096], BF16)
        s_yln = T("s_yln", [128, 4096], BF16)
        s_u = T("s_u", [128, 16384], BF16)
        s_mu = T("s_mu", [1, 1024])
        s_m2 = T("s_m2", [1, 1024])
        s_var = T("s_var", [1, 1024])
        s_rs = T("s_rs", [1, 1024])
        s_kb = T("s_kb", [1, 128], BF16)
        s_rowb = T("s_rowb", [1, 2048], BF16)
        s_crow = s_rowb[0:1, 0:512]
        s_mur = T("s_mur", [128, 1024])
        s_rsr = T("s_rsr", [128, 1024])
        s_a = T("s_a", [128, 1024])
        s_b = T("s_b", [128, 1024])
        s_y = T("s_y", [128, 4096])

        p_a = PS("p_a", [128, 512])
        p_b = PS("p_b", [128, 512])
        p_s = PS("p_s", [1, 1024])
        p_q = PS("p_q", [1, 1024])
        p_bc = PS("p_bc", [128, 512])
        p_f = PS("p_f", [1, 16])

        def mmfence(t, name, n=1):
            sy.inc(t.matmul(p_f[0:1, 0:1], s_eps[:, 0:1], s_eps[:, 0:1], start=True, stop=True), name, n)

        @block.sync
        def _(q):
            for dst, src in [(s_h, h_d), (s_cv, cv_d), (s_pc, pc_d), (s_b1, b1_d), (s_onesk, onesk_d)]:
                sy.inc(q.dma_start(out=dst[:, :], in_=src.ap()), "dq", 16)

        @block.gpsimd
        def _(g):
            for dst, src in [(s_w1, w1_d), (s_w2, w2_d)]:
                sy.inc(g.dma_start(out=dst[:, :], in_=src.ap()), "dg", 16)
            sy.inc(g.dma_start(out=s_kb[:, :], in_=onesk_d.ap()), "dg", 16)
            sy.inc(g.dma_start(out=s_ones1[:, :], in_=onesk_d.ap().rearrange("a b -> b a")), "dg", 16)
        LDQ, LDG = sy.val("dq"), sy.val("dg")

        C = lambda i: slice(i * 1024, (i + 1) * 1024)
        col = lambda base, j: s_pc[:, base * 4 + j:base * 4 + j + 1]

        def stats_phase(v_first_marker):
            pass

        @block.vector
        def _(v):
            sy.wait(v, "dq", LDQ)
            sy.wait(v, "dg", LDG)
            sy.selfedge(v, v.memset(s_eps[:, :], 1e-5))
            for i in range(4):
                v.scalar_tensor_tensor(s_h[:, C(i)], s_cv[:, C(i)], col(0, i), s_h[:, C(i)],
                                       op0=ALU.mult, op1=ALU.add)
                sy.inc(v.tensor_copy(s_hb[:, C(i)], s_h[:, C(i)]), "v", 1)

        @block.scalar
        def _(sc):
            for i in range(4):
                sy.wait(sc, "v", i + 1)
                sy.inc(sc.activation(s_sq[:, C(i)], s_h[:, C(i)], AF.Square), "s", 1)

        @block.tensor
        def _(t):
            sy.wait(t, "v", 4)
            sy.wait(t, "s", 4)
            for half in range(2):
                for i in range(4):
                    t.matmul(p_s[0:1, half * 512:(half + 1) * 512], s_ones1[:, :],
                             s_hb[:, i * 1024 + half * 512:i * 1024 + (half + 1) * 512],
                             start=(i == 0), stop=(i == 3))
            for half in range(2):
                for i in range(4):
                    t.matmul(p_q[0:1, half * 512:(half + 1) * 512], s_ones1[:, :],
                             s_sq[:, i * 1024 + half * 512:i * 1024 + (half + 1) * 512],
                             start=(i == 0), stop=(i == 3))
            mmfence(t, "mm1", 1)
        ST1 = sy.val("mm1")

        def rows_and_bcast(tag, stat_wait):
            @block.vector
            def _(v):
                sy.wait(v, "mm1" if tag == "A" else "mm6", stat_wait)
                v.tensor_copy(s_mu[:, 0:512], p_s[:, 0:512])
                v.tensor_copy(s_mu[:, 512:1024], p_s[:, 512:1024])
                v.tensor_copy(s_m2[:, 0:512], p_q[:, 0:512])
                v.tensor_copy(s_m2[:, 512:1024], p_q[:, 512:1024])
                v.tensor_scalar(s_mu[:, :], s_mu[:, :], 1.0 / 512, None, op0=ALU.mult)
                v.tensor_scalar(s_m2[:, :], s_m2[:, :], 1.0 / 512, None, op0=ALU.mult)
                v.tensor_tensor(s_var[:, :], s_mu[:, :], s_mu[:, :], op=ALU.mult)
                sy.inc(v.tensor_tensor(s_var[:, :], s_m2[:, :], s_var[:, :], op=ALU.subtract), "v", 1)

            @block.scalar
            def _(sc):
                sy.wait(sc, "v")
                sy.inc(sc.activation(s_rs[:, :], s_var[:, :], AF.Sqrt, bias=s_eps[0:1, :]), "s", 1)

            @block.vector
            def _(v):
                sy.wait(v, "s")
                sy.selfedge(v, v.reciprocal(s_rs[:, :], s_rs[:, :]))
                sy.inc(v.tensor_tensor(s_a[0:1, 0:4], s_rs[0:1, 0:4], s_mu[0:1, 0:4], op=ALU.bypass), "v", 1)

            @block.vector
            def _(v):
                v.tensor_copy(s_rowb[0:1, 0:1024], s_mu[:, :])
                sy.inc(v.tensor_copy(s_rowb[0:1, 1024:2048], s_rs[:, :]), "v", 1)

            @block.tensor
            def _(t):
                sy.wait(t, "v")
                for half in range(2):
                    for n in range(2):
                        if sy.val("bc") > 0:
                            sy.wait(t, "bce", sy.val("bc"))
                        sy.inc(t.matmul(p_bc[:, :], s_kb[:, :],
                                        s_rowb[0:1, half * 1024 + n * 512:half * 1024 + (n + 1) * 512],
                                        start=True, stop=True), "bc", 1)

            @block.vector
            def _(v):
                for half, dst in [(0, s_mur), (1, s_rsr)]:
                    for n in range(2):
                        sy.wait(v, "bc", sy.val("bce") + 1)
                        sy.inc(v.tensor_copy(dst[:, n * 512:(n + 1) * 512], p_bc[:, :]), "bce", 1)

        rows_and_bcast("A", ST1)
        BC1 = sy.val("bce")

        @block.vector
        def _(v):
            sy.wait(v, "bce", BC1)
            for i in range(4):
                v.tensor_tensor(s_a[:, :], s_h[:, C(i)], s_mur[:, :], op=ALU.subtract)
                v.tensor_tensor(s_b[:, :], s_a[:, :], s_rsr[:, :], op=ALU.mult)
                sy.inc(v.tensor_scalar(s_yln[:, C(i)], s_b[:, :], col(1, i), col(2, i),
                                       op0=ALU.mult, op1=ALU.add), "yl", 1)
        YL = sy.val("yl")

        @block.tensor
        def _(t):
            sy.wait(t, "dg", LDG)
            sy.wait(t, "yl", YL)
            for m in range(16):
                for n in range(2):
                    idx = m * 2 + n
                    pp = p_a if idx % 2 == 0 else p_b
                    if idx >= 2:
                        sy.wait(t, "ue", idx - 1)
                    for k in range(4):
                        t.matmul(pp[:, :], s_w1[:, (m * 4 + k) * 128:(m * 4 + k + 1) * 128],
                                 s_yln[:, k * 1024 + n * 512:k * 1024 + (n + 1) * 512],
                                 start=(k == 0), stop=(k == 3))
                    mmfence(t, "um", 1)

        @block.scalar
        def _(sc):
            for m in range(16):
                for n in range(2):
                    idx = m * 2 + n
                    pp = p_a if idx % 2 == 0 else p_b
                    sy.wait(sc, "um", idx + 1)
                    sy.inc(sc.activation(s_u[:, m * 1024 + n * 512:m * 1024 + (n + 1) * 512],
                                         pp[:, :], AF.Silu, bias=s_b1[:, m:m + 1]), "ue", 1)
        UE = sy.val("ue")

        @block.tensor
        def _(t):
            sy.wait(t, "ue", UE)
            for j in range(4):
                for n in range(2):
                    idx = j * 2 + n
                    pp = p_a if idx % 2 == 0 else p_b
                    if idx >= 2:
                        sy.wait(t, "he", idx - 1)
                    for k in range(16):
                        t.matmul(pp[:, :], s_w2[:, (j * 16 + k) * 128:(j * 16 + k + 1) * 128],
                                 s_u[:, k * 1024 + n * 512:k * 1024 + (n + 1) * 512],
                                 start=(k == 0), stop=(k == 15))
                    mmfence(t, "hm", 1)

        @block.vector
        def _(v):
            for j in range(4):
                for n in range(2):
                    idx = j * 2 + n
                    pp = p_a if idx % 2 == 0 else p_b
                    sy.wait(v, "hm", idx + 1)
                    sl = slice(j * 1024 + n * 512, j * 1024 + (n + 1) * 512)
                    sy.inc(v.scalar_tensor_tensor(s_h[:, sl], pp[:, :], col(5, j), s_h[:, sl],
                                                  op0=ALU.add, op1=ALU.add), "he", 1)
        HE = sy.val("he")

        @block.vector
        def _(v):
            sy.wait(v, "he", HE)
            for i in range(4):
                sy.inc(v.tensor_copy(s_hb[:, C(i)], s_h[:, C(i)]), "v2", 1)

        @block.scalar
        def _(sc):
            sy.wait(sc, "he", HE)
            for i in range(4):
                sy.wait(sc, "v2", i + 1)
                sy.inc(sc.activation(s_sq[:, C(i)], s_h[:, C(i)], AF.Square), "s2", 1)

        @block.tensor
        def _(t):
            sy.wait(t, "v2", 4)
            sy.wait(t, "s2", 4)
            for half in range(2):
                for i in range(4):
                    t.matmul(p_s[0:1, half * 512:(half + 1) * 512], s_ones1[:, :],
                             s_hb[:, i * 1024 + half * 512:i * 1024 + (half + 1) * 512],
                             start=(i == 0), stop=(i == 3))
            for half in range(2):
                for i in range(4):
                    t.matmul(p_q[0:1, half * 512:(half + 1) * 512], s_ones1[:, :],
                             s_sq[:, i * 1024 + half * 512:i * 1024 + (half + 1) * 512],
                             start=(i == 0), stop=(i == 3))
            mmfence(t, "mm6", 1)
        ST6 = sy.val("mm6")

        rows_and_bcast("B", ST6)
        BC2 = sy.val("bce")

        @block.vector
        def _(v):
            sy.wait(v, "bce", BC2)
            for i in range(4):
                if i > 0:
                    sy.wait(v, "ye", i)
                v.tensor_tensor(s_a[:, :], s_h[:, C(i)], s_mur[:, :], op=ALU.subtract)
                sy.inc(v.tensor_tensor(s_b[:, :], s_a[:, :], s_rsr[:, :], op=ALU.mult), "yb", 1)

        @block.scalar
        def _(sc):
            for i in range(4):
                sy.wait(sc, "yb", i + 1)
                sy.inc(sc.activation(s_y[:, C(i)], s_b[:, :], AF.Silu,
                                     scale=col(3, i), bias=col(4, i)), "ye", 1)

        @block.sync
        def _(q):
            sy.wait(q, "he", HE)
            sy.inc(q.dma_start(out=ho_d.ap(), in_=s_h[:, :]), "os", 16)
            sy.wait(q, "ye", 4)
            sy.inc(q.dma_start(out=y_d.ap(), in_=s_y[:, :]), "os", 16)
            sy.wait(q, "os")

    return nc


def build_biga(upto=9):
    nc = bass.Bass("TRN2", target_bir_lowering=False)
    dI = lambda n, s: nc.dram_tensor(n, s, F32, kind="ExternalInput")
    dB = lambda n, s: nc.dram_tensor(n, s, BF16, kind="ExternalInput")
    xin_d = dI("xin", [1536, 4096])
    wq_d = dB("wqT", [128, 32 * 4096])
    wk_d = dB("wkT", [128, 4096])
    wv_d = dB("wvT", [128, 4096])
    wo_d = dB("woT", [128, 32 * 4096])
    cosq_d = dB("cosq", [128, 1024])
    sinq_d = dB("sinq", [128, 1024])
    cosk_d = dB("cosk", [128, 1536])
    sink_d = dB("sink", [128, 1536])
    aA_d = dB("aA", [1, 1024])
    aB_d = dB("aB", [1, 1024])
    mb_d = dI("mb", [128, 12])
    bw_d = dB("bwT", [128, 32 * 4352])
    gw_d = dB("gwT", [128, 32 * 28])
    la_d = dB("laT", [128, 32 * 256])
    win_d = dB("winT", [128, 32 * 512])
    pc_d = dI("pc", [128, 12])
    onesk_d = dI("onesk", [1, 128])
    cstE_d = dI("cstE", [128, 1])
    cstO_d = dI("cstO", [128, 1])
    ident_d = dI("ident", [128, 128])
    rot_d = dI("rot", [128, 128])
    xn2_d = nc.dram_tensor("xn2", [128, 32768], BF16, kind="ExternalOutput")
    musig_d = nc.dram_tensor("musig", [1, 2048], F32, kind="ExternalOutput")
    h_d = nc.dram_tensor("h", [128, 4096], F32, kind="ExternalOutput")
    y0_d = nc.dram_tensor("y0", [128, 4096], F32, kind="ExternalOutput")
    dmu = nc.dram_tensor("dmu", [2, 1024], F32)
    dmu1 = nc.dram_tensor("dmu1", [2, 1024], F32)

    SCALE = 1.0 / float(np.sqrt(128.0))

    with contextlib.ExitStack() as st:
        T = lambda n, s, d=F32: st.enter_context(nc.sbuf_tensor(n, s, d))
        PS = lambda n, s, d=F32: st.enter_context(nc.psum_tensor(n, s, d))
        sy = Sy(nc, st)
        block = st.enter_context(nc.Block())

        s_xnT = T("s_xnT", [128, 32768], BF16)
        s_big = T("s_big", [128, 32768], BF16)
        s_str = T("s_str", [128, 12288], BF16)
        s_xin = s_str[:, 0:8192].bitcast(F32)
        s_rows = T("s_rows", [1, 1024])
        s_kr = T("s_kr", [128, 1536], BF16)
        s_v = T("s_v", [128, 1536], BF16)
        s_qr = T("s_qr", [128, 1024], BF16)
        s_sig = T("s_sig", [128, 2048], BF16)
        s_avu = T("s_avu", [128, 2048], BF16)
        s_aAB = s_rows[0:1, :].bitcast(BF16)

        s_cab = T("s_cab", [128, 1024], BF16)
        s_mb = T("s_mb", [128, 12])
        s_cq = T("s_cq", [128, 2048], BF16)
        s_ck = T("s_ck", [128, 3072], BF16)
        s_gw = T("s_gw", [128, 896], BF16)
        s_id = T("s_id", [128, 128], BF16)
        s_rt = T("s_rt", [128, 128], BF16)
        s_onesk = T("s_onesk", [1, 128])
        s_kb = T("s_kb", [1, 128], BF16)
        s_ones1 = T("s_ones1", [128, 1], BF16)
        s_eps = T("s_eps", [128, 1])
        s_pc = T("s_pc", [128, 12])
        s_w = [T(f"s_w{i}", [128, 512]) for i in range(2)]
        s_sq = T("s_sq", [128, 512], BF16)
        s_rep = [T(f"s_rep{i}", [128, 512]) for i in range(3)]
        s_rowb = T("s_rowb", [1, 2048], BF16)
        s_crow = s_rowb[0:1, 0:512]
        s_lg = T("s_lg", [128, 256])
        s_gt = None
        s_gx = None
        s_hl = None
        s_stat = T("s_stat", [128, 64])
        s_den = T("s_den", [1, 2048], BF16)

        p_a = PS("p_a", [128, 512])
        p_b = PS("p_b", [128, 512])
        p_s = PS("p_s", [1, 1024])
        p_q = PS("p_q", [1, 1024])
        p_t1 = PS("p_t1", [128, 256], BF16)
        p_r = PS("p_r", [1, 512])
        p_tb = p_a[:, :].bitcast(BF16)

        s_aux = s_big[:, 0:16384]
        s_wk = s_str[:, 0:4096]
        s_wv = s_str[:, 4096:8192]
        s_gx = s_avu[:, :]
        s_gt = s_qr[0:28, :]
        s_hl = s_sig[:, :]
        C1 = lambda i: slice(i * 1024, (i + 1) * 1024)

        @block.sync
        def _(q):
            for dst, src in [(s_mb[:, :], mb_d.ap()), (s_onesk[:, :], onesk_d.ap()), (s_pc[:, :], pc_d.ap()),
                             (s_eps[:, :], cstE_d.ap())]:
                sy.inc(q.dma_start(out=dst, in_=src), "dq", 16)

        @block.gpsimd
        def _(g):
            for dst, src in [(s_gw, gw_d), (s_id, ident_d), (s_rt, rot_d)]:
                sy.inc(g.dma_start(out=dst[:, :], in_=src.ap()), "dg", 16)
            sy.inc(g.dma_start(out=s_ones1[:, :], in_=cstO_d.ap()), "dg", 16)
            sy.inc(g.dma_start(out=s_kb[:, :], in_=onesk_d.ap()), "dg", 16)
            sy.inc(g.dma_start(out=s_cq[:, 0:1024], in_=cosq_d.ap()), "dg", 16)
            sy.inc(g.dma_start(out=s_cq[:, 1024:], in_=sinq_d.ap()), "dg", 16)
            sy.inc(g.dma_start(out=s_ck[:, 0:1536], in_=cosk_d.ap()), "dg", 16)
            sy.inc(g.dma_start(out=s_ck[:, 1536:], in_=sink_d.ap()), "dg", 16)
        LDQ, LDG = sy.val("dq"), sy.val("dg")

        @block.vector
        def _(v):
            sy.wait(v, "dq", LDQ)
            sy.wait(v, "dg", LDG)
            sy.inc(v.nop(), "v", 1)

        for t in range(12):
            @block.sync
            def _(q, t=t):
                if t > 0:
                    sy.wait(q, "trA", t * 32)
                sy.inc(q.dma_start(out=s_xin, in_=xin_d[128 * t:128 * (t + 1), :]), "xt", 16)

            @block.scalar
            def _(sc, t=t):
                sy.wait(sc, "xt", 16 * (t + 1))
                if t == 0:
                    sy.wait(sc, "dq", LDQ)
                    sy.wait(sc, "dg", LDG)
                if t >= 1:
                    sy.wait(sc, "bnA", t)
                sy.inc(sc.activation(s_big[:, 16384:20480], s_xin[:, :], AF.Square), "sqA", 1)

            @block.vector
            def _(v, t=t):
                sy.wait(v, "xt", 16 * (t + 1))
                sy.wait(v, "v", 1)
                sy.wait(v, "sqA", t + 1)
                sy.selfedge(v, v.tensor_reduce(s_stat[:, 48:49], s_xin[:, :], axis=AX.X, op=ALU.add))
                sy.selfedge(v, v.tensor_reduce(s_stat[:, 49:50], s_big[:, 16384:20480], axis=AX.X, op=ALU.add))
                if 3 <= t <= 10:
                    sy.wait(v, "dmu", 32 * (t - 2))
                sy.selfedge(v, v.tensor_scalar(s_stat[:, 50:51], s_stat[:, 48:49], 1.0 / 4096, None, op0=ALU.mult))
                sy.selfedge(v, v.tensor_scalar(s_stat[:, 53:54], s_stat[:, 49:50], 1.0 / 4096, None, op0=ALU.mult))
                sy.selfedge(v, v.tensor_tensor(s_stat[:, 54:55], s_stat[:, 50:51], s_stat[:, 50:51], op=ALU.mult))
                sy.selfedge(v, v.tensor_tensor(s_stat[:, 51:52], s_stat[:, 53:54], s_stat[:, 54:55], op=ALU.subtract))
                sy.inc(v.nop(), "bnA", 1)

            @block.scalar
            def _(sc, t=t):
                sy.wait(sc, "bnA", t + 1)
                sy.selfedge(sc, sc.activation(s_stat[:, 51:52], s_stat[:, 51:52], AF.Sqrt, bias=s_eps[:, :]))
                sy.inc(sc.nop(), "sgA", 1)

            @block.vector
            def _(v, t=t):
                sy.wait(v, "sgA", t + 1)
                sy.selfedge(v, v.reciprocal(s_stat[:, 52:53], s_stat[:, 51:52]))
                sy.inc(v.tensor_scalar(s_str[:, 8192:12288], s_str[:, 0:8192].bitcast(F32), s_stat[:, 50:51], s_stat[:, 52:53],
                                       op0=ALU.subtract, op1=ALU.mult), "nrmA", 1)

            @block.gpsimd
            def _(g, t=t):
                if 2 <= t <= 9:
                    sy.wait(g, "nrmA", t + 1)
                    sy.inc(g.dma_start(out=dmu.ap()[0:1, 128 * (t - 2):128 * (t - 1)].rearrange("a b -> b a"),
                                       in_=s_stat[:, 50:51]), "dmu", 16)
                    sy.inc(g.dma_start(out=dmu.ap()[1:2, 128 * (t - 2):128 * (t - 1)].rearrange("a b -> b a"),
                                       in_=s_stat[:, 51:52]), "dmu", 16)

            @block.tensor
            def _(tn, t=t):
                sy.wait(tn, "nrmA", t + 1)
                for c in range(32):
                    k = sy.val("trA")
                    if k >= 2:
                        sy.wait(tn, "evA", k - 1)
                    pt = p_t1[:, 0:128] if k % 2 == 0 else p_tb[:, 0:128]
                    sy.inc(tn.transpose(pt, s_str[:, 8192 + c * 128:8192 + (c + 1) * 128], s_id[:, :]), "trA", 1)

            @block.vector
            def _(v, t=t):
                for c in range(32):
                    k = t * 32 + c
                    sy.wait(v, "trA", k + 1)
                    if 2 <= t <= 9:
                        dst = s_xnT[:, c * 1024 + 128 * (t - 2):c * 1024 + 128 * (t - 1)]
                    elif t < 2:
                        dst = s_aux[:, c * 512 + 128 * t:c * 512 + 128 * (t + 1)]
                    else:
                        dst = s_aux[:, c * 512 + 256 + 128 * (t - 10):c * 512 + 256 + 128 * (t - 9)]
                    pt = p_t1[:, 0:128] if k % 2 == 0 else p_tb[:, 0:128]
                    sy.inc(v.tensor_copy(dst, pt), "evA", 1)
        TRA = sy.val("evA")

        if upto == 0:
            @block.sync
            def _(q):
                sy.wait(q, "evA", TRA)
                for part in range(4):
                    sy.inc(q.dma_start(out=xn2_d.ap()[:, part * 8192:(part + 1) * 8192],
                                       in_=s_xnT[:, part * 8192:(part + 1) * 8192]), "out", 16)
                sy.inc(q.dma_start(out=h_d.ap(), in_=s_big[:, 0:8192].bitcast(F32)), "out", 16)
                sy.inc(q.dma_start(out=y0_d.ap(), in_=s_big[:, 8192:16384].bitcast(F32)), "out", 16)
                sy.wait(q, "out", 96)
            return nc

        @block.gpsimd
        def _(g):
            sy.wait(g, "evA", TRA)
            sy.inc(g.dma_start(out=s_str[:, 0:4096], in_=wk_d.ap()), "wkv", 16)
            sy.inc(g.dma_start(out=s_str[:, 4096:8192], in_=wv_d.ap()), "wkv", 16)

        def kbuf_rhs(c, lo, wid):
            out = []
            pos = lo
            while wid > 0:
                if pos < 256:
                    w = min(wid, 256 - pos)
                    out.append((s_aux[:, c * 512 + pos:c * 512 + pos + w], pos - lo, w))
                elif pos < 1280:
                    w = min(wid, 1280 - pos)
                    out.append((s_xnT[:, c * 1024 + pos - 256:c * 1024 + pos - 256 + w], pos - lo, w))
                else:
                    w = min(wid, 1536 - pos)
                    out.append((s_aux[:, c * 512 + 256 + pos - 1280:c * 512 + 256 + pos - 1280 + w], pos - lo, w))
                pos += w
                wid -= w
            return out

        @block.tensor
        def _(t):
            sy.wait(t, "wkv", 32)
            for grp in range(3):
                pp = p_a if grp % 2 == 0 else p_b
                if grp >= 2:
                    sy.wait(t, "krv", 1)
                nseg = len(kbuf_rhs(0, grp * 512, 512))
                for j in range(nseg):
                    for c in range(32):
                        srcs, off, w = kbuf_rhs(c, grp * 512, 512)[j]
                        mm = t.matmul(pp[:, off:off + w], s_wk[:, c * 128:(c + 1) * 128], srcs,
                                      start=(c == 0), stop=(c == 31))
                sy.inc(mm, "kmm", 1)
            for vt in range(12):
                pp = p_a if vt % 2 == 0 else p_b
                sy.wait(t, "krv", min(3, 1 + sy.val("vev")) if False else sy.val("vev"))
                if vt >= 1:
                    sy.wait(t, "vev", vt - 1)
                if vt == 0:
                    sy.wait(t, "krv", 3)
                for c in range(32):
                    for srcs, off, w in kbuf_rhs(c, 0, 1536):
                        pass
                for c in range(32):
                    if vt < 2:
                        lh = s_aux[:, c * 512 + 128 * vt:c * 512 + 128 * (vt + 1)]
                    elif vt < 10:
                        lh = s_xnT[:, c * 1024 + 128 * (vt - 2):c * 1024 + 128 * (vt - 1)]
                    else:
                        lh = s_aux[:, c * 512 + 256 + 128 * (vt - 10):c * 512 + 256 + 128 * (vt - 9)]
                    mm = t.matmul(pp[:, 0:128], lh, s_wv[:, c * 128:(c + 1) * 128],
                                  start=(c == 0), stop=(c == 31))
                sy.inc(mm, "vmm", 1)

        @block.vector
        def _(v):
            for grp in range(3):
                sy.wait(v, "kmm", grp + 1)
                pp = p_a if grp % 2 == 0 else p_b
                sl = slice(grp * 512, (grp + 1) * 512)
                ck, sk2 = s_ck[:, sl], s_ck[:, 1536 + grp * 512:1536 + (grp + 1) * 512]
                sy.inc(v.tensor_copy(s_kr[:, sl], pp[:, :]), "krv", 1)

        @block.tensor
        def _(t):
            sy.wait(t, "krv", 3)
            for grp in range(3):
                pp = p_a if grp % 2 == 0 else p_b
                sy.wait(t, "vev", 12)
                if grp >= 2:
                    sy.wait(t, "krf", grp - 1)
                sy.inc(t.matmul(pp[:, :], s_rt[:, :], s_kr[:, grp * 512:(grp + 1) * 512],
                                start=True, stop=True), "krot", 1)

        @block.scalar
        def _(sc):
            for vt in range(12):
                sy.wait(sc, "vmm", vt + 1)
                pp = p_a if vt % 2 == 0 else p_b
                sy.inc(sc.activation(s_v[:, vt * 128:(vt + 1) * 128], pp[:, 0:128], AF.Copy), "vev", 1)

        @block.vector
        def _(v):
            for grp in range(3):
                sy.wait(v, "krot", grp + 1)
                pp = p_a if grp % 2 == 0 else p_b
                sl = slice(grp * 512, (grp + 1) * 512)
                v.tensor_tensor(s_w[0][:, :], s_kr[:, sl], s_ck[:, sl], op=ALU.mult)
                v.tensor_tensor(s_w[1][:, :], pp[:, :], s_ck[:, 1536 + grp * 512:1536 + (grp + 1) * 512], op=ALU.mult)
                sy.inc(v.tensor_tensor(s_kr[:, sl], s_w[0][:, :], s_w[1][:, :], op=ALU.add), "krf", 1)
        KRF = sy.val("krf")

        if upto == 1:
            @block.sync
            def _(q):
                sy.wait(q, "krf", KRF)
                sy.inc(q.dma_start(out=xn2_d.ap()[:, 0:1536], in_=s_kr[:, :]), "out", 16)
                sy.inc(q.dma_start(out=xn2_d.ap()[:, 2048:3584], in_=s_v[:, :]), "out", 16)
                sy.wait(q, "out", 32)
            return nc

        W_OFF = [0, 256, 768, 1280, 1792]
        W_SPAN = [256, 512, 512, 512, 256]
        W_Q0 = [0, 0, 256, 512, 768]
        @block.gpsimd
        def _(g):
            sy.wait(g, "krf", KRF)
            sy.inc(g.dma_start(out=s_aAB[0:1, 0:1024], in_=aA_d.ap()), "ab", 16)
            sy.inc(g.dma_start(out=s_aAB[0:1, 1024:2048], in_=aB_d.ap()), "ab", 16)

        for h in range(32):
            @block.gpsimd
            def _(g, h=h):
                if h >= 2:
                    sy.wait(g, "qmm", 2 * (h - 1))
                sy.inc(g.dma_start(out=s_str[:, (h % 2) * 4096:(h % 2 + 1) * 4096],
                                   in_=wq_d.ap()[:, h * 4096:(h + 1) * 4096]), "wq", 16)

            @block.tensor
            def _(t, h=h):
                sy.wait(t, "wq", 16 * (h + 1))
                if h == 0:
                    sy.wait(t, "krf", KRF)
                    sy.wait(t, "ab", 32)
                else:
                    sy.wait(t, "bl", h * 4)
                wq = s_str[:, (h % 2) * 4096:(h % 2 + 1) * 4096]
                for ng in range(2):
                    pp = p_a if ng == 0 else p_b
                    for c in range(32):
                        mm = t.matmul(pp[:, :], wq[:, c * 128:(c + 1) * 128],
                                      s_xnT[:, c * 1024 + ng * 512:c * 1024 + (ng + 1) * 512],
                                      start=(c == 0), stop=(c == 31))
                    sy.inc(mm, "qmm", 1)
                sy.wait(t, "qev", 2 * (h + 1))
                t.matmul(p_a[:, 0:512], s_rt[:, :], s_qr[:, 0:512], start=True, stop=True)
                sy.inc(t.matmul(p_b[:, 0:512], s_rt[:, :], s_qr[:, 512:1024], start=True, stop=True), "qrot", 1)
                sy.wait(t, "qrf", h + 1)
                for w in range(5):
                    span, q0, s0 = W_SPAN[w], W_Q0[w], 256 * w
                    for j in range(4):
                        if not (w == 0 or (w == 1 and j >= 2) or w >= 2):
                            pass
                        sy.wait(t, "sgev", max(0, sy.val("smm") - 1))
                        pp = p_a if j % 2 == 0 else p_b
                        sy.inc(t.matmul(pp[:, 0:span], s_kr[:, s0 + j * 128:s0 + (j + 1) * 128],
                                        s_qr[:, q0:q0 + span], start=True, stop=True), "smm", 1)
                    sy.wait(t, "sgev", sy.val("smm"))
                    for j in range(4):
                        mm = t.matmul(p_r[0:1, 0:span], s_ones1[:, :], s_sig[:, j * 512:j * 512 + span],
                                      start=(j == 0), stop=(j == 3))
                    sy.inc(mm, "dmm", 1)
                    sy.wait(t, "dev", sy.val("dmm"))
                    for j in range(4):
                        mm = t.matmul(p_a[:, 0:span], s_v[:, (s0 // 128 + j) * 128:(s0 // 128 + j + 1) * 128],
                                      s_sig[:, j * 512:j * 512 + span], start=(j == 0), stop=(j == 3))
                    sy.inc(mm, "amm", 1)
                    sy.wait(t, "aev", sy.val("amm"))

            @block.scalar
            def _(sc, h=h):
                sy.wait(sc, "qmm", 2 * h + 1)
                sc.activation(s_qr[:, 0:512], p_a[:, :], AF.Copy)
                sy.wait(sc, "qmm", 2 * h + 2)
                sy.inc(sc.activation(s_qr[:, 512:1024], p_b[:, :], AF.Copy), "qev", 2)
                for w in range(5):
                    span = W_SPAN[w]
                    base = sy.val("sgev")
                    for j in range(4):
                        sy.wait(sc, "smm", 20 * h + 4 * w + j + 1)
                        pp = p_a if j % 2 == 0 else p_b
                        kt = (256 * w) // 128 + j
                        sy.inc(sc.activation(s_sig[:, j * 512:j * 512 + span], pp[:, 0:span], AF.Sigmoid,
                                             scale=SCALE, bias=s_mb[:, kt:kt + 1]), "sgev", 1)
                    sy.wait(sc, "dmm", 5 * h + w + 1)
                    sy.selfedge(sc, sc.activation(s_den[0:1, W_OFF[w]:W_OFF[w] + span],
                                                  p_r[0:1, 0:span], AF.Copy))
                    sy.inc(sc.nop(), "dev", 1)
                    sy.wait(sc, "amm", 5 * h + w + 1)
                    sy.inc(sc.activation(s_avu[:, W_OFF[w]:W_OFF[w] + span], p_a[:, 0:span], AF.Copy), "aev", 1)

            @block.vector
            def _(v, h=h):
                sy.wait(v, "qrot", h + 1)
                v.tensor_tensor(s_w[0][:, :], s_qr[:, 0:512], s_cq[:, 0:512], op=ALU.mult)
                v.tensor_tensor(s_w[1][:, :], p_a[:, 0:512], s_cq[:, 1024:1536], op=ALU.mult)
                v.tensor_tensor(s_qr[:, 0:512], s_w[0][:, :], s_w[1][:, :], op=ALU.add)
                v.tensor_tensor(s_w[0][:, :], s_qr[:, 512:1024], s_cq[:, 512:1024], op=ALU.mult)
                v.tensor_tensor(s_w[1][:, :], p_b[:, 0:512], s_cq[:, 1536:2048], op=ALU.mult)
                sy.inc(v.tensor_tensor(s_qr[:, 512:1024], s_w[0][:, :], s_w[1][:, :], op=ALU.add), "qrf", 1)

            @block.vector
            def _(v, h=h):
                for m in range(4):
                    sy.wait(v, "aev", 5 * h + m + 2)
                    wA, wB = m, m + 1
                    offA = W_OFF[wA] + (256 * m - W_Q0[wA])
                    offB = W_OFF[wB] + (256 * m - W_Q0[wB])
                    sy.selfedge(v, v.reciprocal(s_w[0][0:1, 0:256], s_den[0:1, offA:offA + 256]))
                    sy.selfedge(v, v.reciprocal(s_w[0][0:1, 256:512], s_den[0:1, offB:offB + 256]))
                    v.tensor_tensor(s_crow[0:1, 0:256], s_aAB[0:1, 256 * m:256 * (m + 1)], s_w[0][0:1, 0:256], op=ALU.mult)
                    sy.inc(v.tensor_tensor(s_crow[0:1, 256:512], s_aAB[0:1, 1024 + 256 * m:1024 + 256 * (m + 1)], s_w[0][0:1, 256:512], op=ALU.mult), "crw", 1)
                    sy.wait(v, "cbc", sy.val("crw"))
                    v.tensor_tensor(s_w[0][:, 0:256], s_avu[:, offA:offA + 256], p_b[:, 0:256], op=ALU.mult)
                    v.tensor_tensor(s_w[1][:, 0:256], s_avu[:, offB:offB + 256], p_b[:, 256:512], op=ALU.mult)
                    sy.inc(v.tensor_tensor(s_big[:, h * 1024 + 256 * m:h * 1024 + 256 * (m + 1)],
                                           s_w[0][:, 0:256], s_w[1][:, 0:256], op=ALU.add), "bl", 1)

            @block.tensor
            def _(t, h=h):
                for m in range(4):
                    sy.wait(t, "crw", 4 * h + m + 1)
                    t.matmul(p_b[:, 0:256], s_kb[:, :], s_crow[0:1, 0:256], start=True, stop=True)
                    sy.inc(t.matmul(p_b[:, 256:512], s_kb[:, :], s_crow[0:1, 256:512], start=True, stop=True), "cbc", 1)
        ATTN = sy.val("bl")

        if upto == 2:
            @block.sync
            def _(q):
                sy.wait(q, "bl", ATTN)
                for part in range(4):
                    sy.inc(q.dma_start(out=xn2_d.ap()[:, part * 8192:(part + 1) * 8192],
                                       in_=s_big[:, part * 8192:(part + 1) * 8192]), "out", 16)
                sy.wait(q, "out", 64)

            @block.gpsimd
            def _(g):
                sy.wait(g, "bl", ATTN)
                sy.inc(g.dma_start(out=musig_d.ap(), in_=s_den[0:1, :]), "outd", 16)
                sy.inc(g.dma_start(out=y0_d.ap()[:, 0:2048], in_=s_avu[:, :]), "outd", 16)
                sy.inc(g.dma_start(out=h_d.ap()[0:1, 0:2048], in_=s_aAB[0:1, :]), "outd", 16)
                sy.inc(g.dma_start(out=h_d.ap()[1:2, 0:128], in_=s_kb[0:1, :]), "outd", 16)
                sy.inc(g.dma_start(out=h_d.ap()[2:3, 0:512], in_=s_crow[0:1, :]), "outd", 16)
                sy.inc(g.dma_start(out=h_d.ap()[3:4, 0:128], in_=s_onesk[0:1, :]), "outd", 16)
                sy.wait(g, "outd", 96)
            return nc

        fview = []
        for buf, ntile in [(s_sig, 4), (s_avu, 4), (s_qr, 2), (s_kr, 3), (s_v, 3),
                           (s_cq, 4), (s_ck, 6), (s_str[:, 8192:12288], 8)]:
            for i2 in range(ntile):
                fview.append(buf[:, i2 * 512:(i2 + 1) * 512])
        fview = fview[:32]
        s_musv = s_den[0:1, :].bitcast(F32)

        for half in range(2):
            @block.gpsimd
            def _(g, half=half):
                if half == 0:
                    sy.wait(g, "bl", ATTN)
                else:
                    sy.wait(g, "ep0h", 32)
                sy.wait(g, "dmu", 256)
                sy.inc(g.dma_start(out=s_musv[0:1, 0:512], in_=dmu.ap()[0:1, half * 512:(half + 1) * 512]), "mus", 16)
                sy.inc(g.dma_start(out=s_musv[0:1, 512:1024], in_=dmu.ap()[1:2, half * 512:(half + 1) * 512]), "mus", 16)
                for o in range(32):
                    if half * 32 + o >= 2:
                        sy.wait(g, "fev", 2 * (half * 32 + o - 1))
                    sy.inc(g.dma_start(out=s_str[:, (o % 2) * 4096:(o % 2 + 1) * 4096],
                                       in_=wo_d.ap()[:, o * 4096:(o + 1) * 4096]), "wo", 16)

            @block.tensor
            def _(t, half=half):
                if half == 0:
                    sy.wait(t, "bl", ATTN)
                else:
                    sy.wait(t, "ep0h", 32)
                for o in range(32):
                    sy.wait(t, "wo", 16 * (half * 32 + o + 1))
                    if half * 32 + o >= 1:
                        sy.wait(t, "fev", 2 * (half * 32 + o))
                    wob = s_str[:, (o % 2) * 4096:(o % 2 + 1) * 4096]
                    for c in range(32):
                        mm = t.matmul(p_a[:, :], wob[:, c * 128:(c + 1) * 128],
                                      s_big[:, c * 1024 + half * 512:c * 1024 + (half + 1) * 512],
                                      start=(c == 0), stop=(c == 31))
                    sy.inc(mm, "fmm", 1)
                    sy.wait(t, "fev", 2 * (half * 32 + o) + 2)
                    t.matmul(p_s[0:1, 0:512], s_ones1[:, :], fview[o], start=(o == 0), stop=(o == 31))
                    sy.inc(t.matmul(p_q[0:1, 0:512], s_ones1[:, :], s_sq[:, 0:512], start=(o == 0), stop=(o == 31)), "sqm", 1)

            @block.scalar
            def _(sc, half=half):
                for o in range(32):
                    sy.wait(sc, "fmm", half * 32 + o + 1)
                    if half * 32 + o >= 1:
                        sy.wait(sc, "sqm", half * 32 + o)
                    sc.activation(fview[o], p_a[:, :], AF.Copy)
                    sy.inc(sc.activation(s_sq[:, 0:512], p_a[:, :], AF.Square), "fev", 2)

            @block.vector
            def _(v, half=half):
                sy.wait(v, "sqm", 32 * (half + 1))
                sy.wait(v, "mus", 32 * (half + 1))
                v.tensor_copy(s_rows[0:1, 0:512], p_s[0:1, 0:512])
                v.tensor_copy(s_w[0][0:1, 0:512], p_q[0:1, 0:512])
                v.tensor_scalar(s_rows[0:1, 0:512], s_rows[0:1, 0:512], 1.0 / 4096, None, op0=ALU.mult)
                v.tensor_scalar(s_w[0][0:1, 0:512], s_w[0][0:1, 0:512], 1.0 / 4096, None, op0=ALU.mult)
                v.tensor_tensor(s_w[1][0:1, 0:512], s_rows[0:1, 0:512], s_rows[0:1, 0:512], op=ALU.mult)
                sy.inc(v.tensor_tensor(s_w[0][0:1, 0:512], s_w[0][0:1, 0:512], s_w[1][0:1, 0:512], op=ALU.subtract), "vq0", 1)

            @block.scalar
            def _(sc, half=half):
                sy.wait(sc, "vq0", half + 1)
                sy.inc(sc.activation(s_w[0][0:1, 0:512], s_w[0][0:1, 0:512], AF.Sqrt, bias=s_eps[0:1, :]), "sq0", 1)

            @block.vector
            def _(v, half=half):
                sy.wait(v, "sq0", half + 1)
                sy.selfedge(v, v.reciprocal(s_w[0][0:1, 0:512], s_w[0][0:1, 0:512]))
                v.tensor_tensor(s_w[1][0:1, 0:512], s_rows[0:1, 0:512], s_w[0][0:1, 0:512], op=ALU.mult)
                v.tensor_tensor(s_w[1][0:1, 0:512], s_musv[0:1, 0:512], s_w[1][0:1, 0:512], op=ALU.subtract)
                v.tensor_copy(s_rowb[0:1, 0:512], s_w[0][0:1, 0:512])
                v.tensor_copy(s_rowb[0:1, 512:1024], s_musv[0:1, 512:1024])
                sy.inc(v.tensor_copy(s_rowb[0:1, 1024:1536], s_w[1][0:1, 0:512]), "rb", 1)

            @block.tensor
            def _(t, half=half):
                sy.wait(t, "rb", sy.val("rb"))
                for r in range(3):
                    if sy.val("bc3") > 0:
                        sy.wait(t, "bce3", sy.val("bc3"))
                    sy.inc(t.matmul(p_b[:, :], s_kb[:, :], s_rowb[0:1, r * 512:(r + 1) * 512],
                                    start=True, stop=True), "bc3", 1)

            @block.vector
            def _(v, half=half):
                for r in range(3):
                    sy.wait(v, "bc3", sy.val("bce3") + 1)
                    sy.inc(v.tensor_copy(s_rep[r][:, :], p_b[:, :]), "bce3", 1)

            @block.vector
            def _(v, half=half):
                sy.wait(v, "bce3", sy.val("bce3"))
                for o in range(32):
                    dst = s_xnT[:, o * 1024 + half * 512:o * 1024 + (half + 1) * 512]
                    v.tensor_tensor(s_w[0][:, 0:512], fview[o], s_rep[0][:, :], op=ALU.mult)
                    v.tensor_tensor(s_w[1][:, 0:512], dst, s_rep[1][:, :], op=ALU.mult)
                    v.tensor_tensor(s_w[0][:, 0:512], s_w[0][:, 0:512], s_w[1][:, 0:512], op=ALU.add)
                    sy.inc(v.tensor_tensor(dst, s_w[0][:, 0:512], s_rep[2][:, :], op=ALU.add), "xn1", 1)

            @block.scalar
            def _(sc, half=half):
                for o in range(32):
                    sy.wait(sc, "xn1", half * 32 + o + 1)
                    if sy.val("xq1") > 0:
                        sy.wait(sc, "xqm", sy.val("xq1"))
                    sy.inc(sc.activation(s_sq[:, 0:512], s_xnT[:, o * 1024 + half * 512:o * 1024 + (half + 1) * 512], AF.Square), "xq1", 1)

            @block.tensor
            def _(t, half=half):
                for o in range(32):
                    sy.wait(t, "xq1", half * 32 + o + 1)
                    sy.inc(t.matmul(p_q[0:1, 512:1024], s_ones1[:, :], s_sq[:, 0:512], start=(o == 0), stop=(o == 31)), "xqm", 1)

            @block.vector
            def _(v, half=half):
                sy.wait(v, "xqm", 32 * (half + 1))
                v.tensor_copy(s_w[0][0:1, 0:512], p_q[0:1, 512:1024])
                v.tensor_scalar(s_w[0][0:1, 0:512], s_w[0][0:1, 0:512], 1.0 / 4096, None, op0=ALU.mult)
                v.tensor_tensor(s_w[1][0:1, 0:512], s_musv[0:1, 0:512], s_musv[0:1, 0:512], op=ALU.mult)
                sy.inc(v.tensor_tensor(s_w[0][0:1, 0:512], s_w[0][0:1, 0:512], s_w[1][0:1, 0:512], op=ALU.subtract), "vv1", 1)

            @block.scalar
            def _(sc, half=half):
                sy.wait(sc, "vv1", half + 1)
                sy.inc(sc.activation(s_w[0][0:1, 0:512], s_w[0][0:1, 0:512], AF.Sqrt, bias=s_eps[0:1, :]), "sg1s", 1)

            @block.gpsimd
            def _(g, half=half):
                sy.wait(g, "sg1v", half + 1)
                sy.inc(g.dma_start(out=dmu1.ap()[1:2, half * 512:(half + 1) * 512], in_=s_rows[0:1, 512:1024]), "dm1", 16)
                sy.inc(g.dma_start(out=dmu1.ap()[0:1, half * 512:(half + 1) * 512], in_=s_musv[0:1, 0:512]), "dm1", 16)

            @block.vector
            def _(v, half=half):
                sy.wait(v, "sg1s", half + 1)
                v.tensor_copy(s_rows[0:1, 512:1024], s_w[0][0:1, 0:512])
                sy.selfedge(v, v.reciprocal(s_w[0][0:1, 0:512], s_w[0][0:1, 0:512]))
                v.tensor_copy(s_rowb[0:1, 0:512], s_w[0][0:1, 0:512])
                v.tensor_copy(s_rowb[0:1, 512:1024], s_musv[0:1, 0:512])
                sy.inc(v.tensor_copy(s_w[1][0:1, 0:4], s_rowb[0:1, 0:4]), "sg1v", 1)

            @block.tensor
            def _(t, half=half):
                sy.wait(t, "sg1v", half + 1)
                for r in range(2):
                    if sy.val("bc1") > 0:
                        sy.wait(t, "bce1", sy.val("bc1"))
                    sy.inc(t.matmul(p_b[:, :], s_kb[:, :], s_rowb[0:1, r * 512:(r + 1) * 512],
                                    start=True, stop=True), "bc1", 1)

            @block.vector
            def _(v, half=half):
                for r in range(2):
                    sy.wait(v, "bc1", sy.val("bce1") + 1)
                    sy.inc(v.tensor_copy(s_rep[r][:, :], p_b[:, :]), "bce1", 1)

            @block.vector
            def _(v, half=half):
                sy.wait(v, "bce1", sy.val("bce1"))
                for o in range(32):
                    dst = s_xnT[:, o * 1024 + half * 512:o * 1024 + (half + 1) * 512]
                    v.tensor_tensor(s_w[0][:, 0:512], dst, s_rep[1][:, :], op=ALU.subtract)
                    sy.inc(v.tensor_tensor(dst, s_w[0][:, 0:512], s_rep[0][:, :], op=ALU.mult), "ep0h", 1)


        EP0 = 64

        if upto == 3:
            @block.sync
            def _(q):
                sy.wait(q, "ep0h", EP0)
                for part in range(4):
                    sy.inc(q.dma_start(out=xn2_d.ap()[:, part * 8192:(part + 1) * 8192],
                                       in_=s_xnT[:, part * 8192:(part + 1) * 8192]), "out", 16)
                sy.wait(q, "out", 64)
            return nc

        for _new, _old in [("lge", "xt"), ("lgm", "bnA"), ("gtb", "sgA"), ("gte", "nrmA"),
                           ("gtt", "trA"), ("gm", "evA"), ("gsg", "wkv"), ("gxd", "krv"),
                           ("lad", "kmm"), ("hle", "vev"), ("hlm", "vmm"), ("hg", "krot"),
                           ("f1e", "krf"), ("bwd", "ab"), ("f1m", "qmm"), ("f1s", "wq"),
                           ("mu1", "qev"), ("vq2", "qrot"), ("sq2", "qrf"), ("rb2", "sgev"),
                           ("bc4", "smm"), ("bce4", "dmm"), ("xn2", "dev"), ("xq2", "amm"),
                           ("xqm2", "aev"), ("st2", "crw"), ("vv2", "cbc"), ("sg2s", "mus"),
                           ("sg2v", "fev"), ("msd", "wo"), ("bc5", "fmm"), ("bce5", "sqm"),
                           ("ep1h", "dmu")]:
            sy.alias(_new, _old)
        s_lgb = s_cab[:, 0:224]
        s_sq2 = s_lg[:, :].bitcast(BF16)

        @block.tensor
        def _(t):
            sy.wait(t, "ep0h", EP0)
            for tt in range(8):
                if tt >= 1:
                    sy.wait(t, "lge", tt)
                for c in range(32):
                    mm = t.matmul(p_a[:, 0:28], s_xnT[:, c * 1024 + tt * 128:c * 1024 + (tt + 1) * 128],
                                  s_gw[:, c * 28:(c + 1) * 28], start=(c == 0), stop=(c == 31))
                sy.inc(mm, "lgm", 1)
            for tt in range(8):
                sy.wait(t, "gtb", tt + 1)
                if tt >= 1:
                    sy.wait(t, "gte", tt)
                sy.inc(t.transpose(p_t1[0:28, 0:128], s_lgb[:, tt * 28:(tt + 1) * 28], s_id[:, :]), "gtt", 1)

        @block.scalar
        def _(sc):
            for tt in range(8):
                sy.wait(sc, "lgm", tt + 1)
                sy.inc(sc.activation(s_lg[:, tt * 28:(tt + 1) * 28], p_a[:, 0:28], AF.Copy), "lge", 1)

        @block.vector
        def _(v):
            for tt in range(8):
                sy.wait(v, "lge", tt + 1)
                lg = s_lg[:, tt * 28:(tt + 1) * 28]
                c0 = s_stat[:, 0:1]
                sy.selfedge(v, v.tensor_reduce(c0, lg, axis=AX.X, op=ALU.max))
                sy.selfedge(v, v.tensor_scalar(s_w[0][:, 0:28], lg, c0, None, op0=ALU.is_equal))
                sy.selfedge(v, v.scalar_tensor_tensor(s_w[1][:, 0:28], s_w[0][:, 0:28], -1e9, lg, op0=ALU.mult, op1=ALU.add))
                sy.selfedge(v, v.tensor_reduce(s_stat[:, 1:2], s_w[1][:, 0:28], axis=AX.X, op=ALU.max))
                sy.selfedge(v, v.tensor_scalar(s_w[1][:, 0:28], lg, s_stat[:, 1:2], None, op0=ALU.is_equal))
                sy.selfedge(v, v.tensor_tensor(s_stat[:, 3:4], s_stat[:, 1:2], c0, op=ALU.subtract))
                sy.inc(v.nop(), "gm", 1)
                sy.wait(v, "gsg", tt + 1)
                sy.selfedge(v, v.tensor_scalar(s_stat[:, 5:6], s_stat[:, 4:5], -1.0, 1.0, op0=ALU.mult, op1=ALU.add))
                sy.selfedge(v, v.tensor_scalar(s_w[0][:, 0:28], s_w[0][:, 0:28], s_stat[:, 5:6], None, op0=ALU.mult))
                sy.selfedge(v, v.tensor_scalar(s_w[1][:, 0:28], s_w[1][:, 0:28], s_stat[:, 4:5], None, op0=ALU.mult))
                sy.selfedge(v, v.tensor_tensor(s_w[0][:, 0:28], s_w[0][:, 0:28], s_w[1][:, 0:28], op=ALU.add))
                sy.selfedge(v, v.tensor_copy(s_lgb[:, tt * 28:(tt + 1) * 28], s_w[0][:, 0:28]))
                sy.inc(v.nop(), "gtb", 1)
            for tt in range(8):
                sy.wait(v, "gtt", tt + 1)
                sy.inc(v.tensor_copy(s_gt[0:28, tt * 128:(tt + 1) * 128], p_t1[0:28, 0:128]), "gte", 1)

        @block.scalar
        def _(sc):
            for tt in range(8):
                sy.wait(sc, "gm", tt + 1)
                sy.selfedge(sc, sc.activation(s_stat[:, 4:5], s_stat[:, 3:4], AF.Sigmoid))
                sy.inc(sc.nop(), "gsg", 1)

        @block.gpsimd
        def _(g):
            sy.wait(g, "gte", 8)
            for rr in range(8):
                dst = s_gx[32 * (rr % 4):32 * (rr % 4) + 28, (rr // 4) * 1024:(rr // 4 + 1) * 1024]
                sy.inc(g.dma_start(out=dst, in_=s_gt[0:28, :]), "gxd", 16)
            sy.inc(g.dma_start(out=s_str[:, 0:8192], in_=la_d.ap()), "lad", 16)

        @block.tensor
        def _(t):
            sy.wait(t, "lad", 16)
            for m2 in range(2):
                for ng in range(2):
                    idx = m2 * 2 + ng
                    if idx >= 1:
                        sy.wait(t, "hle", idx)
                    for c in range(32):
                        mm = t.matmul(p_a[:, :], s_str[:, c * 256 + m2 * 128:c * 256 + (m2 + 1) * 128],
                                      s_xnT[:, c * 1024 + ng * 512:c * 1024 + (ng + 1) * 512],
                                      start=(c == 0), stop=(c == 31))
                    sy.inc(mm, "hlm", 1)

        @block.scalar
        def _(sc):
            for idx in range(4):
                sy.wait(sc, "hlm", idx + 1)
                m2, ng = divmod(idx, 2)
                sy.inc(sc.activation(s_hl[:, m2 * 1024 + ng * 512:m2 * 1024 + (ng + 1) * 512],
                                     p_a[:, :], AF.Copy), "hle", 1)

        @block.vector
        def _(v):
            sy.wait(v, "hle", 4)
            sy.wait(v, "gxd", 16 * 8)
            v.scalar_tensor_tensor(s_hl[:, 0:1024], s_hl[:, 0:1024], 2.0, s_gx[:, 0:1024], op0=ALU.mult, op1=ALU.mult)
            sy.inc(v.scalar_tensor_tensor(s_hl[:, 1024:2048], s_hl[:, 1024:2048], 2.0, s_gx[:, 1024:2048], op0=ALU.mult, op1=ALU.mult), "hg", 1)
        HG = sy.val("hg")

        @block.gpsimd
        def _(g):
            sy.wait(g, "hg", HG)
            for o in range(32):
                if o >= 2:
                    sy.wait(g, "f1e", 4 * (o - 1))
                sy.inc(g.dma_start(out=s_str[:, (o % 2) * 6144:(o % 2) * 6144 + 4352],
                                   in_=bw_d.ap()[:, o * 4352:(o + 1) * 4352]), "bwd", 16)

        @block.tensor
        def _(t):
            sy.wait(t, "hg", HG)
            for o in range(32):
                sy.wait(t, "bwd", 16 * (o + 1))
                if o >= 1:
                    sy.wait(t, "f1e", 4 * o)
                blk = s_str[:, (o % 2) * 6144:(o % 2) * 6144 + 4352]
                for ng in range(2):
                    pp = p_a if ng == 0 else p_b
                    for c in range(32):
                        t.matmul(pp[:, :], blk[:, c * 128:(c + 1) * 128],
                                 s_xnT[:, c * 1024 + ng * 512:c * 1024 + (ng + 1) * 512],
                                 start=(c == 0), stop=False)
                    t.matmul(pp[:, :], blk[:, 4096:4224], s_hl[:, 0 * 1024 + ng * 512:0 * 1024 + (ng + 1) * 512], start=False, stop=False)
                    mm = t.matmul(pp[:, :], blk[:, 4224:4352], s_hl[:, 1 * 1024 + ng * 512:1 * 1024 + (ng + 1) * 512], start=False, stop=True)
                sy.inc(mm, "f1m", 1)
                sy.wait(t, "f1e", 4 * o + 2)
                for ng in range(2):
                    t.matmul(p_s[0:1, ng * 512:(ng + 1) * 512], s_ones1[:, :],
                             s_big[:, o * 1024 + ng * 512:o * 1024 + (ng + 1) * 512], start=(o == 0), stop=(o == 31))
                    mm = t.matmul(p_q[0:1, ng * 512:(ng + 1) * 512], s_ones1[:, :],
                                  s_sq[:, 0:512] if ng == 0 else s_sq2[:, 0:512], start=(o == 0), stop=(o == 31))
                sy.inc(mm, "f1s", 1)

        @block.scalar
        def _(sc):
            for o in range(32):
                sy.wait(sc, "f1m", o + 1)
                if o >= 1:
                    sy.wait(sc, "f1s", o)
                sc.activation(s_big[:, o * 1024:o * 1024 + 512], p_a[:, :], AF.Copy)
                sc.activation(s_sq[:, 0:512], p_a[:, :], AF.Square)
                sc.activation(s_big[:, o * 1024 + 512:(o + 1) * 1024], p_b[:, :], AF.Copy)
                sy.inc(sc.activation(s_sq2[:, 0:512], p_b[:, :], AF.Square), "f1e", 4)
        F1S = 32

        for half in range(2):
            @block.gpsimd
            def _(g, half=half):
                sy.wait(g, "f1s", F1S)
                sy.wait(g, "dm1", 64)
                if half == 1:
                    sy.wait(g, "ep1h", 32)
                sy.inc(g.dma_start(out=s_musv[0:1, 0:512], in_=dmu1.ap()[0:1, half * 512:(half + 1) * 512]), "mu1", 16)
                sy.inc(g.dma_start(out=s_musv[0:1, 512:1024], in_=dmu1.ap()[1:2, half * 512:(half + 1) * 512]), "mu1", 16)

            @block.vector
            def _(v, half=half):
                sy.wait(v, "f1s", F1S)
                sy.wait(v, "mu1", 32 * (half + 1))
                if half == 1:
                    sy.wait(v, "ep1h", 32)
                v.tensor_copy(s_rows[0:1, 0:512], p_s[0:1, half * 512:(half + 1) * 512])
                v.tensor_copy(s_w[0][0:1, 0:512], p_q[0:1, half * 512:(half + 1) * 512])
                v.tensor_scalar(s_rows[0:1, 0:512], s_rows[0:1, 0:512], 1.0 / 4096, None, op0=ALU.mult)
                v.tensor_scalar(s_w[0][0:1, 0:512], s_w[0][0:1, 0:512], 1.0 / 4096, None, op0=ALU.mult)
                v.tensor_tensor(s_w[1][0:1, 0:512], s_rows[0:1, 0:512], s_rows[0:1, 0:512], op=ALU.mult)
                sy.inc(v.tensor_tensor(s_w[0][0:1, 0:512], s_w[0][0:1, 0:512], s_w[1][0:1, 0:512], op=ALU.subtract), "vq2", 1)

            @block.scalar
            def _(sc, half=half):
                sy.wait(sc, "vq2", half + 1)
                sy.inc(sc.activation(s_w[0][0:1, 0:512], s_w[0][0:1, 0:512], AF.Sqrt, bias=s_eps[0:1, :]), "sq2", 1)

            @block.vector
            def _(v, half=half):
                sy.wait(v, "sq2", half + 1)
                sy.selfedge(v, v.reciprocal(s_w[0][0:1, 0:512], s_w[0][0:1, 0:512]))
                v.tensor_tensor(s_w[1][0:1, 0:512], s_rows[0:1, 0:512], s_w[0][0:1, 0:512], op=ALU.mult)
                v.tensor_tensor(s_w[1][0:1, 0:512], s_musv[0:1, 0:512], s_w[1][0:1, 0:512], op=ALU.subtract)
                v.tensor_copy(s_rowb[0:1, 0:512], s_w[0][0:1, 0:512])
                v.tensor_copy(s_rowb[0:1, 512:1024], s_musv[0:1, 512:1024])
                sy.inc(v.tensor_copy(s_rowb[0:1, 1024:1536], s_w[1][0:1, 0:512]), "rb2", 1)

            @block.tensor
            def _(t, half=half):
                sy.wait(t, "rb2", half + 1)
                for r in range(3):
                    if sy.val("bc4") > 0:
                        sy.wait(t, "bce4", sy.val("bc4"))
                    sy.inc(t.matmul(p_b[:, :], s_kb[:, :], s_rowb[0:1, r * 512:(r + 1) * 512],
                                    start=True, stop=True), "bc4", 1)

            @block.vector
            def _(v, half=half):
                for r in range(3):
                    sy.wait(v, "bc4", sy.val("bce4") + 1)
                    sy.inc(v.tensor_copy(s_rep[r][:, :], p_b[:, :]), "bce4", 1)

            @block.vector
            def _(v, half=half):
                sy.wait(v, "bce4", sy.val("bce4"))
                for o in range(32):
                    dst = s_xnT[:, o * 1024 + half * 512:o * 1024 + (half + 1) * 512]
                    fx = s_big[:, o * 1024 + half * 512:o * 1024 + (half + 1) * 512]
                    v.tensor_tensor(s_w[0][:, 0:512], fx, s_rep[0][:, :], op=ALU.mult)
                    v.tensor_tensor(s_w[1][:, 0:512], dst, s_rep[1][:, :], op=ALU.mult)
                    v.tensor_tensor(s_w[0][:, 0:512], s_w[0][:, 0:512], s_w[1][:, 0:512], op=ALU.add)
                    sy.inc(v.tensor_tensor(dst, s_w[0][:, 0:512], s_rep[2][:, :], op=ALU.add), "xn2", 1)

            @block.scalar
            def _(sc, half=half):
                for o in range(32):
                    sy.wait(sc, "xn2", half * 32 + o + 1)
                    if sy.val("xq2") > 0:
                        sy.wait(sc, "xqm2", sy.val("xq2"))
                    sy.inc(sc.activation(s_sq[:, 0:512], s_xnT[:, o * 1024 + half * 512:o * 1024 + (half + 1) * 512], AF.Square), "xq2", 1)

            @block.tensor
            def _(t, half=half):
                for o in range(32):
                    sy.wait(t, "xq2", half * 32 + o + 1)
                    sy.inc(t.matmul(p_q[0:1, half * 512:(half + 1) * 512], s_ones1[:, :], s_sq[:, 0:512],
                                    start=(o == 0), stop=(o == 31)), "xqm2", 1)

            @block.vector
            def _(v, half=half):
                sy.wait(v, "xqm2", 32 * (half + 1))
                v.tensor_copy(s_w[0][0:1, 0:512], p_q[0:1, half * 512:(half + 1) * 512])
                v.tensor_scalar(s_w[0][0:1, 0:512], s_w[0][0:1, 0:512], 1.0 / 4096, None, op0=ALU.mult)
                v.tensor_tensor(s_w[1][0:1, 0:512], s_musv[0:1, 0:512], s_musv[0:1, 0:512], op=ALU.mult)
                sy.inc(v.tensor_tensor(s_w[0][0:1, 0:512], s_w[0][0:1, 0:512], s_w[1][0:1, 0:512], op=ALU.subtract), "vv2", 1)

            @block.scalar
            def _(sc, half=half):
                sy.wait(sc, "vv2", half + 1)
                sy.inc(sc.activation(s_w[0][0:1, 0:512], s_w[0][0:1, 0:512], AF.Sqrt, bias=s_eps[0:1, :]), "sg2s", 1)

            @block.vector
            def _(v, half=half):
                sy.wait(v, "sg2s", half + 1)
                v.tensor_copy(s_rows[0:1, 512:1024], s_w[0][0:1, 0:512])
                sy.selfedge(v, v.reciprocal(s_w[0][0:1, 0:512], s_w[0][0:1, 0:512]))
                v.tensor_copy(s_rowb[0:1, 0:512], s_w[0][0:1, 0:512])
                v.tensor_copy(s_rowb[0:1, 512:1024], s_musv[0:1, 0:512])
                sy.inc(v.tensor_copy(s_w[1][0:1, 0:4], s_rowb[0:1, 0:4]), "sg2v", 1)

            @block.gpsimd
            def _(g, half=half):
                sy.wait(g, "sg2v", half + 1)
                sy.inc(g.dma_start(out=musig_d.ap()[0:1, half * 512:(half + 1) * 512], in_=s_musv[0:1, 0:512]), "msd", 16)
                sy.inc(g.dma_start(out=musig_d.ap()[0:1, 1024 + half * 512:1024 + (half + 1) * 512], in_=s_rows[0:1, 512:1024]), "msd", 16)

            @block.tensor
            def _(t, half=half):
                sy.wait(t, "sg2v", half + 1)
                for r in range(2):
                    if sy.val("bc5") > 0:
                        sy.wait(t, "bce5", sy.val("bc5"))
                    sy.inc(t.matmul(p_b[:, :], s_kb[:, :], s_rowb[0:1, r * 512:(r + 1) * 512],
                                    start=True, stop=True), "bc5", 1)

            @block.vector
            def _(v, half=half):
                for r in range(2):
                    sy.wait(v, "bc5", sy.val("bce5") + 1)
                    sy.inc(v.tensor_copy(s_rep[r][:, :], p_b[:, :]), "bce5", 1)

            @block.vector
            def _(v, half=half):
                sy.wait(v, "bce5", sy.val("bce5"))
                for o in range(32):
                    dst = s_xnT[:, o * 1024 + half * 512:o * 1024 + (half + 1) * 512]
                    v.tensor_tensor(s_w[0][:, 0:512], dst, s_rep[1][:, :], op=ALU.subtract)
                    sy.inc(v.tensor_tensor(dst, s_w[0][:, 0:512], s_rep[0][:, :], op=ALU.mult), "ep1h", 1)
        EP1 = 64

        if upto == 4:
            @block.sync
            def _(q):
                sy.wait(q, "ep1h", EP1)
                for part in range(4):
                    sy.inc(q.dma_start(out=xn2_d.ap()[:, part * 8192:(part + 1) * 8192],
                                       in_=s_xnT[:, part * 8192:(part + 1) * 8192]), "out", 16)
                sy.wait(q, "msd", 64)
                sy.wait(q, "out", 64)
            return nc

        for _new, _old in [("whe", "vq0"), ("wnd", "sq0"), ("whm", "rb"), ("gsm", "bc3"),
                           ("gsq", "bce3"), ("ghb", "xn1"), ("gst", "xq1"), ("gvv", "xqm"),
                           ("gsr", "bl"), ("grb", "vv1"), ("gbc", "sg1s"), ("gbe", "sg1v"),
                           ("gy", "bc1"), ("gye", "bce1"), ("out", "dq")]:
            sy.alias(_new, _old)
        s_hF = s_big[:, 0:8192].bitcast(F32)
        s_yF = s_big[:, 8192:16384].bitcast(F32)

        @block.gpsimd
        def _(g):
            sy.wait(g, "ep1h", EP1)
            for m in range(4):
                if m >= 2:
                    sy.wait(g, "whe", 2 * (m - 1))
                sy.inc(g.dma_start(out=s_str[:, (m % 2) * 4096:(m % 2 + 1) * 4096],
                                   in_=win_d.ap()[:, m * 4096:(m + 1) * 4096]), "wnd", 16)

        @block.tensor
        def _(t):
            sy.wait(t, "ep1h", EP1)
            for m in range(4):
                sy.wait(t, "wnd", 16 * (m + 1))
                if m >= 1:
                    sy.wait(t, "whe", 2 * m)
                blk = s_str[:, (m % 2) * 4096:(m % 2 + 1) * 4096]
                for ng in range(2):
                    pp = p_a if ng == 0 else p_b
                    for c in range(32):
                        mm = t.matmul(pp[:, :], blk[:, c * 128:(c + 1) * 128],
                                      s_xnT[:, c * 1024 + ng * 512:c * 1024 + (ng + 1) * 512],
                                      start=(c == 0), stop=(c == 31))
                sy.inc(mm, "whm", 1)

        @block.vector
        def _(v):
            for m in range(4):
                sy.wait(v, "whm", m + 1)
                v.tensor_scalar(s_hF[:, m * 1024:m * 1024 + 512], p_a[:, :], s_pc[:, m:m + 1], None, op0=ALU.add)
                sy.inc(v.tensor_scalar(s_hF[:, m * 1024 + 512:(m + 1) * 1024], p_b[:, :], s_pc[:, m:m + 1], None, op0=ALU.add), "whe", 2)
        WHE = 8

        s_gtb = s_big[:, 16384:20480]

        @block.scalar
        def _(sc):
            sy.wait(sc, "whe", WHE)
            for i2 in range(4):
                if i2 >= 1:
                    sy.wait(sc, "gsm", 2 * i2)
                sc.activation(s_sq[:, 0:512], s_hF[:, i2 * 1024:i2 * 1024 + 512], AF.Square)
                sy.inc(sc.activation(s_sq2[:, 0:512], s_hF[:, i2 * 1024 + 512:(i2 + 1) * 1024], AF.Square), "gsq", 2)

        @block.vector
        def _(v):
            sy.wait(v, "whe", WHE)
            for i2 in range(4):
                sy.inc(v.tensor_copy(s_gtb[:, i2 * 1024:(i2 + 1) * 1024], s_hF[:, i2 * 1024:(i2 + 1) * 1024]), "ghb", 1)

        @block.tensor
        def _(t):
            for i2 in range(4):
                sy.wait(t, "ghb", i2 + 1)
                sy.wait(t, "gsq", 2 * (i2 + 1))
                t.matmul(p_s[0:1, 0:512], s_ones1[:, :], s_gtb[:, i2 * 1024:i2 * 1024 + 512], start=(i2 == 0), stop=(i2 == 3))
                t.matmul(p_s[0:1, 512:1024], s_ones1[:, :], s_gtb[:, i2 * 1024 + 512:(i2 + 1) * 1024], start=(i2 == 0), stop=(i2 == 3))
                t.matmul(p_q[0:1, 0:512], s_ones1[:, :], s_sq[:, 0:512], start=(i2 == 0), stop=(i2 == 3))
                sy.inc(t.matmul(p_q[0:1, 512:1024], s_ones1[:, :], s_sq2[:, 0:512], start=(i2 == 0), stop=(i2 == 3)), "gsm", 2)

        @block.vector
        def _(v):
            sy.wait(v, "gsm", 8)
            for part in range(2):
                v.tensor_copy(s_rows[0:1, part * 512:(part + 1) * 512], p_s[0:1, part * 512:(part + 1) * 512])
                v.tensor_copy(s_w[part][0:1, 0:512], p_q[0:1, part * 512:(part + 1) * 512])
            v.tensor_scalar(s_rows[0:1, 0:1024], s_rows[0:1, 0:1024], 1.0 / 512, None, op0=ALU.mult)
            v.tensor_scalar(s_w[0][0:1, 0:512], s_w[0][0:1, 0:512], 1.0 / 512, None, op0=ALU.mult)
            v.tensor_scalar(s_w[1][0:1, 0:512], s_w[1][0:1, 0:512], 1.0 / 512, None, op0=ALU.mult)
            v.tensor_tensor(s_rowb[0:1, 0:1024].bitcast(F32)[0:1, 0:512], s_rows[0:1, 0:512], s_rows[0:1, 0:512], op=ALU.mult)
            v.tensor_tensor(s_w[0][0:1, 0:512], s_w[0][0:1, 0:512], s_rowb[0:1, 0:1024].bitcast(F32)[0:1, 0:512], op=ALU.subtract)
            v.tensor_tensor(s_rowb[0:1, 0:1024].bitcast(F32)[0:1, 0:512], s_rows[0:1, 512:1024], s_rows[0:1, 512:1024], op=ALU.mult)
            sy.inc(v.tensor_tensor(s_w[1][0:1, 0:512], s_w[1][0:1, 0:512], s_rowb[0:1, 0:1024].bitcast(F32)[0:1, 0:512], op=ALU.subtract), "gvv", 1)

        @block.scalar
        def _(sc):
            sy.wait(sc, "gvv", 1)
            sc.activation(s_w[0][0:1, 0:512], s_w[0][0:1, 0:512], AF.Sqrt, bias=s_eps[0:1, :])
            sy.inc(sc.activation(s_w[1][0:1, 0:512], s_w[1][0:1, 0:512], AF.Sqrt, bias=s_eps[0:1, :]), "gsr", 1)

        @block.vector
        def _(v):
            sy.wait(v, "gsr", 1)
            sy.selfedge(v, v.reciprocal(s_w[0][0:1, 0:512], s_w[0][0:1, 0:512]))
            sy.selfedge(v, v.reciprocal(s_w[1][0:1, 0:512], s_w[1][0:1, 0:512]))
            v.tensor_copy(s_rowb[0:1, 0:512], s_rows[0:1, 0:512])
            v.tensor_copy(s_rowb[0:1, 512:1024], s_rows[0:1, 512:1024])
            v.tensor_copy(s_rowb[0:1, 1024:1536], s_w[0][0:1, 0:512])
            sy.inc(v.tensor_copy(s_rowb[0:1, 1536:2048], s_w[1][0:1, 0:512]), "grb", 1)

        @block.tensor
        def _(t):
            sy.wait(t, "grb", 1)
            for r in range(4):
                if sy.val("gbc") > 0:
                    sy.wait(t, "gbe", sy.val("gbc"))
                sy.inc(t.matmul(p_b[:, :], s_kb[:, :], s_rowb[0:1, r * 512:(r + 1) * 512],
                                start=True, stop=True), "gbc", 1)

        @block.vector
        def _(v):
            dsts = [s_rep[0], s_rep[1], s_rep[2], s_w[0]]
            for r in range(4):
                sy.wait(v, "gbc", sy.val("gbe") + 1)
                sy.inc(v.tensor_copy(dsts[r][:, 0:512], p_b[:, :]), "gbe", 1)

        @block.vector
        def _(v):
            sy.wait(v, "gbe", 4)
            for i2 in range(4):
                for part in range(2):
                    idx = i2 * 2 + part
                    if idx >= 1:
                        sy.wait(v, "gye", idx)
                    sl = slice(i2 * 1024 + part * 512, i2 * 1024 + (part + 1) * 512)
                    mu = s_rep[part]
                    rr2 = s_rep[2] if part == 0 else s_w[0]
                    v.tensor_tensor(s_w[1][:, 0:512], s_hF[:, sl], mu[:, 0:512], op=ALU.subtract)
                    sy.inc(v.tensor_tensor(s_w[1][:, 0:512], s_w[1][:, 0:512], rr2[:, 0:512], op=ALU.mult), "gy", 1)

        @block.scalar
        def _(sc):
            for i2 in range(4):
                for part in range(2):
                    sy.wait(sc, "gy", i2 * 2 + part + 1)
                    sl = slice(i2 * 1024 + part * 512, i2 * 1024 + (part + 1) * 512)
                    sy.inc(sc.activation(s_yF[:, sl], s_w[1][:, 0:512], AF.Silu,
                                         scale=s_pc[:, 4 + i2:5 + i2], bias=s_pc[:, 8 + i2:9 + i2]), "gye", 1)
        GYE = 8

        @block.sync
        def _(q):
            sy.wait(q, "ep1h", EP1)
            for part in range(4):
                sy.inc(q.dma_start(out=xn2_d.ap()[:, part * 8192:(part + 1) * 8192],
                                   in_=s_xnT[:, part * 8192:(part + 1) * 8192]), "out", 16)
            sy.wait(q, "whe", WHE)
            sy.inc(q.dma_start(out=h_d.ap(), in_=s_hF[:, :]), "out", 16)
            sy.wait(q, "gye", GYE)
            sy.inc(q.dma_start(out=y0_d.ap(), in_=s_yF[:, :]), "out", 16)
            sy.wait(q, "msd", 64)
            sy.wait(q, "out", 96)


    return nc


def rope_cache_ext():
    inv_freq = 1.0 / 10000.0 ** (np.arange(0, 128, 2, dtype=np.float32) / 128)
    t = np.arange(-256, 8448, dtype=np.float32)
    f = np.outer(t, inv_freq)
    cos = np.zeros((8704, 128), np.float32)
    sin = np.zeros((8704, 128), np.float32)
    cos[:, 0::2] = np.cos(f); cos[:, 1::2] = np.cos(f)
    sin[:, 0::2] = np.sin(f); sin[:, 1::2] = np.sin(f)
    return cos, sin


def biga_shared(Wq, Wk, Wv, Wo, base_W, gate_W, lora_A, lora_B, Win, bin_, ln1w0, ln1b0):
    f32 = np.float32
    kc = lambda W, o: kstack(np.ascontiguousarray(W.T[:, 128 * o:128 * (o + 1)], dtype=f32))
    d = {}
    d["wqT"] = np.concatenate([kc(Wq, h) for h in range(32)], axis=1).astype(BF)
    d["wkT"] = kstack(np.ascontiguousarray(Wk.T, dtype=f32)).astype(BF)
    d["wvT"] = kstack(np.ascontiguousarray(Wv.T, dtype=f32)).astype(BF)
    d["woT"] = np.concatenate([kc(Wo, o) for o in range(32)], axis=1).astype(BF)
    lbT = np.zeros((2, 128, 4096), f32)
    for e in range(28):
        for r in range(8):
            m2, rr = divmod(r, 4)
            lbT[m2, 32 * rr + e, :] = lora_B[e, :, r]
    blocks = []
    for o in range(32):
        blocks.append(np.concatenate(
            [kc(base_W, o), lbT[0][:, 128 * o:128 * (o + 1)], lbT[1][:, 128 * o:128 * (o + 1)]], axis=1))
    d["bwT"] = np.ascontiguousarray(np.concatenate(blocks, axis=1)).astype(BF)
    d["gwT"] = kstack(np.ascontiguousarray(gate_W.T, dtype=f32)).astype(BF)
    laP = np.zeros((4096, 256), f32)
    for e in range(28):
        for r in range(8):
            m2, rr = divmod(r, 4)
            laP[:, m2 * 128 + 32 * rr + e] = lora_A[e, r, :]
    d["laT"] = kstack(laP).astype(BF)
    d["winT"] = np.concatenate([kc(Win, m) for m in range(4)], axis=1).astype(BF)
    pc = np.zeros((128, 12), f32)
    for j in range(4):
        pc[:, j] = bin_[128 * j:128 * (j + 1)]
        pc[:, 4 + j] = ln1w0[128 * j:128 * (j + 1)]
        pc[:, 8 + j] = ln1b0[128 * j:128 * (j + 1)]
    d["pc"] = pc
    d["onesk"] = np.ones((1, 128), f32)
    d["cstE"] = np.full((128, 1), 1e-5, f32)
    d["cstO"] = np.ones((128, 1), f32)
    d["ident"] = np.eye(128, dtype=f32)
    R = np.zeros((128, 128), f32)
    for i in range(64):
        R[i + 64, i] = -1.0
        R[i, i + 64] = 1.0
    d["rot"] = R
    d["_cos"], d["_sin"] = rope_cache_ext()
    return d


def biga_core_inputs(c, x0, sh):
    f32 = np.float32
    im = {k: v for k, v in sh.items() if not k.startswith("_")}
    xin = np.zeros((1536, 4096), f32)
    lo, hi = 1024 * c - 256, 1024 * c + 1280
    slo, shi = max(lo, 0), min(hi, 8192)
    xin[slo - lo:shi - lo, :] = x0[slo:shi]
    im["xin"] = xin
    cos, sin = sh["_cos"], sh["_sin"]
    im["cosq"] = np.ascontiguousarray(cos[256 + 1024 * c:256 + 1024 * c + 1024].T).astype(BF)
    im["sinq"] = np.ascontiguousarray(sin[256 + 1024 * c:256 + 1024 * c + 1024].T).astype(BF)
    im["cosk"] = np.ascontiguousarray(cos[1024 * c:1024 * c + 1536].T).astype(BF)
    im["sink"] = np.ascontiguousarray(sin[1024 * c:1024 * c + 1536].T).astype(BF)
    al = np.arange(256, dtype=f32) / 255.0
    aA = np.zeros((1, 1024), f32)
    aB = np.zeros((1, 1024), f32)
    for qb in range(4):
        if 4 * c + qb == 0:
            aA[0, 256 * qb:256 * (qb + 1)] = 0.0
            aB[0, 256 * qb:256 * (qb + 1)] = 1.0
        else:
            aA[0, 256 * qb:256 * (qb + 1)] = 1.0 - al
            aB[0, 256 * qb:256 * (qb + 1)] = al
    im["aA"], im["aB"] = aA.astype(BF), aB.astype(BF)
    mb = np.zeros((128, 12), f32)
    if c == 0:
        mb[:, 0:2] = -60.0
    if c == NCORE - 1:
        mb[:, 10:12] = -60.0
    im["mb"] = mb
    return im


def build_bigb(upto=9):
    nc = bass.Bass("TRN2", target_bir_lowering=False)
    dI = lambda n, s: nc.dram_tensor(n, s, F32, kind="ExternalInput")
    dB = lambda n, s: nc.dram_tensor(n, s, BF16, kind="ExternalInput")
    xn2_d = dB("xn2", [128, 32768])
    musig_d = dI("musig", [1, 2048])
    h_d = dI("h", [128, 4096])
    pc_d = dI("pc", [128, 40])
    wout_d = dB("woutT", [128, 16384])
    a13_d = dB("a13T", [128, 128])
    w13_d = dB("w13", [128, 86 * 2432])
    w2s_d = dB("w2s", [128, 32 * 2944])
    onesk_d = dI("onesk", [1, 128])
    cstE_d = dI("cstE", [128, 1])
    cstO_d = dI("cstO", [128, 1])
    out_d = nc.dram_tensor("out", [128, 32768], F32, kind="ExternalOutput")
    dbg_d = nc.dram_tensor("dbg", [4, 2048], F32, kind="ExternalOutput")

    def fold_segs(m):
        f0 = (128 * m) % 2752
        j = (128 * m) // 2752
        raw = []
        if f0 + 128 <= 2752:
            raw.append((0, f0, 128, j == 0))
        else:
            l1 = 2752 - f0
            raw.append((0, f0, l1, j == 0))
            raw.append((l1, 0, 128 - l1, False))
        segs = []
        for (sl, dp, ln, fw) in raw:
            while ln > 0:
                take = min(ln, 128 - (dp % 128))
                segs.append((sl, dp, take, fw))
                sl += take
                dp += take
                ln -= take
        return segs

    with contextlib.ExitStack() as st:
        T = lambda n, s, d=F32: st.enter_context(nc.sbuf_tensor(n, s, d))
        PS = lambda n, s, d=F32: st.enter_context(nc.psum_tensor(n, s, d))
        sy = Sy(nc, st)
        block = st.enter_context(nc.Block())

        s_xnT = T("s_xnT", [128, 32768], BF16)
        s_big = T("s_big", [128, 16384], BF16)
        s_guf = T("s_guf", [128, 22528], BF16)
        s_str = T("s_str", [128, 5888], BF16)
        s_gud = T("s_gud", [128, 2048], BF16)
        s_ms = T("s_ms", [1, 2048], BF16)
        s_sg3 = s_ms[0:1, 1024:2048]
        s_rows = T("s_rows", [1, 1024])
        s_w = [T(f"s_w{i}", [128, 512]) for i in range(2)]
        s_rep = [T(f"s_rep{i}", [128, 512]) for i in range(3)]
        s_rowb = T("s_rowb", [1, 2048], BF16)
        s_t13 = T("s_t13", [2, 2048], BF16)
        s_t2 = T("s_t2", [2, 1024], BF16)
        s_a13 = T("s_a13", [128, 128], BF16)
        s_sq = T("s_sq", [128, 512], BF16)
        s_gsh = T("s_gsh", [128, 1024], BF16)
        s_kb = T("s_kb", [1, 128], BF16)
        s_ones1 = T("s_ones1", [128, 1], BF16)
        s_eps = T("s_eps", [128, 1])
        s_pc = T("s_pc", [128, 40])

        s_h = s_guf[:, 0:8192].bitcast(F32)
        s_hn = s_guf[:, 8192:12288]
        s_hb = s_guf[:, 12288:16384]
        s_sq2 = s_guf[:, 16384:16896]
        s_gate = s_str[:, 4864:5888]
        s_gf = s_gud[:, :].bitcast(F32)
        s_xs4 = s_big[:, 0:8192]

        p_a = PS("p_a", [128, 512])
        p_b = PS("p_b", [128, 512])
        p_c = PS("p_c", [128, 512])
        p_d = PS("p_d", [128, 512])
        p_s = PS("p_s", [1, 1024])
        p_t2 = PS("p_t2", [4, 1024])
        p_sq = p_t2[0:1, :]

        @block.sync
        def _(q):
            for dst, src in [(s_pc[:, :], pc_d.ap()), (s_eps[:, :], cstE_d.ap()),
                             (s_a13[:, :], a13_d.ap())]:
                sy.inc(q.dma_start(out=dst, in_=src), "dq", 16)
            for part in range(4):
                sy.inc(q.dma_start(out=s_xnT[:, part * 8192:(part + 1) * 8192],
                                   in_=xn2_d.ap()[:, part * 8192:(part + 1) * 8192]), "dq", 16)

        @block.gpsimd
        def _(g):
            sy.inc(g.dma_start(out=s_ones1[:, :], in_=cstO_d.ap()), "dg", 16)
            sy.inc(g.dma_start(out=s_ms[:, :], in_=musig_d.ap()), "dg", 16)
            sy.inc(g.dma_start(out=s_kb[:, :], in_=onesk_d.ap()), "dg", 16)
            sy.inc(g.dma_start(out=s_h[:, :], in_=h_d.ap()), "dg", 16)
        LDQ, LDG = sy.val("dq"), sy.val("dg")

        if upto != 85:
            @block.vector
            def _(v):
                sy.wait(v, "dq", LDQ)
                sy.wait(v, "dg", LDG)
                for i2 in range(4):
                    sy.inc(v.tensor_copy(s_hb[:, i2 * 1024:(i2 + 1) * 1024], s_h[:, i2 * 1024:(i2 + 1) * 1024]), "hbv", 1)

            @block.scalar
            def _(sc):
                sy.wait(sc, "dq", LDQ)
                sy.wait(sc, "dg", LDG)
                for i2 in range(4):
                    if i2 >= 1:
                        sy.wait(sc, "hsm", 2 * i2)
                    sc.activation(s_sq[:, 0:512], s_h[:, i2 * 1024:i2 * 1024 + 512], AF.Square)
                    sy.inc(sc.activation(s_sq2[:, 0:512], s_h[:, i2 * 1024 + 512:(i2 + 1) * 1024], AF.Square), "hsq", 2)

            @block.tensor
            def _(t):
                for i2 in range(4):
                    sy.wait(t, "hbv", i2 + 1)
                    sy.wait(t, "hsq", 2 * (i2 + 1))
                    t.matmul(p_s[0:1, 0:512], s_ones1[:, :], s_hb[:, i2 * 1024:i2 * 1024 + 512], start=(i2 == 0), stop=(i2 == 3))
                    t.matmul(p_s[0:1, 512:1024], s_ones1[:, :], s_hb[:, i2 * 1024 + 512:(i2 + 1) * 1024], start=(i2 == 0), stop=(i2 == 3))
                    t.matmul(p_sq[0:1, 0:512], s_ones1[:, :], s_sq[:, 0:512], start=(i2 == 0), stop=(i2 == 3))
                    sy.inc(t.matmul(p_sq[0:1, 512:1024], s_ones1[:, :], s_sq2[:, 0:512], start=(i2 == 0), stop=(i2 == 3)), "hsm", 2)

            @block.vector
            def _(v):
                sy.wait(v, "hsm", 8)
                for part in range(2):
                    v.tensor_copy(s_rows[0:1, part * 512:(part + 1) * 512], p_s[0:1, part * 512:(part + 1) * 512])
                    v.tensor_copy(s_w[part][0:1, 0:512], p_sq[0:1, part * 512:(part + 1) * 512])
                v.tensor_scalar(s_rows[0:1, 0:1024], s_rows[0:1, 0:1024], 1.0 / 512, None, op0=ALU.mult)
                v.tensor_scalar(s_w[0][0:1, 0:512], s_w[0][0:1, 0:512], 1.0 / 512, None, op0=ALU.mult)
                v.tensor_scalar(s_w[1][0:1, 0:512], s_w[1][0:1, 0:512], 1.0 / 512, None, op0=ALU.mult)
                v.tensor_tensor(s_gf[0:1, 0:512], s_rows[0:1, 0:512], s_rows[0:1, 0:512], op=ALU.mult)
                v.tensor_tensor(s_w[0][0:1, 0:512], s_w[0][0:1, 0:512], s_gf[0:1, 0:512], op=ALU.subtract)
                v.tensor_scalar(s_w[0][0:1, 0:512], s_w[0][0:1, 0:512], 1e-5, None, op0=ALU.add)
                v.tensor_tensor(s_gf[0:1, 0:512], s_rows[0:1, 512:1024], s_rows[0:1, 512:1024], op=ALU.mult)
                v.tensor_tensor(s_w[1][0:1, 0:512], s_w[1][0:1, 0:512], s_gf[0:1, 0:512], op=ALU.subtract)
                sy.inc(v.tensor_scalar(s_w[1][0:1, 0:512], s_w[1][0:1, 0:512], 1e-5, None, op0=ALU.add), "hvv", 1)

            @block.scalar
            def _(sc):
                sy.wait(sc, "hvv", 1)
                sc.activation(s_w[0][0:1, 0:512], s_w[0][0:1, 0:512], AF.Sqrt)
                sy.inc(sc.activation(s_w[1][0:1, 0:512], s_w[1][0:1, 0:512], AF.Sqrt), "hsr", 1)

            @block.vector
            def _(v):
                sy.wait(v, "hsr", 1)
                sy.selfedge(v, v.reciprocal(s_w[0][0:1, 0:512], s_w[0][0:1, 0:512]))
                sy.selfedge(v, v.reciprocal(s_w[1][0:1, 0:512], s_w[1][0:1, 0:512]))
                v.tensor_copy(s_rowb[0:1, 0:512], s_rows[0:1, 0:512])
                v.tensor_copy(s_rowb[0:1, 512:1024], s_rows[0:1, 512:1024])
                v.tensor_copy(s_rowb[0:1, 1024:1536], s_w[0][0:1, 0:512])
                sy.inc(v.tensor_copy(s_rowb[0:1, 1536:2048], s_w[1][0:1, 0:512]), "hrb", 1)

            @block.tensor
            def _(t):
                sy.wait(t, "hrb", 1)
                for r in range(4):
                    if sy.val("hbc") > 0:
                        sy.wait(t, "hbe", sy.val("hbc"))
                    sy.inc(t.matmul(p_b[:, :], s_kb[:, :], s_rowb[0:1, r * 512:(r + 1) * 512],
                                    start=True, stop=True), "hbc", 1)

            @block.vector
            def _(v):
                dsts = [s_rep[0], s_rep[1], s_rep[2], s_gf[:, 0:512]]
                for r in range(4):
                    sy.wait(v, "hbc", sy.val("hbe") + 1)
                    sy.inc(v.tensor_copy(dsts[r][:, 0:512], p_b[:, :]), "hbe", 1)

            @block.vector
            def _(v):
                sy.wait(v, "hbe", 4)
                for i2 in range(4):
                    for part in range(2):
                        sl = slice(i2 * 1024 + part * 512, i2 * 1024 + (part + 1) * 512)
                        mu = s_rep[part]
                        rr2 = s_rep[2] if part == 0 else s_gf[:, 0:512]
                        v.tensor_tensor(s_w[0][:, 0:512], s_h[:, sl], mu[:, 0:512], op=ALU.subtract)
                        v.tensor_tensor(s_w[0][:, 0:512], s_w[0][:, 0:512], rr2[:, 0:512], op=ALU.mult)
                        sy.inc(v.tensor_scalar(s_hn[:, sl], s_w[0][:, 0:512], s_pc[:, i2:i2 + 1], s_pc[:, 4 + i2:5 + i2],
                                               op0=ALU.mult, op1=ALU.add), "hnv", 1)
            HNV = 8

            for half in range(2):
                @block.gpsimd
                def _(g, half=half):
                    if half == 0:
                        sy.wait(g, "hnv", HNV)
                    else:
                        sy.wait(g, "ep2", 32)
                    for o in range(32):
                        if half * 32 + o >= 2:
                            sy.wait(g, "f2e", half * 32 + o - 1)
                        sy.inc(g.dma_start(out=s_str[:, (o % 2) * 512:(o % 2) * 512 + 512],
                                           in_=wout_d.ap()[:, o * 512:(o + 1) * 512]), "wod", 16)

                @block.tensor
                def _(t, half=half):
                    if half == 0:
                        sy.wait(t, "hnv", HNV)
                    else:
                        sy.wait(t, "ep2", 32)
                    for o in range(32):
                        sy.wait(t, "wod", 16 * (half * 32 + o + 1))
                        if half * 32 + o >= 1:
                            sy.wait(t, "f2e", half * 32 + o)
                        blk = s_str[:, (o % 2) * 512:(o % 2) * 512 + 512]
                        for k in range(4):
                            mm = t.matmul(p_a[:, :], blk[:, k * 128:(k + 1) * 128],
                                          s_hn[:, k * 1024 + half * 512:k * 1024 + (half + 1) * 512],
                                          start=(k == 0), stop=(k == 3))
                        sy.inc(mm, "f2m", 1)
                        sy.wait(t, "f2q", half * 32 + o + 1)
                        t.matmul(p_s[0:1, half * 512:(half + 1) * 512], s_ones1[:, :],
                                 s_big[:, o * 512:(o + 1) * 512], start=(o == 0), stop=(o == 31))
                        sy.inc(t.matmul(p_sq[0:1, half * 512:(half + 1) * 512], s_ones1[:, :],
                                        s_sq[:, 0:512], start=(o == 0), stop=(o == 31)), "f2s", 1)

                @block.vector
                def _(v, half=half):
                    for o in range(32):
                        sy.wait(v, "f2m", half * 32 + o + 1)
                        sy.inc(v.tensor_scalar(s_big[:, o * 512:(o + 1) * 512], p_a[:, :],
                                               s_pc[:, 8 + o:9 + o], None, op0=ALU.add), "f2e", 1)

                @block.scalar
                def _(sc, half=half):
                    for o in range(32):
                        sy.wait(sc, "f2e", half * 32 + o + 1)
                        if half * 32 + o >= 1:
                            sy.wait(sc, "f2s", half * 32 + o)
                        sy.inc(sc.activation(s_sq[:, 0:512], s_big[:, o * 512:(o + 1) * 512], AF.Square), "f2q", 1)

                @block.vector
                def _(v, half=half):
                    sy.wait(v, "f2s", 32 * (half + 1))
                    v.tensor_copy(s_rows[0:1, 0:512], p_s[0:1, half * 512:(half + 1) * 512])
                    v.tensor_copy(s_w[0][0:1, 0:512], p_sq[0:1, half * 512:(half + 1) * 512])
                    v.tensor_scalar(s_rows[0:1, 0:512], s_rows[0:1, 0:512], 1.0 / 4096, None, op0=ALU.mult)
                    v.tensor_scalar(s_w[0][0:1, 0:512], s_w[0][0:1, 0:512], 1.0 / 4096, None, op0=ALU.mult)
                    v.tensor_tensor(s_w[1][0:1, 0:512], s_rows[0:1, 0:512], s_rows[0:1, 0:512], op=ALU.mult)
                    v.tensor_tensor(s_w[0][0:1, 0:512], s_w[0][0:1, 0:512], s_w[1][0:1, 0:512], op=ALU.subtract)
                    sy.inc(v.tensor_scalar(s_w[0][0:1, 0:512], s_w[0][0:1, 0:512], 1e-5, None, op0=ALU.add), "vq2", 1)

                @block.scalar
                def _(sc, half=half):
                    sy.wait(sc, "vq2", half + 1)
                    sy.inc(sc.activation(s_w[0][0:1, 0:512], s_w[0][0:1, 0:512], AF.Sqrt), "sq2", 1)

                @block.vector
                def _(v, half=half):
                    sy.wait(v, "sq2", half + 1)
                    sy.selfedge(v, v.reciprocal(s_w[0][0:1, 0:512], s_w[0][0:1, 0:512]))
                    v.tensor_tensor(s_w[1][0:1, 0:512], s_rows[0:1, 0:512], s_w[0][0:1, 0:512], op=ALU.mult)
                    v.tensor_tensor(s_w[1][0:1, 0:512], s_ms[0:1, half * 512:(half + 1) * 512], s_w[1][0:1, 0:512], op=ALU.subtract)
                    v.tensor_copy(s_rowb[0:1, 0:512], s_w[0][0:1, 0:512])
                    v.tensor_copy(s_rowb[0:1, 512:1024], s_ms[0:1, 1024 + half * 512:1024 + (half + 1) * 512])
                    sy.inc(v.tensor_copy(s_rowb[0:1, 1024:1536], s_w[1][0:1, 0:512]), "rb2", 1)

                @block.tensor
                def _(t, half=half):
                    sy.wait(t, "rb2", half + 1)
                    for r in range(3):
                        if sy.val("bc2") > 0:
                            sy.wait(t, "be2", sy.val("bc2"))
                        sy.inc(t.matmul(p_b[:, :], s_kb[:, :], s_rowb[0:1, r * 512:(r + 1) * 512],
                                        start=True, stop=True), "bc2", 1)

                @block.vector
                def _(v, half=half):
                    for r in range(3):
                        sy.wait(v, "bc2", sy.val("be2") + 1)
                        sy.inc(v.tensor_copy(s_rep[r][:, :], p_b[:, :]), "be2", 1)

                @block.vector
                def _(v, half=half):
                    sy.wait(v, "be2", sy.val("be2"))
                    for o in range(32):
                        dst = s_xnT[:, o * 1024 + half * 512:o * 1024 + (half + 1) * 512]
                        fx = s_big[:, o * 512:(o + 1) * 512]
                        v.tensor_tensor(s_w[0][:, 0:512], fx, s_rep[0][:, :], op=ALU.mult)
                        v.tensor_tensor(s_w[1][:, 0:512], dst, s_rep[1][:, :], op=ALU.mult)
                        v.tensor_tensor(s_w[0][:, 0:512], s_w[0][:, 0:512], s_w[1][:, 0:512], op=ALU.add)
                        sy.inc(v.tensor_tensor(dst, s_w[0][:, 0:512], s_rep[2][:, :], op=ALU.add), "x3v", 1)

                @block.scalar
                def _(sc, half=half):
                    for o in range(32):
                        sy.wait(sc, "x3v", half * 32 + o + 1)
                        if sy.val("x3q") > 0:
                            sy.wait(sc, "x3m", sy.val("x3q"))
                        sy.inc(sc.activation(s_sq[:, 0:512], s_xnT[:, o * 1024 + half * 512:o * 1024 + (half + 1) * 512], AF.Square), "x3q", 1)

                @block.tensor
                def _(t, half=half):
                    for o in range(32):
                        sy.wait(t, "x3q", half * 32 + o + 1)
                        sy.inc(t.matmul(p_sq[0:1, half * 512:(half + 1) * 512], s_ones1[:, :], s_sq[:, 0:512],
                                        start=(o == 0), stop=(o == 31)), "x3m", 1)

                @block.vector
                def _(v, half=half):
                    sy.wait(v, "x3m", 32 * (half + 1))
                    v.tensor_copy(s_w[0][0:1, 0:512], p_sq[0:1, half * 512:(half + 1) * 512])
                    v.tensor_scalar(s_w[0][0:1, 0:512], s_w[0][0:1, 0:512], 1.0 / 4096, None, op0=ALU.mult)
                    v.tensor_tensor(s_w[1][0:1, 0:512], s_ms[0:1, half * 512:(half + 1) * 512],
                                    s_ms[0:1, half * 512:(half + 1) * 512], op=ALU.mult)
                    v.tensor_tensor(s_w[0][0:1, 0:512], s_w[0][0:1, 0:512], s_w[1][0:1, 0:512], op=ALU.subtract)
                    sy.inc(v.tensor_scalar(s_w[0][0:1, 0:512], s_w[0][0:1, 0:512], 1e-5, None, op0=ALU.add), "vv3", 1)

                @block.scalar
                def _(sc, half=half):
                    sy.wait(sc, "vv3", half + 1)
                    sy.inc(sc.activation(s_w[0][0:1, 0:512], s_w[0][0:1, 0:512], AF.Sqrt), "sg3s", 1)

                @block.vector
                def _(v, half=half):
                    sy.wait(v, "sg3s", half + 1)
                    v.tensor_copy(s_sg3[0:1, half * 512:(half + 1) * 512], s_w[0][0:1, 0:512])
                    sy.selfedge(v, v.reciprocal(s_w[0][0:1, 0:512], s_w[0][0:1, 0:512]))
                    v.tensor_copy(s_rowb[0:1, 0:512], s_w[0][0:1, 0:512])
                    v.tensor_copy(s_rowb[0:1, 512:1024], s_ms[0:1, half * 512:(half + 1) * 512])
                    sy.inc(v.tensor_copy(s_w[1][0:1, 0:4], s_rowb[0:1, 0:4]), "sg3v", 1)

                @block.tensor
                def _(t, half=half):
                    sy.wait(t, "sg3v", half + 1)
                    for r in range(2):
                        if sy.val("bc3") > 0:
                            sy.wait(t, "be3", sy.val("bc3"))
                        sy.inc(t.matmul(p_b[:, :], s_kb[:, :], s_rowb[0:1, r * 512:(r + 1) * 512],
                                        start=True, stop=True), "bc3", 1)

                @block.vector
                def _(v, half=half):
                    for r in range(2):
                        sy.wait(v, "bc3", sy.val("be3") + 1)
                        sy.inc(v.tensor_copy(s_rep[r][:, :], p_b[:, :]), "be3", 1)

                @block.vector
                def _(v, half=half):
                    sy.wait(v, "be3", sy.val("be3"))
                    for o in range(32):
                        dst = s_xnT[:, o * 1024 + half * 512:o * 1024 + (half + 1) * 512]
                        v.tensor_tensor(s_w[0][:, 0:512], dst, s_rep[1][:, :], op=ALU.subtract)
                        sy.inc(v.tensor_tensor(dst, s_w[0][:, 0:512], s_rep[0][:, :], op=ALU.mult), "ep2", 1)
            EP2 = 64

            if upto == 0:
                @block.gpsimd
                def _(g):
                    sy.wait(g, "ep2", EP2)
                    sy.inc(g.dma_start(out=out_d.ap(), in_=s_xnT[:, :]), "out", 16)
                    sy.inc(g.dma_start(out=dbg_d.ap()[0:1, 0:1024], in_=s_sg3[0:1, :]), "out", 16)
                    sy.wait(g, "out", 32)
                return nc



        else:
            EP2 = 64
        if upto != 85:
            @block.vector
            def _(v):
                sy.wait(v, "ep2", EP2)
                for fc in range(8):
                    v.tensor_copy(s_xs4[:, fc * 1024:(fc + 1) * 1024], s_xnT[:, fc * 1024:(fc + 1) * 1024])
                    for j in range(1, 4):
                        cc = fc + 8 * j
                        mm = v.tensor_tensor(s_xs4[:, fc * 1024:(fc + 1) * 1024], s_xs4[:, fc * 1024:(fc + 1) * 1024],
                                             s_xnT[:, cc * 1024:(cc + 1) * 1024], op=ALU.add)
                    sy.inc(mm, "xs4", 1)
                if upto == 84:
                    sy.inc(v.tensor_tensor(s_guf[0:128, 20 * 1024:21 * 1024], s_xnT[0:128, 0:1024],
                                           s_xnT[0:128, 0:1024], op=ALU.subtract), "gz", 1)
                else:
                    sy.inc(v.tensor_tensor(s_guf[64:128, 21 * 1024:22 * 1024], s_xnT[64:128, 0:1024],
                                           s_xnT[64:128, 0:1024], op=ALU.subtract), "gz", 1)

            if upto == 4:
                @block.gpsimd
                def _(g):
                    sy.wait(g, "xs4", 8)
                    sy.wait(g, "gz", 1)
                    sy.inc(g.dma_start(out=out_d.ap()[:, 0:8192], in_=s_xs4[:, :]), "out", 16)
                    sy.wait(g, "out", 16)
                return nc

            @block.tensor
            def _(t):
                sy.wait(t, "ep2", EP2)
                for ng in range(2):
                    pp = p_a if ng == 0 else p_b
                    for c in range(32):
                        mm = t.matmul(pp[0:4, :], s_a13[:, c * 4:(c + 1) * 4],
                                      s_xnT[:, c * 1024 + ng * 512:c * 1024 + (ng + 1) * 512],
                                      start=(c == 0), stop=(c == 31))
                    sy.inc(mm, "t13m", 1)

            @block.vector
            def _(v):
                sy.wait(v, "t13m", 2)
                s_t13s = s_gud[0:4, 0:1024]
                sy.selfedge(v, v.tensor_scalar(s_t13s[:, 0:512], p_a[0:4, :], 0.5, None, op0=ALU.mult))
                sy.selfedge(v, v.tensor_scalar(s_t13s[:, 512:1024], p_b[0:4, :], 0.5, None, op0=ALU.mult))
                sy.inc(v.nop(), "t13v", 1)

            @block.gpsimd
            def _(g):
                sy.wait(g, "t13v", 1)
                sy.inc(g.dma_start(out=s_t13[0:2, 0:1024], in_=s_gud[0:2, 0:1024]), "t13d", 16)
                sy.inc(g.dma_start(out=s_t13[0:2, 1024:2048], in_=s_gud[2:4, 0:1024]), "t13d", 16)

            @block.gpsimd
            def _(g):
                sy.wait(g, "t13d", 32)
                sy.inc(g.nop(), "t13e", 1)

            if upto == 5:
                @block.gpsimd
                def _(g):
                    sy.wait(g, "t13e", 1)
                    sy.wait(g, "xs4", 8)
                    sy.wait(g, "gz", 1)
                    sy.inc(g.dma_start(out=dbg_d.ap()[0:2, 0:2048], in_=s_t13[:, :]), "out", 16)
                    sy.wait(g, "out", 16)
                return nc


        else:
            @block.vector
            def _(v):
                sy.wait(v, "dq", LDQ)
                sy.wait(v, "dg", LDG)
                for _ in range(8):
                    sy.inc(v.nop(), "xs4", 1)
                sy.inc(v.nop(), "gz", 1)
                sy.inc(v.nop(), "t13v", 1)

            @block.gpsimd
            def _(g):
                sy.wait(g, "t13v", 1)
                sy.inc(g.nop(), "t13e", 1)
                sy.inc(g.nop(), "t13m", 0)

        W1O, W3O, B1O, B3O, A2O = 0, 1024, 2048, 2176, 2304

        def seg_src(m):
            out = []
            for (sl, dp, ln, fw) in fold_segs(m):
                off = dp % 128
                if sl % 128 == off:
                    out.append(("gud", sl, dp, ln, fw))
                else:
                    assert (sl + 64) % 128 == off
                    out.append(("gsh", off, dp, ln, fw))
            return out

        needs_gsh = [any(s[0] == "gsh" for s in seg_src(m)) for m in range(86)]
        cum_fld = [0]
        for m in range(86):
            cum_fld.append(cum_fld[-1] + len(fold_segs(m)))
        cum_gsh = [0]
        for m in range(86):
            cum_gsh.append(cum_gsh[-1] + (1 if needs_gsh[m] else 0))

        DO_T2 = upto != 6
        DO_FOLD = upto not in (6, 7)
        DO_GSH = upto not in (8, 87, 88)
        if upto in (81, 84, 85, 87):
            pass

        def emit_gsh(g, m):
            sy.wait(g, "guv", 2 * m + 2)
            sy.wait(g, "fld", cum_fld[m])
            sl = (m % 2) * 1024
            sy.inc(g.dma_start(out=s_gsh[64:128, :], in_=s_gud[0:64, sl:sl + 1024]), "gshd", 16)
            sy.inc(g.dma_start(out=s_gsh[0:64, :], in_=s_gud[64:128, sl:sl + 1024]), "gshd", 16)

        @block.gpsimd
        def _(g):
            sy.wait(g, "t13e", 1)
            sy.wait(g, "xs4", 8)
            for m in range(86):
                if m >= 2:
                    if DO_T2:
                        sy.wait(g, "t2m", m - 1)
                    else:
                        sy.wait(g, "gum", 4 * (m - 1))
                sy.inc(g.dma_start(out=s_str[:, (m % 2) * 2432:(m % 2) * 2432 + 2432],
                                   in_=w13_d.ap()[:, m * 2432:(m + 1) * 2432]), "wsd", 16)
                mg = m - 1
                if DO_FOLD and DO_GSH and mg >= 0 and needs_gsh[mg]:
                    emit_gsh(g, mg)
            if DO_FOLD and DO_GSH and needs_gsh[85]:
                emit_gsh(g, 85)

        @block.tensor
        def _(t):
            sy.wait(t, "t13e", 1)
            sy.wait(t, "xs4", 8)
            for m in range(86):
                sy.wait(t, "wsd", 16 * (m + 1))
                blk = s_str[:, (m % 2) * 2432:(m % 2) * 2432 + 2432]
                if m >= 1:
                    sy.wait(t, "gue", 2 * m)
                for ng in range(2):
                    pg = p_a if ng == 0 else p_c
                    pu = p_b if ng == 0 else p_d
                    for k in range(8):
                        t.matmul(pg[:, :], blk[:, W1O + k * 128:W1O + (k + 1) * 128],
                                 s_xs4[:, k * 1024 + ng * 512:k * 1024 + (ng + 1) * 512],
                                 start=(k == 0), stop=False)
                    sy.inc(t.matmul(pg[:, :], blk[0:2, B1O:B1O + 128], s_t13[0:2, ng * 512:(ng + 1) * 512],
                                    start=False, stop=True), "gum", 1)
                    if m >= 1:
                        sy.wait(t, "guv", 2 * m)
                    for k in range(8):
                        t.matmul(pu[:, :], blk[:, W3O + k * 128:W3O + (k + 1) * 128],
                                 s_xs4[:, k * 1024 + ng * 512:k * 1024 + (ng + 1) * 512],
                                 start=(k == 0), stop=False)
                    sy.inc(t.matmul(pu[:, :], blk[0:2, B3O:B3O + 128], s_t13[0:2, 1024 + ng * 512:1024 + (ng + 1) * 512],
                                    start=False, stop=True), "gum", 1)
                if DO_T2:
                    sy.wait(t, "guv", 2 * m + 2)
                    for ng in range(2):
                        mm = t.matmul(p_t2[0:2, ng * 512:(ng + 1) * 512], blk[:, A2O:A2O + 2],
                                      s_gud[:, (m % 2) * 1024 + ng * 512:(m % 2) * 1024 + (ng + 1) * 512],
                                      start=(m == 0), stop=(m == 85))
                    sy.inc(mm, "t2m", 1)

        @block.scalar
        def _(sc):
            for m in range(86):
                for ng in range(2):
                    sy.wait(sc, "gum", 4 * m + 2 * ng + 1)
                    if 2 * m + ng >= 2:
                        sy.wait(sc, "guv", 2 * m + ng - 1)
                    pg = p_a if ng == 0 else p_c
                    sy.inc(sc.activation(s_gate[:, ng * 512:(ng + 1) * 512], pg[:, :], AF.Silu), "gue", 1)

        @block.vector
        def _(v):
            for m in range(86):
                for ng in range(2):
                    sy.wait(v, "gue", 2 * m + ng + 1)
                    sy.wait(v, "gum", 4 * m + 2 * ng + 2)
                    pu = p_b if ng == 0 else p_d
                    sy.inc(v.tensor_tensor(s_gud[:, (m % 2) * 1024 + ng * 512:(m % 2) * 1024 + (ng + 1) * 512],
                                           s_gate[:, ng * 512:(ng + 1) * 512], pu[:, :], op=ALU.mult), "guv", 1)
                if not DO_FOLD:
                    continue
                if m == 0:
                    sy.wait(v, "gz", 1)
                if needs_gsh[m] and DO_GSH:
                    sy.wait(v, "gshd", 32 * cum_gsh[m + 1])
                for (kind, sl, dst_pos, ln, first) in seg_src(m):
                    if upto == 88:
                        continue
                    if upto == 87:
                        sy.inc(v.nop(), "fld", 1)
                        continue
                    if kind == "gsh" and not DO_GSH:
                        sy.inc(v.nop(), "fld", 1)
                        continue
                    dst_chunk, dst_off = divmod(dst_pos, 128)
                    dst = s_guf[dst_off:dst_off + ln, dst_chunk * 1024:(dst_chunk + 1) * 1024]
                    if kind == "gud":
                        srcv = s_gud[sl:sl + ln, (m % 2) * 1024:(m % 2) * 1024 + 1024]
                    else:
                        srcv = s_gsh[sl:sl + ln, :]
                    if first:
                        sy.inc(v.tensor_copy(dst, srcv), "fld", 1)
                    else:
                        sy.inc(v.tensor_tensor(dst, dst, srcv, op=ALU.add), "fld", 1)
        FLD = sy.val("fld")

        if upto in (6, 7):
            @block.gpsimd
            def _(g):
                sy.wait(g, "guv", 172)
                if upto == 7:
                    sy.wait(g, "t2m", 86)
                sy.inc(g.dma_start(out=out_d.ap()[:, 0:2048], in_=s_gud[:, :]), "out", 16)
                sy.wait(g, "out", 16)
            return nc

        @block.vector
        def _(v):
            sy.wait(v, "t2m", 86)
            v.tensor_scalar(s_t2[0:2, 0:512], p_t2[0:2, 0:512], 0.5, None, op0=ALU.mult)
            sy.inc(v.tensor_scalar(s_t2[0:2, 512:1024], p_t2[0:2, 512:1024], 0.5, None, op0=ALU.mult), "t2e", 1)

        if upto in (81, 84, 85, 87, 88):
            @block.gpsimd
            def _(g):
                if upto != 88:
                    sy.wait(g, "fld", FLD)
                sy.wait(g, "guv", 172)
                sy.wait(g, "t2m", 86)
                sy.inc(g.dma_start(out=out_d.ap()[:, 0:2048], in_=s_guf[:, 0:2048]), "out", 16)
                sy.wait(g, "out", 16)
            return nc

        if upto in (1, 8):
            @block.gpsimd
            def _(g):
                sy.wait(g, "fld", FLD)
                sy.wait(g, "t2e", 1)
                sy.inc(g.dma_start(out=out_d.ap()[:, 0:22528], in_=s_guf[:, :]), "out", 16)
                sy.inc(g.dma_start(out=dbg_d.ap()[0:2, 0:1024], in_=s_t2[:, :]), "out", 16)
                sy.wait(g, "out", 32)
            return nc

        for half in range(2):
            @block.gpsimd
            def _(g, half=half):
                if half == 0:
                    sy.wait(g, "fld", FLD)
                    sy.wait(g, "t2e", 1)
                else:
                    sy.wait(g, "x4d", 16 * 32)
                for o in range(32):
                    if half * 32 + o >= 2:
                        sy.wait(g, "f3q", half * 32 + o - 1)
                    sy.inc(g.dma_start(out=s_str[:, (o % 2) * 2944:(o % 2) * 2944 + 2944],
                                       in_=w2s_d.ap()[:, o * 2944:(o + 1) * 2944]), "w2d", 16)

            @block.tensor
            def _(t, half=half):
                if half == 0:
                    sy.wait(t, "fld", FLD)
                    sy.wait(t, "t2e", 1)
                else:
                    sy.wait(t, "ep3", 32)
                for o in range(32):
                    sy.wait(t, "w2d", 16 * (half * 32 + o + 1))
                    if half * 32 + o >= 1:
                        sy.wait(t, "f3q", half * 32 + o)
                    blk = s_str[:, (o % 2) * 2944:(o % 2) * 2944 + 2944]
                    for k in range(22):
                        t.matmul(p_a[:, :], blk[:, k * 128:(k + 1) * 128],
                                 s_guf[:, k * 1024 + half * 512:k * 1024 + (half + 1) * 512],
                                 start=(k == 0), stop=False)
                    sy.inc(t.matmul(p_a[:, :], blk[0:2, 2816:2944], s_t2[0:2, half * 512:(half + 1) * 512],
                                    start=False, stop=True), "f3m", 1)
                    sy.wait(t, "f3q", half * 32 + o + 1)
                    t.matmul(p_s[0:1, half * 512:(half + 1) * 512], s_ones1[:, :],
                             s_big[:, o * 512:(o + 1) * 512], start=(o == 0), stop=(o == 31))
                    sy.inc(t.matmul(p_sq[0:1, half * 512:(half + 1) * 512], s_ones1[:, :],
                                    s_sq[:, 0:512], start=(o == 0), stop=(o == 31)), "f3s", 1)

            @block.scalar
            def _(sc, half=half):
                for o in range(32):
                    sy.wait(sc, "f3m", half * 32 + o + 1)
                    if half * 32 + o >= 1:
                        sy.wait(sc, "f3s", half * 32 + o)
                    sc.activation(s_big[:, o * 512:(o + 1) * 512], p_a[:, :], AF.Copy)
                    sy.inc(sc.activation(s_sq[:, 0:512], p_a[:, :], AF.Square), "f3q", 1)

            @block.vector
            def _(v, half=half):
                sy.wait(v, "f3s", 32 * (half + 1))
                v.tensor_copy(s_rows[0:1, 0:512], p_s[0:1, half * 512:(half + 1) * 512])
                v.tensor_copy(s_w[0][0:1, 0:512], p_sq[0:1, half * 512:(half + 1) * 512])
                v.tensor_scalar(s_rows[0:1, 0:512], s_rows[0:1, 0:512], 1.0 / 4096, None, op0=ALU.mult)
                v.tensor_scalar(s_w[0][0:1, 0:512], s_w[0][0:1, 0:512], 1.0 / 4096, None, op0=ALU.mult)
                v.tensor_tensor(s_w[1][0:1, 0:512], s_rows[0:1, 0:512], s_rows[0:1, 0:512], op=ALU.mult)
                v.tensor_tensor(s_w[0][0:1, 0:512], s_w[0][0:1, 0:512], s_w[1][0:1, 0:512], op=ALU.subtract)
                sy.inc(v.tensor_scalar(s_w[0][0:1, 0:512], s_w[0][0:1, 0:512], 1e-5, None, op0=ALU.add), "vq4", 1)

            @block.scalar
            def _(sc, half=half):
                sy.wait(sc, "vq4", half + 1)
                sy.inc(sc.activation(s_w[0][0:1, 0:512], s_w[0][0:1, 0:512], AF.Sqrt), "sq4", 1)

            @block.vector
            def _(v, half=half):
                sy.wait(v, "sq4", half + 1)
                sy.selfedge(v, v.reciprocal(s_w[0][0:1, 0:512], s_w[0][0:1, 0:512]))
                v.tensor_tensor(s_w[1][0:1, 0:512], s_rows[0:1, 0:512], s_w[0][0:1, 0:512], op=ALU.mult)
                v.tensor_tensor(s_w[1][0:1, 0:512], s_ms[0:1, half * 512:(half + 1) * 512], s_w[1][0:1, 0:512], op=ALU.subtract)
                v.tensor_copy(s_rowb[0:1, 0:512], s_w[0][0:1, 0:512])
                v.tensor_copy(s_rowb[0:1, 512:1024], s_sg3[0:1, half * 512:(half + 1) * 512])
                sy.inc(v.tensor_copy(s_rowb[0:1, 1024:1536], s_w[1][0:1, 0:512]), "rb4", 1)

            @block.tensor
            def _(t, half=half):
                sy.wait(t, "rb4", half + 1)
                for r in range(3):
                    if sy.val("bc4") > 0:
                        sy.wait(t, "be4", sy.val("bc4"))
                    sy.inc(t.matmul(p_b[:, :], s_kb[:, :], s_rowb[0:1, r * 512:(r + 1) * 512],
                                    start=True, stop=True), "bc4", 1)

            @block.vector
            def _(v, half=half):
                for r in range(3):
                    sy.wait(v, "bc4", sy.val("be4") + 1)
                    sy.inc(v.tensor_copy(s_rep[r][:, :], p_b[:, :]), "be4", 1)

            @block.vector
            def _(v, half=half):
                sy.wait(v, "be4", sy.val("be4"))
                for o in range(32):
                    idx = half * 32 + o
                    if idx >= 2:
                        sy.wait(v, "x4d", 16 * (idx - 1))
                    ob = s_gf[:, (o % 2) * 512:(o % 2) * 512 + 512]
                    v.tensor_tensor(ob, s_big[:, o * 512:(o + 1) * 512], s_rep[0][:, :], op=ALU.mult)
                    v.tensor_tensor(s_w[0][:, 0:512], s_xnT[:, o * 1024 + half * 512:o * 1024 + (half + 1) * 512],
                                    s_rep[1][:, :], op=ALU.mult)
                    v.tensor_tensor(ob, ob, s_w[0][:, 0:512], op=ALU.add)
                    sy.inc(v.tensor_tensor(ob, ob, s_rep[2][:, :], op=ALU.add), "ep3", 1)

            @block.sync
            def _(q, half=half):
                for o in range(32):
                    sy.wait(q, "ep3", half * 32 + o + 1)
                    sy.inc(q.dma_start(out=out_d.ap()[:, o * 1024 + half * 512:o * 1024 + (half + 1) * 512],
                                       in_=s_gf[:, (o % 2) * 512:(o % 2) * 512 + 512]), "x4d", 16)

        @block.sync
        def _(q):
            sy.wait(q, "x4d", 16 * 64)


    return nc


def bigb_shared(Wout, bout, lnlw, lnlb, w1q, a1, b1, w3q, a3, b3, w2q, a2, b2):
    import ml_dtypes
    bf = ml_dtypes.bfloat16
    f32 = np.float32
    d = {}
    WoT = np.ascontiguousarray(Wout.T, dtype=f32)
    d["woutT"] = np.concatenate([kstack(np.ascontiguousarray(WoT[:, 128 * o:128 * (o + 1)]))
                                 for o in range(32)], axis=1).astype(bf)
    a13 = np.concatenate([np.asarray(a1, f32).T, np.asarray(a3, f32).T], axis=1)
    d["a13T"] = kstack(np.ascontiguousarray(a13)).astype(bf)
    blocks = []
    for m in range(86):
        cols = np.zeros((128, 2432), f32)
        for k in range(8):
            cols[:, k * 128:(k + 1) * 128] = w1q[128 * m:128 * (m + 1), 128 * k:128 * (k + 1)].T
            cols[:, 1024 + k * 128:1024 + (k + 1) * 128] = w3q[128 * m:128 * (m + 1), 128 * k:128 * (k + 1)].T
        cols[0:2, 2048:2176] = b1[128 * m:128 * (m + 1), :].T
        cols[0:2, 2176:2304] = b3[128 * m:128 * (m + 1), :].T
        cols[:, 2304:2306] = a2.T[128 * m:128 * (m + 1), :]
        blocks.append(cols)
    d["w13"] = np.concatenate(blocks, axis=1).astype(bf)
    w2p = np.zeros((2816, 4096), f32)
    w2p[0:2752] = np.asarray(w2q, f32).T
    blocks = []
    for o in range(32):
        cols = np.zeros((128, 2944), f32)
        for k in range(22):
            cols[:, k * 128:(k + 1) * 128] = w2p[128 * k:128 * (k + 1), 128 * o:128 * (o + 1)]
        cols[0:2, 2816:2944] = b2[128 * o:128 * (o + 1), :].T
        blocks.append(cols)
    d["w2s"] = np.concatenate(blocks, axis=1).astype(bf)
    pc = np.zeros((128, 40), f32)
    for j in range(4):
        pc[:, j] = lnlw[128 * j:128 * (j + 1)]
        pc[:, 4 + j] = lnlb[128 * j:128 * (j + 1)]
    for o in range(32):
        pc[:, 8 + o] = bout[128 * o:128 * (o + 1)]
    d["pc"] = pc
    d["onesk"] = np.ones((1, 128), f32)
    d["cstE"] = np.full((128, 1), 1e-5, f32)
    d["cstO"] = np.ones((128, 1), f32)
    return d


def _assemble_seq_512(res, key):
    cols = []
    for c in range(NCORE):
        a = np.asarray(res.results[c][key], np.float32)
        cols.append(a.reshape(128, 4, 1024).transpose(1, 0, 2).reshape(512, 1024).T)
    return np.ascontiguousarray(np.concatenate(cols, axis=0))


def _run_conv(ncC, consts, yfull, pre, pim, lcre, lcim):
    in_maps = []
    for c in range(NCORE):
        sl = slice(64 * c, 64 * c + 64)
        im = dict(consts)
        im["y"] = np.ascontiguousarray(
            yfull[:, sl].reshape(64, 128, 64).transpose(1, 0, 2).reshape(128, 4096))
        im["phre"] = np.broadcast_to(pre[sl], (128, 64)).copy()
        im["phim"] = np.broadcast_to(pim[sl], (128, 64)).copy()
        im["lc"] = np.concatenate([lcre[sl], lcim[sl]])[None, :].astype(np.float32)
        in_maps.append(im)
    res = run_bass_kernel_spmd(ncC, in_maps, core_ids=list(range(NCORE)))
    out = np.zeros((8192, 512), np.float32)
    for c in range(NCORE):
        o = np.asarray(res.results[c]["out"], np.float32)
        out[:, 64 * c:64 * c + 64] = o.reshape(128, 64, 64).transpose(1, 0, 2).reshape(8192, 64)
    return out


def _seq_to_kstack(full):
    out = []
    for c in range(NCORE):
        t = np.ascontiguousarray(full[1024 * c:1024 * (c + 1)].T)
        out.append(np.ascontiguousarray(t.reshape(4, 128, 1024).transpose(1, 0, 2).reshape(128, 4096)))
    return out


def kernel(**inputs):
    f32 = np.float32
    g = lambda k: np.asarray(inputs[k], f32)
    x0 = g("L0_attn_x").reshape(8192, 4096)
    Wq, Wk, Wv, Wo = g("L0_attn_Wq"), g("L0_attn_Wk"), g("L0_attn_Wv"), g("L0_attn_Wo")
    base_W, gate_W = g("L0_ffn_base_W"), g("L0_ffn_gate_W")
    lora_A, lora_B = g("L0_ffn_lora_A"), g("L0_ffn_lora_B")
    Win, bin_ = g("L1_attn_Win"), g("L1_attn_bin")
    ln1w, ln1b = g("L1_attn_ln1_w"), g("L1_attn_ln1_b")
    phz_re, phz_im = g("L1_attn_phz_re"), g("L1_attn_phz_im")
    lc_re, lc_im = g("L1_attn_lc_re"), g("L1_attn_lc_im")
    sc_lin, ln2w, ln2b = g("L1_attn_sc_lin"), g("L1_attn_ln2_w"), g("L1_attn_ln2_b")
    W1, b1, W2, b2 = g("L1_attn_W1"), g("L1_attn_b1"), g("L1_attn_W2"), g("L1_attn_b2")
    lnlw, lnlb = g("L1_attn_lnl_w"), g("L1_attn_lnl_b")
    Wout, bout = g("L1_attn_Wout"), g("L1_attn_bout")
    w1q, a1, b1q = g("L1_ffn_w1q"), g("L1_ffn_a1"), g("L1_ffn_b1")
    w3q, a3, b3q = g("L1_ffn_w3q"), g("L1_ffn_a3"), g("L1_ffn_b3")
    w2q, a2, b2q = g("L1_ffn_w2q"), g("L1_ffn_a2"), g("L1_ffn_b2")

    shA = biga_shared(Wq, Wk, Wv, Wo, base_W, gate_W, lora_A, lora_B, Win, bin_, ln1w[0], ln1b[0])
    in_maps = [biga_core_inputs(c, x0, shA) for c in range(NCORE)]
    ncA = build_biga()
    resA = run_bass_kernel_spmd(ncA, in_maps, core_ids=list(range(NCORE)))
    xn2 = [np.asarray(resA.results[c]["xn2"]) for c in range(NCORE)]
    musig = [np.asarray(resA.results[c]["musig"], f32) for c in range(NCORE)]
    hcur = [np.asarray(resA.results[c]["h"], f32) for c in range(NCORE)]
    ycur = _assemble_seq_512(resA, "y0")

    ncC = build_conv()
    ncM = build_mid()
    consts = conv_consts()
    mid_shared = dict(onesk=np.ones((1, 128), f32))
    for l in range(4):
        cv = _run_conv(ncC, consts, ycur, phz_re[l], phz_im[l], lc_re[l], lc_im[l])
        cvk = _seq_to_kstack(cv)
        nl = l + 1 if l < 3 else 0
        pc = np.zeros((128, 24), f32)
        for bidx, arr in enumerate([sc_lin[l], ln2w[l], ln2b[l], ln1w[nl], ln1b[nl], b2[l]]):
            for j in range(4):
                pc[:, bidx * 4 + j] = arr[128 * j:128 * (j + 1)]
        b1c = np.zeros((128, 16), f32)
        for m in range(16):
            b1c[:, m] = b1[l][128 * m:128 * (m + 1)]
        im0 = dict(mid_shared, pc=pc, b1c=b1c,
                   w1t=mstack(np.ascontiguousarray(W1[l].T)),
                   w2t=mstack(np.ascontiguousarray(W2[l].T)))
        mid_maps = []
        for c in range(NCORE):
            im = dict(im0)
            im["h"] = hcur[c]
            im["cv"] = cvk[c]
            mid_maps.append(im)
        resM = run_bass_kernel_spmd(ncM, mid_maps, core_ids=list(range(NCORE)))
        hcur = [np.asarray(resM.results[c]["hout"], f32) for c in range(NCORE)]
        if l < 3:
            ycur = _assemble_seq_512(resM, "y")

    shB = bigb_shared(Wout, bout, lnlw, lnlb, w1q, a1, b1q, w3q, a3, b3q, w2q, a2, b2q)
    ncB = build_bigb()
    bigb_maps = []
    for c in range(NCORE):
        im = dict(shB)
        im["xn2"] = xn2[c]
        im["musig"] = musig[c]
        im["h"] = hcur[c]
        bigb_maps.append(im)
    resB = run_bass_kernel_spmd(ncB, bigb_maps, core_ids=list(range(NCORE)))
    out = np.zeros((8192, 4096), np.float32)
    for c in range(NCORE):
        a = np.asarray(resB.results[c]["out"], np.float32)
        out[1024 * c:1024 * (c + 1)] = a.reshape(128, 32, 1024).transpose(1, 0, 2).reshape(4096, 1024).T
    return out.reshape(1, 8192, 4096)
